# version 09
# baseline (speedup 1.0000x reference)
import math
import contextlib
import numpy as np
import concourse.bass as bass
import concourse.mybir as mybir
from concourse.bass_utils import run_bass_kernel_spmd

F32 = mybir.dt.float32
BF16 = mybir.dt.bfloat16
I32 = mybir.dt.int32
AF = mybir.ActivationFunctionType
ALU = mybir.AluOpType
AXX = mybir.AxisListType.X

B_ALL, S, D = 16, 2048, 1024
NCORES = 8
B_PER = B_ALL // NCORES
T = B_PER * S
NT = S // 128
DT = D // 128
H, DH = 16, 64
F_FFN = 4096
FT = F_FFN // 128

STAGE = "FULL"

_P1RANK = {"P1L": 0, "P1M1": 1, "P1M2": 2, "P1M3": 3, "P1M": 4}


def _p1rank():
    return _P1RANK.get(STAGE, 99)


_OUT_KW = ("accum_out",)


def _ap_rng(a):
    try:
        name = a.tensor.name
        lo = a.offset
        ext = 1
        for st, ct in a.ap:
            if st > 0:
                ext += st * (ct - 1)
        return (name, lo, lo + ext)
    except Exception:
        return None


class DW:

    _OPS = {
        "tensor_tensor", "tensor_add", "tensor_sub", "tensor_mul", "tensor_max",
        "tensor_scalar", "tensor_single_scalar", "tensor_scalar_add",
        "tensor_scalar_sub", "tensor_scalar_mul", "tensor_scalar_max",
        "tensor_scalar_min", "tensor_relu", "tensor_reduce", "reduce_max",
        "reduce_sum", "scalar_tensor_tensor", "tensor_tensor_scan",
        "tensor_copy", "memset", "memzero", "reciprocal", "activation",
        "copy", "add", "mul", "square", "sqrt", "sign", "select",
    }
    _NO_IN = {"memset", "memzero"}

    def __init__(self, eng):
        self._e = eng
        self._w = []
        self._r = []

    def _overlap(self, rng, lst):
        if rng is None:
            return True
        for (n, lo, hi) in lst:
            if n == rng[0] and rng[1] < hi and lo < rng[2]:
                return True
        return False

    def __getattr__(self, n):
        f = getattr(self._e, n)
        if n not in DW._OPS:
            return f

        def g(*a, **k):
            outs, ins = [], []
            if a:
                outs.append(_ap_rng(a[0]))
            for x in a[1:]:
                r = _ap_rng(x)
                if r is not None:
                    ins.append(r)
            for kw, v in k.items():
                r = _ap_rng(v)
                if r is None:
                    continue
                (outs if kw in _OUT_KW else ins).append(r)
            need = False
            for r in ins:
                if self._overlap(r, self._w):
                    need = True
                    break
            if not need:
                for r in outs:
                    if self._overlap(r, self._w) or self._overlap(r, self._r):
                        need = True
                        break
            if need:
                self._e.drain()
                self._w.clear()
                self._r.clear()
            res = f(*a, **k)
            for r in outs:
                self._w.append(r if r is not None else ("?", 0, 1 << 60))
            if n not in DW._NO_IN:
                for r in ins:
                    self._r.append(r)
            return res
        return g
def attn_sched():
    sched = []
    for kt in range(NT):
        q = kt * 128
        first = True
        while q < S:
            ln = min(1024, S - q)
            sched.append((kt, q, ln, first))
            first = False
            q += ln
    return sched


def aligned_pieces(q, ln):
    res = []
    e = q + ln
    while q < e:
        nxt = min(e, (q // 512 + 1) * 512)
        res.append((q, nxt - q))
        q = nxt
    return res


def build(nc):
    _uid = [0]

    def _nm(tag):
        _uid[0] += 1
        return f'{tag}_{_uid[0]}'

    xin = nc.dram_tensor("x", [T, D], F32, kind="ExternalInput")
    delta = nc.dram_tensor("delta", [128, 16], F32, kind="ExternalInput")
    alpha = nc.dram_tensor("alpha", [128, 16], F32, kind="ExternalInput")
    betaT = nc.dram_tensor("betaT", [128, 16], F32, kind="ExternalInput")
    gammaT = nc.dram_tensor("gammaT", [128, 16], F32, kind="ExternalInput")
    omegaT = nc.dram_tensor("omegaT", [128, 8], F32, kind="ExternalInput")
    w0 = nc.dram_tensor("w0", [D, D], F32, kind="ExternalInput")
    wqd = nc.dram_tensor("wqd", [D, D], F32, kind="ExternalInput")
    wkd = nc.dram_tensor("wkd", [D, D], F32, kind="ExternalInput")
    wvd = nc.dram_tensor("wvd", [D, D], F32, kind="ExternalInput")
    bqc = nc.dram_tensor("bqc", [128, 8], F32, kind="ExternalInput")
    bkc = nc.dram_tensor("bkc", [128, 8], F32, kind="ExternalInput")
    bvr = nc.dram_tensor("bvr", [1, D], F32, kind="ExternalInput")
    wod = nc.dram_tensor("wod", [D, D], F32, kind="ExternalInput")
    boc = nc.dram_tensor("boc", [128, 8], F32, kind="ExternalInput")
    lngc = nc.dram_tensor("lngc", [128, 8], F32, kind="ExternalInput")
    lnbc = nc.dram_tensor("lnbc", [128, 8], F32, kind="ExternalInput")
    fc1q = nc.dram_tensor("fc1q", [D, F_FFN], I32, kind="ExternalInput")
    fc1s = nc.dram_tensor("fc1s", [8, F_FFN], F32, kind="ExternalInput")
    fc1bc = nc.dram_tensor("fc1bc", [128, 32], F32, kind="ExternalInput")
    fc2q = nc.dram_tensor("fc2q", [F_FFN, D], I32, kind="ExternalInput")
    fc2s = nc.dram_tensor("fc2s", [32, D], F32, kind="ExternalInput")
    fc2bc = nc.dram_tensor("fc2bc", [128, 8], F32, kind="ExternalInput")
    identd = nc.dram_tensor("identd", [128, 128], BF16, kind="ExternalInput")
    jrotd = nc.dram_tensor("jrotd", [128, 128], BF16, kind="ExternalInput")
    cosd = nc.dram_tensor("cosd", [128, S], BF16, kind="ExternalInput")
    sind = nc.dram_tensor("sind", [128, S], BF16, kind="ExternalInput")
    maskd = nc.dram_tensor("maskd", [128, 128], BF16, kind="ExternalInput")
    ind16d = nc.dram_tensor("ind16d", [16, D], BF16, kind="ExternalInput")
    out = nc.dram_tensor("out", [T, D], F32, kind="ExternalOutput")
    x1sc = nc.dram_tensor("x1sc", [T, D], F32, kind="Internal")
    xn1sc = nc.dram_tensor("xn1sc", [B_PER, D, S], BF16, kind="Internal")
    gtsc = nc.dram_tensor("gtsc", [B_PER, D, S], BF16, kind="Internal")

    est = contextlib.ExitStack()
    with est:
        _ctr = [0]
        def SB(shape, dt_):
            _ctr[0] += 1
            return est.enter_context(nc.sbuf_tensor(f'sb{_ctr[0]}', shape, dt_))

        id_sb = SB([128, 128], BF16)
        j_sb = SB([128, 128], BF16)
        mask_sb = SB([128, 128], BF16)
        ind_sb = SB([16, D], BF16)
        ones1 = SB([1, 128], BF16)
        ones1f = SB([1, 128], F32)
        pcol = SB([128, 48], F32)
        QC, CC, OMC, WSC, RWSC = 0, 16, 32, 40, 41
        bcol = SB([128, 64], F32)
        BQ0, BK0, BO0, LNG0, LNB0, HB20 = 0, 8, 16, 24, 32, 40
        f1bcol = SB([128, 32], F32)
        qcolb = SB([128, 48], BF16)
        m2cb = SB([128, 32], BF16)
        r2cb = SB([128, 32], BF16)
        smc = SB([128, 64], F32)
        wsum = SB([1, 2], F32)
        halfc = SB([128, 1], F32)
        onescol = SB([128, 1], F32)
        halfc2 = SB([128, 1], F32)

        sems, semv = {}, {}

        def mksem(names):
            for n in names:
                sems[n] = est.enter_context(nc.semaphore(n))
                semv[n] = 0
        mksem(["dp", "dw", "dx", "sq", "vr", "ss", "vn", "xs", "tt", "tc",
               "az", "asc", "aq", "cb", "sl", "bm", "bmm", "be", "gs", "da",
               "vb", "bb", "bc", "sg", "et", "d2", "ex", "d1", "q2", "v2",
               "s2", "n2", "a2", "t2", "cs", "d3", "dr", "dq", "ds", "vq",
               "hm", "hs", "f1", "fe", "f2", "ge", "dg", "qm", "qb", "jm",
               "rp", "vm", "ve", "stm", "xe", "msk", "pvm", "oc", "rc",
               "nbm", "nbc", "wm", "we", "st", "dxa", "dga", "vv", "gm",
               "dx0", "dx1", "dx2", "dx3", "d20", "d21", "d10", "d11", "d30", "d31",
               "st0", "st1", "ds0", "ds1", "dq0", "dq1", "dbv", "ssd0", "ssd1",
               "stmA", "stmB", "xeA", "xeB", "mskA", "mskB", "pvmA", "pvmB",
               "ocA", "ocB", "ssdA", "ssdB"])

        def inc(ins, name, amt=1):
            semv[name] += amt
            return ins.then_inc(sems[name], amt)

        def snap():
            return dict(semv)

        SKIP_P0 = (STAGE == "P1B" or STAGE.startswith("MIN") or _p1rank() < 99)
        with nc.Block() as blk:
            @blk.sync
            def _(e):
                if SKIP_P0:
                    return
                for dst, src in [
                    (id_sb[:], identd.ap()), (j_sb[:], jrotd.ap()),
                    (mask_sb[:], maskd.ap()), (ind_sb[:], ind16d.ap()),
                    (pcol[:, 0:16], delta.ap()), (pcol[:, 16:32], alpha.ap()),
                    (smc[:, 0:16], betaT.ap()), (smc[:, 16:32], gammaT.ap()),
                    (pcol[:, OMC:OMC + 8], omegaT.ap()),
                    (bcol[:, BQ0:BQ0 + 8], bqc.ap()), (bcol[:, BK0:BK0 + 8], bkc.ap()),
                    (bcol[:, BO0:BO0 + 8], boc.ap()), (bcol[:, LNG0:LNG0 + 8], lngc.ap()),
                    (bcol[:, LNB0:LNB0 + 8], lnbc.ap()),
                    (bcol[:, HB20:HB20 + 8], fc2bc.ap()), (f1bcol[:], fc1bc.ap()),
                ]:
                    inc(e.dma_start(dst, src), "dp", 16)
                e.wait_ge(sems["dp"], semv["dp"])

            @blk.vector
            def _(v):
                v = DW(v)
                v.memset(ones1[:], 1.0)
                v.memset(ones1f[:], 1.0)
                v.memset(onescol[:], 1.0)
                v.memset(halfc[:], 0.5)
                if SKIP_P0:
                    return

        with nc.Block() as blk:
            @blk.scalar
            def _(s):
                s = DW(s)
                if SKIP_P0:
                    return
                s.activation(pcol[:, 0:16], pcol[:, 0:16], AF.Sigmoid)
                s.activation(pcol[:, 16:32], pcol[:, 16:32], AF.Sigmoid)

        with nc.Block() as blk:
            @blk.vector
            def _(v):
                v = DW(v)
                if SKIP_P0:
                    return
                v.tensor_mul(smc[:, 32:48], pcol[:, 0:16], pcol[:, 16:32])
                v.tensor_mul(pcol[:, CC:CC + 16], pcol[:, 0:16], smc[:, 0:16])
                v.tensor_mul(pcol[:, CC:CC + 16], pcol[:, CC:CC + 16], smc[:, 16:32])
                v.tensor_scalar_mul(pcol[:, CC:CC + 16], pcol[:, CC:CC + 16],
                                    math.sqrt(0.5))
                v.tensor_scalar(pcol[:, QC:QC + 16], smc[:, 32:48], -1.0, 1.0,
                                ALU.mult, ALU.add)
                v.tensor_scalar_mul(bcol[:, HB20:HB20 + 8], bcol[:, HB20:HB20 + 8], 0.5)

        if STAGE == "XCOPY":
            with nc.sbuf_tensor("xc", [128, 1024], F32) as xc:
                with nc.Block() as blk:
                    @blk.sync
                    def _(e):
                        for tau in range(32):
                            if tau >= 2:
                                e.wait_ge(sems["st"], semv["st"] - 16)
                            inc(e.dma_start(xc[:], xin.ap()[tau * 128:(tau + 1) * 128, :]),
                                "dx", 16)
                            e.wait_ge(sems["dx"], semv["dx"])
                            inc(e.dma_start(out.ap()[tau * 128:(tau + 1) * 128, :], xc[:]),
                                "st", 16)
                        e.wait_ge(sems["st"], semv["st"])
            return nc

        if STAGE == "P0X":
            with nc.Block() as blk:
                @blk.sync
                def _(e):
                    inc(e.dma_start(out.ap()[0:128, 0:48], pcol[:]), "st", 16)
                    e.wait_ge(sems["st"], semv["st"])
            return nc

        if STAGE == "MINPE":
            with (
                nc.sbuf_tensor("mpa", [128, 8], F32) as mpa,
                nc.sbuf_tensor("mpres", [128, 8], F32) as mpres,
                nc.psum_tensor("mpps", [128, 128], F32) as mpps,
            ):
                with nc.Block() as blk:
                    @blk.sync
                    def _(e):
                        inc(e.dma_start(mpa[:], xin.ap()[0:128, 0:8]), "dx", 16)

                    @blk.vector
                    def _(v):
                        v.wait_ge(sems["dx"], semv["dx"])

                with nc.Block() as blk:
                    @blk.tensor
                    def _(t):
                        inc(t.matmul(mpps[0:1, 0:1], mpa[:, 0:1], onescol[:],
                                     start=True, stop=True), "bm")

                    @blk.vector
                    def _(v):
                        v.wait_ge(sems["bm"], semv["bm"])
                        v.tensor_copy(mpres[0:1, 0:1], mpps[0:1, 0:1])
                        v.drain()

                with nc.Block() as blk:
                    @blk.sync
                    def _(e):
                        inc(e.dma_start(out.ap()[0:128, 0:8], mpres[:]), "st", 16)
                        e.wait_ge(sems["st"], semv["st"])
            return nc

        if STAGE.startswith("MINP2"):
            with (
                nc.sbuf_tensor("m2w", [128, 8 * 1024], F32) as m2w,
                nc.sbuf_tensor("m2res", [128, 16], F32) as m2res,
                nc.psum_tensor("m2ps", [128, 128], F32) as m2ps,
                nc.psum_tensor("m2pb", [128, 1024], BF16) as m2pb,
            ):
                with nc.Block() as blk:
                    @blk.sync
                    def _(e):
                        for j in range(8):
                            inc(e.dma_start(m2w[:, j * 1024:(j + 1) * 1024],
                                            w0.ap()[j * 128:(j + 1) * 128, :]),
                                "dw", 16)

                    @blk.vector
                    def _(v):
                        v = DW(v)
                        v.wait_ge(sems["dw"], semv["dw"])
                        if STAGE != "MINP2N":
                            for j in range(8):
                                v.tensor_reduce(m2res[:, j:j + 1],
                                                m2w[:, j * 1024:(j + 1) * 1024],
                                                AXX, ALU.add,
                                                apply_absolute_value=True)
                            v.reduce_sum(m2res[:, 8:9], m2res[:, 0:8], AXX)

                with nc.Block() as blk:
                    @blk.tensor
                    def _(t):
                        inc(t.matmul(m2ps[0:1, 0:1], m2res[:, 8:9], onescol[:],
                                     start=True, stop=True), "bm")

                    @blk.vector
                    def _(v):
                        v.wait_ge(sems["bm"], semv["bm"])
                        v.tensor_copy(m2res[0:1, 9:10], m2ps[0:1, 0:1])
                        v.drain()

                with nc.Block() as blk:
                    @blk.sync
                    def _(e):
                        inc(e.dma_start(out.ap()[0:128, 0:16], m2res[:]), "st", 16)
                        e.wait_ge(sems["st"], semv["st"])

                if STAGE == "MINP2D":
                    with nc.Block() as blk:
                        @blk.gpsimd
                        def _(g):
                            inc(g.dma_start(out.ap()[128:256, 0:48], pcol[:]),
                                "st", 16)
                            inc(g.dma_start(out.ap()[256:257, 0:2], wsum[:]),
                                "st", 16)
                            inc(g.dma_start(out.ap()[384:512, 0:1024],
                                            m2w[:, 0:1024]), "st", 16)
                            inc(g.dma_start(out.ap()[768:896, 0:64], smc[:]),
                                "st", 16)
                            g.wait_ge(sems["st"], semv["st"])
            return nc

        with nc.sbuf_tensor([128, 8 * 1024], BF16) as wqT:
            with (
                nc.sbuf_tensor([128, 8 * 1024], F32) as wfull,
                nc.sbuf_tensor([128, 1024], F32) as tmpa,
                nc.sbuf_tensor([128, 1024], F32) as tmpb,
                nc.sbuf_tensor([128, 1024], I32) as tmpi,
                nc.sbuf_tensor([128, 8 * 1024], BF16) as wqo,
                nc.psum_tensor([128, 512], F32) as ps1,
                nc.psum_tensor([128, 2048], BF16) as pstb,
            ):
                with nc.Block() as blk:
                    BB = snap()

                    @blk.sync
                    def _(e):
                        for j in range(8):
                            inc(e.dma_start(wfull[:, j * 1024:(j + 1) * 1024],
                                            w0.ap()[j * 128:(j + 1) * 128, :]), "dw", 16)

                    @blk.vector
                    def _(v):
                        v = DW(v)
                        v.wait_ge(sems["dw"], BB["dw"] + 16 * 8)
                        for j in range(8):
                            v.tensor_reduce(smc[:, 32 + j:33 + j],
                                            wfull[:, j * 1024:(j + 1) * 1024],
                                            AXX, ALU.add, apply_absolute_value=True)
                        v.reduce_sum(smc[:, 40:41], smc[:, 32:40], AXX)

                if STAGE in ("P1L", "P1M1"):
                    with nc.Block() as blk:
                        @blk.gpsimd
                        def _(g):
                            g.wait_ge(sems["dw"], semv["dw"])
                            inc(g.dma_start(out.ap()[768:896, 0:64], smc[:]),
                                "st", 16)
                            inc(g.dma_start(out.ap()[896:1024, 0:1024],
                                            wfull[:, 0:1024]), "st", 16)
                            g.wait_ge(sems["st"], semv["st"])

                with nc.Block() as blk:
                    @blk.vector
                    def _(v):
                        v = DW(v)
                        if _p1rank() < 1:
                            return
                        v.tensor_copy(halfc2[:], smc[:, 40:41])

                with nc.Block() as blk:
                    @blk.tensor
                    def _(t):
                        if _p1rank() < 1:
                            return
                        import os as _os
                        mmv = _os.environ.get("MMVAR", "tiny")
                        if mmv == "tiny":
                            t.matmul(ps1[0:1, 0:1], halfc2[:], onescol[:],
                                     start=True, stop=True)
                        elif mmv == "full":
                            t.matmul(ps1[0:128, 0:128], wfull[:, 0:128],
                                     wfull[:, 128:256], start=True, stop=True)
                        elif mmv == "mid":
                            t.matmul(ps1[0:1, 0:128], halfc2[:],
                                     wfull[:, 0:128], start=True, stop=True)

                with nc.Block() as blk:
                    @blk.vector
                    def _(v):
                        v = DW(v)
                        if _p1rank() < 2:
                            return
                        v.tensor_copy(wsum[0:1, 0:1], ps1[0:1, 0:1])

                with nc.Block() as blk:
                    @blk.tensor
                    def _(t):
                        if _p1rank() < 3:
                            return
                        t.matmul(ps1[0:128, 0:1], ones1f[:], wsum[0:1, 0:1],
                                 start=True, stop=True)

                with nc.Block() as blk:
                    @blk.vector
                    def _(v):
                        v = DW(v)
                        if _p1rank() < 4:
                            return
                        v.tensor_copy(pcol[:, WSC:WSC + 1], ps1[0:128, 0:1])

                with nc.Block() as blk:
                    @blk.vector
                    def _(v):
                        v = DW(v)
                        if _p1rank() < 4:
                            return
                        v.tensor_scalar(pcol[:, WSC:WSC + 1], pcol[:, WSC:WSC + 1],
                                        1.0 / float(D * D), 1e-8, ALU.mult, ALU.add)
                        v.reciprocal(pcol[:, RWSC:RWSC + 1], pcol[:, WSC:WSC + 1])
                        if STAGE == "P1A" or _p1rank() < 99:
                            return
                        for j in range(8):
                            wj = wfull[:, j * 1024:(j + 1) * 1024]
                            v.tensor_scalar(tmpa[:], wj, pcol[:, RWSC:RWSC + 1], 0.5,
                                            ALU.mult, ALU.add)
                            v.tensor_copy(tmpi[:], tmpa[:])
                            v.tensor_tensor(tmpb[:], tmpi[:], tmpa[:], ALU.is_gt)
                            v.tensor_sub(tmpb[:], tmpi[:], tmpb[:])
                            v.tensor_scalar(wqo[:, j * 1024:(j + 1) * 1024], tmpb[:],
                                            -1.0, 1.0, ALU.max, ALU.min)

                with nc.Block() as blk:
                    BB = snap()

                    @blk.tensor
                    def _(t):
                        if STAGE == "P1B" or _p1rank() < 99:
                            return
                        for a in range(8):
                            for g4 in range(2):
                                gidx = a * 2 + g4
                                h_ = (gidx % 2) * 1024
                                if gidx >= 2:
                                    t.wait_ge(sems["tc"], BB["tc"] + gidx - 1)
                                last = None
                                for k in range(4):
                                    b_ = g4 * 4 + k
                                    last = t.transpose(
                                        pstb[:, h_ + k * 128:h_ + (k + 1) * 128],
                                        wqo[:, a * 1024 + b_ * 128:
                                            a * 1024 + (b_ + 1) * 128],
                                        id_sb[:])
                                inc(last, "tt")

                    @blk.vector
                    def _(v):
                        v = DW(v)
                        if STAGE == "P1B" or _p1rank() < 99:
                            return
                        for a in range(8):
                            for g4 in range(2):
                                gidx = a * 2 + g4
                                h_ = (gidx % 2) * 1024
                                v.wait_ge(sems["tt"], BB["tt"] + gidx + 1)
                                dst = wqT[:].rearrange("p (i c) -> p i c", i=8)[
                                    :, g4 * 4:g4 * 4 + 4, a * 128:(a + 1) * 128]
                                src = pstb[:, h_:h_ + 512].rearrange(
                                    "p (k c) -> p k c", k=4)
                                inc(v.tensor_copy(dst, src), "tc")

            if STAGE == "P1A":
                with nc.Block() as blk:
                    @blk.sync
                    def _(e):
                        inc(e.dma_start(out.ap()[0:128, 0:48], pcol[:]), "st", 16)
                        e.wait_ge(sems["st"], semv["st"])
                return nc

            if STAGE == "P1X":
                with nc.sbuf_tensor("dmp", [128, 1024], F32) as dmp:
                    with nc.Block() as blk:
                        @blk.vector
                        def _(v):
                            v = DW(v)
                            inc(v.tensor_copy(dmp[:], wqT[:, 0:1024]), "vb")

                    with nc.Block() as blk:
                        @blk.sync
                        def _(e):
                            inc(e.dma_start(out.ap()[0:128, :], dmp[:]), "st", 16)
                            e.wait_ge(sems["st"], semv["st"])
                return nc

                if STAGE == "P1B" or _p1rank() < 99:
                    with nc.Block() as blk:
                        @blk.gpsimd
                        def _(g):
                            g.wait_ge(sems["dw"], semv["dw"])
                            inc(g.dma_start(out.ap()[0:128, 0:1024], wqo[:, 0:1024]),
                                "st", 16)
                            inc(g.dma_start(out.ap()[128:256, 0:48], pcol[:]),
                                "st", 16)
                            inc(g.dma_start(out.ap()[256:257, 0:2], wsum[:]),
                                "st", 16)
                            inc(g.dma_start(out.ap()[384:512, 0:1024],
                                            wfull[:, 0:1024]), "st", 16)
                            inc(g.dma_start(out.ap()[512:640, 0:1024],
                                            wfull[:, 1024:2048]), "st", 16)
                            inc(g.dma_start(out.ap()[768:896, 0:64], smc[:]),
                                "st", 16)
                            inc(g.dma_start(out.ap()[640:768, 0:1024],
                                            wfull[:, 7168:8192]), "st", 16)
                            g.wait_ge(sems["st"], semv["st"])

            if STAGE == "P1B" or _p1rank() < 99:
                return nc

            with contextlib.ExitStack() as p2s:
                A2 = lambda sh, dt_: p2s.enter_context(nc.sbuf_tensor(_nm('A2'), sh, dt_))
                P2 = lambda sh, dt_: p2s.enter_context(nc.psum_tensor(_nm('P2'), sh, dt_))
                xt = A2([128, 4 * 1024], F32)
                xt2 = xt
                x1t = A2([128, 2 * 1024], F32)
                xnt = A2([128, 2 * 1024], BF16)
                xn1t = A2([128, 2 * 1024], BF16)
                stg = xnt
                xnT = A2([128, 8 * 2048], BF16)
                aqT = A2([128, 8 * 2048], BF16)
                gT = A2([128, 8 * 2048], BF16)
                tf = A2([128, 2048], F32)
                tf2 = A2([128, 2048], F32)
                h0 = tf
                h1 = tf2
                ti = x1t[:].bitcast(I32)
                gtb = xn1t
                rsb = A2([128, 2048], BF16)
                ysb = A2([128, 2048], F32)
                sc4 = A2([128, 32], F32)
                sqd = ysb[:, 0:1024]
                pstr1 = P2([128, 2048], BF16)
                pstr2 = P2([128, 2048], BF16)
                psb0 = P2([128, 512], F32)
                psb1 = P2([128, 512], F32)
                psbc = P2([128, 512], F32)
                psbit = [psb0, psb1]
                for b in range(B_PER):
                    with nc.Block() as blk:
                        BB = snap()

                        @blk.sync
                        def _(e):
                            for tau in range(NT):
                                p4 = tau % 4
                                if tau >= 4:
                                    e.wait_ge(sems["xs"], BB["xs"] + tau - 3)
                                inc(e.dma_start(
                                    xt[:, p4 * 1024:(p4 + 1) * 1024],
                                    xin.ap()[b * S + tau * 128:b * S + (tau + 1) * 128, :]),
                                    "dx" + str(p4), 16)

                        @blk.scalar
                        def _(s):
                            s = DW(s)
                            s.wait_ge(sems["d30"], BB["d30"])
                            s.wait_ge(sems["d31"], BB["d31"])
                            for tau in range(NT):
                                p = tau % 2
                                p4 = tau % 4
                                s.wait_ge(sems["dx" + str(p4)],
                                          BB["dx" + str(p4)] + 16 * (tau // 4 + 1))
                                inc(s.activation(sqd, xt[:, p4 * 1024:(p4 + 1) * 1024],
                                                 AF.Square,
                                                 accum_out=sc4[:, 4 + p4:5 + p4]), "sq")
                                s.wait_ge(sems["vr"], BB["vr"] + tau + 1)
                                inc(s.activation(sc4[:, 16 + p4:17 + p4],
                                                 sc4[:, 16 + p4:17 + p4],
                                                 AF.Sqrt), "ss")
                                s.wait_ge(sems["vn"], BB["vn"] + tau + 1)
                                if tau >= 2:
                                    s.wait_ge(sems["tc"], BB["tc"] + tau - 1)
                                inc(s.activation(xnt[:, p * 1024:(p + 1) * 1024],
                                                 xt[:, p4 * 1024:(p4 + 1) * 1024],
                                                 AF.Identity,
                                                 bias=sc4[:, 20 + p4:21 + p4],
                                                 scale=sc4[:, 16 + p4:17 + p4]), "xs")

                        @blk.vector
                        def _(v):
                            v = DW(v)
                            for tau in range(NT):
                                p = tau % 2
                                p4 = tau % 4
                                v.wait_ge(sems["dx" + str(p4)],
                                          BB["dx" + str(p4)] + 16 * (tau // 4 + 1))
                                v.reduce_sum(sc4[:, 0 + p4:1 + p4],
                                             xt[:, p4 * 1024:(p4 + 1) * 1024], AXX)
                                v.tensor_scalar_mul(sc4[:, 8 + p4:9 + p4],
                                                    sc4[:, 0 + p4:1 + p4], 1.0 / D)
                                v.wait_ge(sems["sq"], BB["sq"] + tau + 1)
                                v.tensor_mul(sc4[:, 24 + p4:25 + p4],
                                             sc4[:, 8 + p4:9 + p4],
                                             sc4[:, 8 + p4:9 + p4])
                                v.tensor_scalar(sc4[:, 12 + p4:13 + p4],
                                                sc4[:, 4 + p4:5 + p4],
                                                1.0 / D, None, ALU.mult)
                                v.tensor_sub(sc4[:, 12 + p4:13 + p4],
                                             sc4[:, 12 + p4:13 + p4],
                                             sc4[:, 24 + p4:25 + p4])
                                v.tensor_scalar_add(sc4[:, 12 + p4:13 + p4],
                                                    sc4[:, 12 + p4:13 + p4], 1e-5)
                                inc(v.reciprocal(sc4[:, 16 + p4:17 + p4],
                                                 sc4[:, 12 + p4:13 + p4]), "vr")
                                v.wait_ge(sems["ss"], BB["ss"] + tau + 1)
                                inc(v.scalar_tensor_tensor(
                                    sc4[:, 20 + p4:21 + p4], sc4[:, 8 + p4:9 + p4],
                                    -1.0, sc4[:, 16 + p4:17 + p4],
                                    ALU.mult, ALU.mult), "vn")
                                v.wait_ge(sems["xs"], BB["xs"] + tau + 1)
                                v.tensor_reduce(smc[:, 16 + tau:17 + tau],
                                                xnt[:, p * 1024:(p + 1) * 1024],
                                                AXX, ALU.max, apply_absolute_value=True)
                                v.wait_ge(sems["tt"], BB["tt"] + tau + 1)
                                dst = xnT[:].rearrange("p (j s) -> p j s", j=8)[
                                    :, :, tau * 128:(tau + 1) * 128]
                                src = pstr1[:, p * 1024:(p + 1) * 1024].rearrange(
                                    "p (j c) -> p j c", j=8)
                                inc(v.tensor_copy(dst, src), "tc")

                        @blk.tensor
                        def _(t):
                            for tau in range(NT):
                                p = tau % 2
                                t.wait_ge(sems["xs"], BB["xs"] + tau + 1)
                                if tau >= 2:
                                    t.wait_ge(sems["tc"], BB["tc"] + tau - 1)
                                last = None
                                for jj in range(8):
                                    last = t.transpose(
                                        pstr1[:, p * 1024 + jj * 128:
                                              p * 1024 + (jj + 1) * 128],
                                        xnt[:, p * 1024 + jj * 128:
                                            p * 1024 + (jj + 1) * 128],
                                        id_sb[:])
                                inc(last, "tt")

                    if STAGE == "L0A" and b == 0:
                        break

                    with nc.Block() as blk:
                        BB = snap()

                        @blk.vector
                        def _(v):
                            v = DW(v)
                            v.tensor_scalar_add(smc[:, 16:32], smc[:, 16:32], 1e-8)
                            v.tensor_scalar(qcolb[:, 16:32], smc[:, 16:32],
                                            pcol[:, WSC:WSC + 1], 1.0 / 127.0,
                                            ALU.mult, ALU.mult)
                            v.reciprocal(smc[:, 16:32], smc[:, 16:32])
                            v.tensor_scalar_mul(smc[:, 16:32], smc[:, 16:32], 127.0)
                            inc(v.tensor_copy(qcolb[:, 0:16], smc[:, 16:32]), "vb")
                            for tq in range(16):
                                v.wait_ge(sems["bm"], BB["bm"] + 2 * tq + 1)
                                inc(v.tensor_copy(rsb[:, tq * 128:(tq + 1) * 128],
                                                  psbc[:, 0:128]), "bc")
                                v.wait_ge(sems["bm"], BB["bm"] + 2 * tq + 2)
                                inc(v.tensor_copy(ysb[:, tq * 128:(tq + 1) * 128],
                                                  psbc[:, 128:256]), "bc")
                            for jj in range(8):
                                xj = xnT[:, jj * 2048:(jj + 1) * 2048]
                                inc(v.tensor_mul(tf[:], xj, rsb[:]), "az")
                                v.wait_ge(sems["asc"], BB["asc"] + 2 * (jj + 1))
                                v.tensor_tensor(gtb[:], ti, tf2[:], ALU.is_gt)
                                inc(v.tensor_sub(aqT[:, jj * 2048:(jj + 1) * 2048],
                                                 ti, gtb[:]), "aq")
                            for jj in range(8):
                                xj = xnT[:, jj * 2048:(jj + 1) * 2048]
                                v.tensor_tensor_scan(
                                    h0[:], pcol[:, QC + 2 * jj:QC + 2 * jj + 1]
                                    .broadcast_to([128, 2048]), xj, 0.0,
                                    ALU.mult, ALU.add)
                                v.tensor_tensor_scan(
                                    h1[:], pcol[:, QC + 2 * jj + 1:QC + 2 * jj + 2]
                                    .broadcast_to([128, 2048]), xj, 0.0,
                                    ALU.mult, ALU.add)
                                v.tensor_scalar_mul(gtb[:], h0[:],
                                                    pcol[:, CC + 2 * jj:CC + 2 * jj + 1])
                                v.scalar_tensor_tensor(
                                    gtb[:], h1[:],
                                    pcol[:, CC + 2 * jj + 1:CC + 2 * jj + 2],
                                    gtb[:], ALU.mult, ALU.add)
                                inc(v.scalar_tensor_tensor(
                                    xj, xj, pcol[:, OMC + jj:OMC + jj + 1], gtb[:],
                                    ALU.mult, ALU.add), "cb")
                            for k in range(32):
                                tq, ot = k // 8, k % 8
                                v.wait_ge(sems["bmm"], BB["bmm"] + k + 1)
                                inc(v.tensor_mul(
                                    gT[:, ot * 2048 + tq * 512:ot * 2048 + (tq + 1) * 512],
                                    psbit[k % 2][:],
                                    ysb[:, tq * 512:(tq + 1) * 512]), "be")

                        @blk.scalar
                        def _(s):
                            s = DW(s)
                            s.wait_ge(sems["d10"], BB["d10"])
                            s.wait_ge(sems["d11"], BB["d11"])
                            s.wait_ge(sems["st0"], BB["st0"])
                            s.wait_ge(sems["st1"], BB["st1"])
                            for jj in range(8):
                                s.wait_ge(sems["az"], BB["az"] + jj + 1)
                                if jj >= 1:
                                    s.wait_ge(sems["aq"], BB["aq"] + jj)
                                s.activation(tf2[:], tf[:], AF.Identity,
                                             bias=halfc[:])
                                inc(s.copy(ti, tf2[:]), "asc", 2)
                            for jj in range(8):
                                s.wait_ge(sems["cb"], BB["cb"] + jj + 1)
                                seg = xnT[:, jj * 2048:(jj + 1) * 2048]
                                inc(s.activation(seg, seg, AF.Silu), "sl")
                            for k in range(32):
                                tq, ot = k // 8, k % 8
                                s.wait_ge(sems["be"], BB["be"] + k + 1)
                                seg = gT[:, ot * 2048 + tq * 512:
                                         ot * 2048 + (tq + 1) * 512]
                                inc(s.activation(seg, seg, AF.Silu), "gs")

                        @blk.tensor
                        def _(t):
                            t.wait_ge(sems["vb"], BB["vb"] + 1)
                            for tq in range(16):
                                if tq >= 1:
                                    t.wait_ge(sems["bc"], BB["bc"] + 2 * tq)
                                inc(t.matmul(psbc[:, 0:128],
                                             qcolb[:, tq:tq + 1].broadcast_to([128, 128]),
                                             id_sb[:], start=True, stop=True), "bm")
                                t.wait_ge(sems["bc"], BB["bc"] + 2 * tq + 1)
                                inc(t.matmul(psbc[:, 128:256],
                                             qcolb[:, 16 + tq:17 + tq]
                                             .broadcast_to([128, 128]),
                                             id_sb[:], start=True, stop=True), "bm")
                            t.wait_ge(sems["aq"], BB["aq"] + 8)
                            for k in range(32):
                                tq, ot = k // 8, k % 8
                                if k >= 2:
                                    t.wait_ge(sems["be"], BB["be"] + k - 1)
                                last = None
                                for it in range(8):
                                    last = t.matmul(
                                        psbit[k % 2][:],
                                        wqT[:, it * 1024 + ot * 128:
                                            it * 1024 + (ot + 1) * 128],
                                        aqT[:, it * 2048 + tq * 512:
                                            it * 2048 + (tq + 1) * 512],
                                        start=(it == 0), stop=(it == 7))
                                inc(last, "bmm")

                    if STAGE == "L0B" and b == 0:
                        break

                    with nc.Block() as blk:
                        BB = snap()

                        @blk.vector
                        def _(v):
                            v = DW(v)
                            for jj in range(8):
                                sl_ = xnT[:, jj * 2048:(jj + 1) * 2048]
                                inc(v.tensor_mul(sl_, sl_,
                                                 gT[:, jj * 2048:(jj + 1) * 2048]), "sg")
                            for tau in range(NT):
                                p = tau % 2
                                v.wait_ge(sems["et"], BB["et"] + tau + 1)
                                v.wait_ge(sems["d2" + str(p)],
                                          BB["d2" + str(p)] + 16 * (tau // 2 + 1))
                                if tau >= 2:
                                    v.wait_ge(sems["d1" + str(p)],
                                              BB["d1" + str(p)] + 16 * (tau // 2))
                                    if STAGE == "L0":
                                        v.wait_ge(sems["st" + str(p)],
                                                  BB["st" + str(p)] + 16 * (tau // 2))
                                inc(v.tensor_tensor(x1t[:, p * 1024:(p + 1) * 1024],
                                                    xt2[:, p * 1024:(p + 1) * 1024],
                                                    pstr1[:, p * 1024:(p + 1) * 1024],
                                                    ALU.add), "ex")
                                v.reduce_sum(smc[:, 0 + p:1 + p],
                                             x1t[:, p * 1024:(p + 1) * 1024], AXX)
                                v.tensor_scalar_mul(smc[:, 4 + p:5 + p],
                                                    smc[:, 0 + p:1 + p], 1.0 / D)
                                v.wait_ge(sems["q2"], BB["q2"] + tau + 1)
                                v.tensor_mul(smc[:, 12 + p:13 + p], smc[:, 4 + p:5 + p],
                                             smc[:, 4 + p:5 + p])
                                v.tensor_scalar(smc[:, 6 + p:7 + p], smc[:, 2 + p:3 + p],
                                                1.0 / D, None, ALU.mult)
                                v.tensor_sub(smc[:, 6 + p:7 + p], smc[:, 6 + p:7 + p],
                                             smc[:, 12 + p:13 + p])
                                v.tensor_scalar_add(smc[:, 6 + p:7 + p],
                                                    smc[:, 6 + p:7 + p], 1e-5)
                                inc(v.reciprocal(smc[:, 8 + p:9 + p],
                                                 smc[:, 6 + p:7 + p]), "v2")
                                v.wait_ge(sems["s2"], BB["s2"] + 2 * tau + 1)
                                inc(v.scalar_tensor_tensor(
                                    smc[:, 10 + p:11 + p], smc[:, 4 + p:5 + p], -1.0,
                                    smc[:, 8 + p:9 + p], ALU.mult, ALU.mult), "n2")
                                v.wait_ge(sems["a2"], BB["a2"] + 2 * (tau + 1))
                                v.reduce_sum(smc[:, 32 + tau:33 + tau],
                                             xn1t[:, p * 1024:(p + 1) * 1024], AXX)
                                v.tensor_scalar_mul(smc[:, 32 + tau:33 + tau],
                                                    smc[:, 32 + tau:33 + tau], 1.0 / D)
                                v.tensor_mul(smc[:, 12 + p:13 + p],
                                             smc[:, 32 + tau:33 + tau],
                                             smc[:, 32 + tau:33 + tau])
                                v.tensor_scalar(smc[:, 48 + tau:49 + tau],
                                                smc[:, 14 + p:15 + p], 1.0 / D,
                                                None, ALU.mult)
                                v.tensor_sub(smc[:, 48 + tau:49 + tau],
                                             smc[:, 48 + tau:49 + tau],
                                             smc[:, 12 + p:13 + p])
                                v.tensor_scalar_add(smc[:, 48 + tau:49 + tau],
                                                    smc[:, 48 + tau:49 + tau], 1e-5)
                                inc(v.reciprocal(smc[:, 48 + tau:49 + tau],
                                                 smc[:, 48 + tau:49 + tau]), "vv")
                                v.wait_ge(sems["t2"], BB["t2"] + tau + 1)
                                if tau >= 2:
                                    v.wait_ge(sems["d3" + str(p)],
                                              BB["d3" + str(p)] + 16 * (tau // 2))
                                inc(v.tensor_copy(stg[:, p * 1024:(p + 1) * 1024],
                                                  pstr2[:, p * 1024:(p + 1) * 1024]),
                                    "cs")

                        @blk.scalar
                        def _(s):
                            s = DW(s)
                            for tau in range(NT):
                                p = tau % 2
                                s.wait_ge(sems["ex"], BB["ex"] + tau + 1)
                                inc(s.activation(sqd, x1t[:, p * 1024:(p + 1) * 1024],
                                                 AF.Square,
                                                 accum_out=smc[:, 2 + p:3 + p]), "q2")
                                s.wait_ge(sems["v2"], BB["v2"] + tau + 1)
                                inc(s.activation(smc[:, 8 + p:9 + p],
                                                 smc[:, 8 + p:9 + p], AF.Sqrt), "s2")
                                s.wait_ge(sems["n2"], BB["n2"] + tau + 1)
                                if tau >= 2:
                                    s.wait_ge(sems["cs"], BB["cs"] + tau - 1)
                                s.activation(xn1t[:, p * 1024:(p + 1) * 1024],
                                             x1t[:, p * 1024:(p + 1) * 1024],
                                             AF.Identity, bias=smc[:, 10 + p:11 + p],
                                             scale=smc[:, 8 + p:9 + p])
                                inc(s.activation(sqd, xn1t[:, p * 1024:(p + 1) * 1024],
                                                 AF.Square,
                                                 accum_out=smc[:, 14 + p:15 + p]),
                                    "a2", 2)
                                s.wait_ge(sems["vv"], BB["vv"] + tau + 1)
                                inc(s.activation(smc[:, 48 + tau:49 + tau],
                                                 smc[:, 48 + tau:49 + tau], AF.Sqrt),
                                    "s2")

                        @blk.tensor
                        def _(t):
                            t.wait_ge(sems["sg"], BB["sg"] + 8)
                            for tau in range(NT):
                                p = tau % 2
                                if tau >= 2:
                                    t.wait_ge(sems["ex"], BB["ex"] + tau - 1)
                                last = None
                                for jj in range(8):
                                    last = t.transpose(
                                        pstr1[:, p * 1024 + jj * 128:
                                              p * 1024 + (jj + 1) * 128],
                                        xnT[:, jj * 2048 + tau * 128:
                                            jj * 2048 + (tau + 1) * 128],
                                        id_sb[:])
                                inc(last, "et")
                                t.wait_ge(sems["a2"], BB["a2"] + 2 * (tau + 1))
                                if tau >= 2:
                                    t.wait_ge(sems["cs"], BB["cs"] + tau - 1)
                                last = None
                                for jj in range(8):
                                    last = t.transpose(
                                        pstr2[:, p * 1024 + jj * 128:
                                              p * 1024 + (jj + 1) * 128],
                                        xn1t[:, p * 1024 + jj * 128:
                                            p * 1024 + (jj + 1) * 128],
                                        id_sb[:])
                                inc(last, "t2")

                        def _store(e, tau):
                            p = tau % 2
                            e.wait_ge(sems["ex"], BB["ex"] + tau + 1)
                            inc(e.dma_start(
                                x1sc.ap()[b * S + tau * 128:b * S + (tau + 1) * 128, :],
                                x1t[:, p * 1024:(p + 1) * 1024]), "d1" + str(p), 16)
                            if STAGE == "L0":
                                inc(e.dma_start(
                                    out.ap()[b * S + tau * 128:
                                             b * S + (tau + 1) * 128, :],
                                    x1t[:, p * 1024:(p + 1) * 1024]),
                                    "st" + str(p), 16)
                            e.wait_ge(sems["cs"], BB["cs"] + tau + 1)
                            dst = xn1sc.ap()[b, :, :].rearrange(
                                "(j p) s -> p j s", p=128)[:, :,
                                                           tau * 128:(tau + 1) * 128]
                            src = stg[:, p * 1024:(p + 1) * 1024].rearrange(
                                "p (j c) -> p j c", j=8)
                            inc(e.dma_start(dst, src), "d3" + str(p), 16)

                        @blk.sync
                        def _(e):
                            for tau in range(NT):
                                p = tau % 2
                                if tau >= 2:
                                    _store(e, tau - 2)
                                inc(e.dma_start(
                                    xt2[:, p * 1024:(p + 1) * 1024],
                                    xin.ap()[b * S + tau * 128:b * S + (tau + 1) * 128, :]),
                                    "d2" + str(p), 16)
                            _store(e, NT - 2)
                            _store(e, NT - 1)

                    with nc.Block() as blk:
                        @blk.vector
                        def _(v):
                            v = DW(v)
                            v.tensor_copy(r2cb[:, b * 16:(b + 1) * 16], smc[:, 48:64])
                            v.tensor_mul(smc[:, 32:48], smc[:, 32:48], smc[:, 48:64])
                            v.tensor_copy(m2cb[:, b * 16:(b + 1) * 16], smc[:, 32:48])

            if STAGE in ("L0A", "L0B"):
                with nc.Block() as blk:
                    @blk.gpsimd
                    def _(g):
                        dst = out.ap().rearrange("(x p) d -> p x d", p=128)[:, 0:16, :]
                        src = (xnT[:] if STAGE == "L0A" else gT[:]).rearrange(
                            "p (j s) -> p j s", j=16)
                        inc(g.dma_start(dst, src), "st", 16)
                        g.wait_ge(sems["st"], semv["st"])
                return nc

        if STAGE == "L0":
            with nc.Block() as blk:
                @blk.sync
                def _(e):
                    e.wait_ge(sems["st0"], semv["st0"])
                    e.wait_ge(sems["st1"], semv["st1"])
            return nc

        with (
            nc.sbuf_tensor([128, 8 * F_FFN], BF16) as w1sb,
            nc.sbuf_tensor([128, FT * 1024], BF16) as w2sb,
        ):
            with nc.sbuf_tensor([128, 2 * F_FFN], F32) as scb:
                with nc.Block() as blk:
                    BB = snap()

                    @blk.gpsimd
                    def _(g):
                        for nm in ("d10", "d11", "d30", "d31"):
                            g.wait_ge(sems[nm], BB[nm])
                        for gidx in range(8 + FT):
                            p = gidx % 2
                            if gidx >= 2:
                                g.wait_ge(sems["vq"], BB["vq"] + gidx - 1)
                            if gidx < 8:
                                scsrc = fc1s.ap()[gidx:gidx + 1, :]
                                wdst = w1sb[:, gidx * F_FFN:(gidx + 1) * F_FFN]
                                wsrc = fc1q.ap()[gidx * 128:(gidx + 1) * 128, :]
                                scw = F_FFN
                            else:
                                gg = gidx - 8
                                scsrc = fc2s.ap()[gg:gg + 1, :]
                                wdst = w2sb[:, gg * 1024:(gg + 1) * 1024]
                                wsrc = fc2q.ap()[gg * 128:(gg + 1) * 128, :]
                                scw = 1024
                            inc(g.dma_start(scb[:, p * F_FFN:p * F_FFN + scw],
                                            scsrc.partition_broadcast(128)),
                                "ds" + str(p), 16)
                            inc(g.dma_start(wdst, wsrc), "dq" + str(p), 16)

                    @blk.vector
                    def _(v):
                        v = DW(v)
                        for gidx in range(8 + FT):
                            p = gidx % 2
                            v.wait_ge(sems["ds" + str(p)],
                                      BB["ds" + str(p)] + 16 * (gidx // 2 + 1))
                            v.wait_ge(sems["dq" + str(p)],
                                      BB["dq" + str(p)] + 16 * (gidx // 2 + 1))
                            if gidx < 8:
                                seg = w1sb[:, gidx * F_FFN:(gidx + 1) * F_FFN]
                                scw = F_FFN
                            else:
                                gg = gidx - 8
                                seg = w2sb[:, gg * 1024:(gg + 1) * 1024]
                                scw = 1024
                            inc(v.scalar_tensor_tensor(
                                seg, seg, -128.0, scb[:, p * F_FFN:p * F_FFN + scw],
                                ALU.add, ALU.mult), "vq")

            with contextlib.ExitStack() as p3s:
                A3 = lambda sh, dt_: p3s.enter_context(nc.sbuf_tensor(_nm('A3'), sh, dt_))
                P3 = lambda sh, dt_: p3s.enter_context(nc.psum_tensor(_nm('P3'), sh, dt_))
                TCW = 512
                NTC = S // TCW
                xsl = A3([128, 8 * TCW], BF16)
                hsl = A3([128, 8 * TCW], BF16)
                y1s = A3([128, FT * TCW], BF16)
                r2b = A3([128, TCW], F32)
                m2b = A3([128, TCW], F32)
                htmp = A3([128, 4 * TCW], F32)
                tgs = A3([128, 2 * TCW], F32)
                gstg = A3([128, 8 * TCW], BF16)
                pf = [P3([128, 512], F32) for _ in range(4)]
                pg = [P3([128, 512], F32) for _ in range(2)]
                psbc2 = P3([128, 1024], F32)
                for b in range(B_PER):
                    for tc in range(NTC):
                        with nc.Block() as blk:
                            BB = snap()

                            @blk.sync
                            def _(e):
                                e.wait_ge(sems["d30"], BB["d30"])
                                e.wait_ge(sems["d31"], BB["d31"])
                                if tc >= 1 or b >= 1:
                                    e.wait_ge(sems["vm"], BB["vm"])
                                dst = xsl[:].rearrange("p (j c) -> p j c", j=8)
                                src = xn1sc.ap()[b, :, :].rearrange(
                                    "(j p) s -> p j s", p=128)[:, :,
                                                               tc * TCW:(tc + 1) * TCW]
                                inc(e.dma_start(dst, src), "dxa", 16)

                            @blk.tensor
                            def _(t):
                                last = None
                                for u in range(4):
                                    tau = tc * 4 + u
                                    last = t.matmul(
                                        psbc2[:, u * 128:(u + 1) * 128],
                                        m2cb[:, b * 16 + tau:b * 16 + tau + 1]
                                        .broadcast_to([128, 128]),
                                        id_sb[:], start=True, stop=True)
                                for u in range(4):
                                    tau = tc * 4 + u
                                    last = t.matmul(
                                        psbc2[:, 512 + u * 128:512 + (u + 1) * 128],
                                        r2cb[:, b * 16 + tau:b * 16 + tau + 1]
                                        .broadcast_to([128, 128]),
                                        id_sb[:], start=True, stop=True)
                                inc(last, "bb", 2)
                                t.wait_ge(sems["hs"], BB["hs"] + 8)
                                for ft in range(FT):
                                    if ft >= 4:
                                        t.wait_ge(sems["fe"], BB["fe"] + ft - 3)
                                    last = None
                                    for jj in range(8):
                                        last = t.matmul(
                                            pf[ft % 4][:, 0:TCW],
                                            w1sb[:, jj * F_FFN + ft * 128:
                                                 jj * F_FFN + (ft + 1) * 128],
                                            hsl[:, jj * TCW:(jj + 1) * TCW],
                                            start=(jj == 0), stop=(jj == 7))
                                    inc(last, "f1")
                                t.wait_ge(sems["fe"], BB["fe"] + FT)
                                for ht in range(8):
                                    if ht >= 2:
                                        t.wait_ge(sems["vm"], BB["vm"] + ht - 1)
                                    last = None
                                    for ft in range(FT):
                                        last = t.matmul(
                                            pg[ht % 2][:, 0:TCW],
                                            w2sb[:, ft * 1024 + ht * 128:
                                                 ft * 1024 + (ht + 1) * 128],
                                            y1s[:, ft * TCW:(ft + 1) * TCW],
                                            start=(ft == 0), stop=(ft == FT - 1))
                                    inc(last, "f2")

                            @blk.vector
                            def _(v):
                                v = DW(v)
                                v.wait_ge(sems["bb"], BB["bb"] + 2)
                                inc(v.tensor_copy(m2b[:], psbc2[:, 0:TCW]), "bc")
                                inc(v.tensor_copy(r2b[:], psbc2[:, 512:512 + TCW]), "bc")
                                v.wait_ge(sems["dxa"], BB["dxa"] + 16)
                                for jj in range(8):
                                    hp = jj % 2
                                    if jj >= 2:
                                        v.wait_ge(sems["hs"], BB["hs"] + jj - 1)
                                    xj = xsl[:, jj * TCW:(jj + 1) * TCW]
                                    v.tensor_tensor(
                                        htmp[:, hp * 2 * TCW:hp * 2 * TCW + TCW],
                                        xj, r2b[:], ALU.mult)
                                    inc(v.tensor_sub(
                                        htmp[:, hp * 2 * TCW + TCW:(hp + 1) * 2 * TCW],
                                        htmp[:, hp * 2 * TCW:hp * 2 * TCW + TCW],
                                        m2b[:]), "hm")
                                for ht in range(8):
                                    gp = ht % 2
                                    v.wait_ge(sems["f2"], BB["f2"] + ht + 1)
                                    if ht >= 2:
                                        v.wait_ge(sems["ge"], BB["ge"] + ht - 1)
                                    xj = xsl[:, ht * TCW:(ht + 1) * TCW]
                                    inc(v.scalar_tensor_tensor(
                                        tgs[:, gp * TCW:(gp + 1) * TCW],
                                        pg[ht % 2][:, 0:TCW], 0.5, xj,
                                        ALU.mult, ALU.add), "vm")

                            @blk.scalar
                            def _(s):
                                s = DW(s)
                                for jj in range(8):
                                    hp = jj % 2
                                    s.wait_ge(sems["hm"], BB["hm"] + jj + 1)
                                    inc(s.activation(
                                        hsl[:, jj * TCW:(jj + 1) * TCW],
                                        htmp[:, hp * 2 * TCW + TCW:(hp + 1) * 2 * TCW],
                                        AF.Identity,
                                        bias=bcol[:, LNB0 + jj:LNB0 + jj + 1],
                                        scale=bcol[:, LNG0 + jj:LNG0 + jj + 1]), "hs")
                                for ft in range(FT):
                                    s.wait_ge(sems["f1"], BB["f1"] + ft + 1)
                                    inc(s.activation(y1s[:, ft * TCW:(ft + 1) * TCW],
                                                     pf[ft % 4][:, 0:TCW], AF.Silu,
                                                     bias=f1bcol[:, ft:ft + 1]), "fe")
                                s.wait_ge(sems["dga"], BB["dga"])
                                for ht in range(8):
                                    gp = ht % 2
                                    s.wait_ge(sems["vm"], BB["vm"] + ht + 1)
                                    inc(s.activation(
                                        gstg[:, ht * TCW:(ht + 1) * TCW],
                                        tgs[:, gp * TCW:(gp + 1) * TCW], AF.Silu,
                                        bias=bcol[:, HB20 + ht:HB20 + ht + 1]), "ge")

                            @blk.gpsimd
                            def _(g):
                                g.wait_ge(sems["ge"], BB["ge"] + 8)
                                dst = gtsc.ap()[b, :, :].rearrange(
                                    "(j p) s -> p j s", p=128)[:, :,
                                                               tc * TCW:(tc + 1) * TCW]
                                src = gstg[:].rearrange("p (j c) -> p j c", j=8)
                                inc(g.dma_start(dst, src), "dga", 16)

        if True:
            VROW = 70
            TROW = H * VROW
            SCHED = attn_sched()
            NCH = len(SCHED)
            CB = [0] * (NT + 1)
            for (kt_, q_, l_, f_) in SCHED:
                CB[kt_ + 1] += 1
            for i in range(NT):
                CB[i + 1] += CB[i]

            for b in range(B_PER):
                with contextlib.ExitStack() as p4b:
                    A4 = lambda sh, dt_: p4b.enter_context(nc.sbuf_tensor(_nm('A4'), sh, dt_))
                    xnt1 = A4([128, 8 * S], BF16)
                    qT = A4([128, 8 * S], BF16)
                    kT = A4([128, 8 * S], BF16)
                    vext = A4([128, NT * TROW], BF16)
                    with nc.Block() as blk:
                        @blk.sync
                        def _(e):
                            e.wait_ge(sems["d30"], semv["d30"])
                            e.wait_ge(sems["d31"], semv["d31"])
                            e.wait_ge(sems["dga"], semv["dga"])
                            e.wait_ge(sems["st0"], semv["st0"])
                            e.wait_ge(sems["st1"], semv["st1"])
                            dst = xnt1[:].rearrange("p (j s) -> p j s", j=8)
                            src = xn1sc.ap()[b, :, :].rearrange("(j p) s -> p j s", p=128)
                            inc(e.dma_start(dst, src), "dxa", 16)
                            e.wait_ge(sems["dxa"], semv["dxa"])

                    with contextlib.ExitStack() as p4p:
                        A5 = lambda sh, dt_: p4p.enter_context(nc.sbuf_tensor(_nm('A5'), sh, dt_))
                        P5 = lambda sh, dt_: p4p.enter_context(nc.psum_tensor(_nm('P5'), sh, dt_))
                        wvsb = A5([128, 8 * 1024], BF16)
                        wqsb = A5([128, 8 * 1024], BF16)
                        wksb = A5([128, 8 * 1024], BF16)
                        qbb = A5([128, 4 * 512], BF16)
                        t1b = A5([128, 512], BF16)
                        t2b = A5([128, 512], BF16)
                        cosb = A5([128, S], BF16)
                        sinb = A5([128, S], BF16)
                        bvb = A5([128, 1024], F32)
                        pp = [P5([128, 512], F32) for _ in range(4)]
                        pj = [P5([128, 512], F32) for _ in range(2)]
                        psv = P5([128, 1024], F32)
                        with nc.Block() as blk:
                            BB = snap()

                            @blk.sync
                            def _(e):
                                inc(e.dma_start(cosb[:], cosd.ap()), "dp", 16)
                                inc(e.dma_start(sinb[:], sind.ap()), "dp", 16)

                            @blk.gpsimd
                            def _(g):
                                inc(g.dma_start(
                                    bvb[:], bvr.ap().partition_broadcast(128)),
                                    "dbv", 16)
                                for wi, wd in enumerate([wqd, wkd]):
                                    dst = (wqsb if wi == 0 else wksb)[:].rearrange(
                                        "p (j c) -> p j c", j=8)
                                    src = wd.ap().rearrange(
                                        "(j p) hd -> p j hd", p=128)
                                    inc(g.dma_start(dst, src),
                                        "dq" + str(wi), 16)
                                dst = wvsb[:].rearrange("p (j c) -> p j c", j=8)
                                src = wvd.ap().rearrange("(j p) hd -> p j hd", p=128)
                                inc(g.dma_start(dst, src), "dq", 16)

                            @blk.tensor
                            def _(t):
                                kk = 0
                                for wi in range(2):
                                    wsb = wqsb if wi == 0 else wksb
                                    t.wait_ge(sems["dq" + str(wi)],
                                              BB["dq" + str(wi)] + 16)
                                    for jt in range(8):
                                        for tq in range(4):
                                            if kk >= 4:
                                                t.wait_ge(sems["qb"], BB["qb"] + kk - 3)
                                            last = None
                                            for jj in range(8):
                                                last = t.matmul(
                                                    pp[kk % 4][:],
                                                    wsb[:, jj * 1024 + jt * 128:
                                                        jj * 1024 + (jt + 1) * 128],
                                                    xnt1[:, jj * S + tq * 512:
                                                         jj * S + (tq + 1) * 512],
                                                    start=(jj == 0), stop=(jj == 7))
                                            inc(last, "qm")
                                            t.wait_ge(sems["qb"], BB["qb"] + kk + 1)
                                            if kk >= 2:
                                                t.wait_ge(sems["rp"], BB["rp"] + kk - 1)
                                            inc(t.matmul(
                                                pj[kk % 2][:], j_sb[:],
                                                qbb[:, (kk % 4) * 512:(kk % 4 + 1) * 512],
                                                start=True, stop=True), "jm")
                                            kk += 1
                                t.wait_ge(sems["dq"], BB["dq"] + 16)
                                t.wait_ge(sems["dbv"], BB["dbv"] + 16)
                                for tau in range(NT):
                                    if tau >= 1:
                                        t.wait_ge(sems["ve"], BB["ve"] + tau)
                                    last = None
                                    for jj in range(8):
                                        for hh in range(2):
                                            last = t.matmul(
                                                psv[:, hh * 512:(hh + 1) * 512],
                                                xnt1[:, jj * S + tau * 128:
                                                     jj * S + (tau + 1) * 128],
                                                wvsb[:, jj * 1024 + hh * 512:
                                                     jj * 1024 + (hh + 1) * 512],
                                                start=(jj == 0), stop=(jj == 7))
                                    inc(last, "vm")

                            @blk.scalar
                            def _(s):
                                s = DW(s)
                                kk = 0
                                for wi in range(2):
                                    bb_ = [BQ0, BK0][wi]
                                    for jt in range(8):
                                        for tq in range(4):
                                            s.wait_ge(sems["qm"],
                                                      BB["qm"] + kk + 1)
                                            if kk >= 4:
                                                s.wait_ge(sems["jm"],
                                                          BB["jm"] + kk - 3)
                                                s.wait_ge(sems["rp"],
                                                          BB["rp"] + kk - 3)
                                            inc(s.activation(
                                                qbb[:, (kk % 4) * 512:
                                                    (kk % 4 + 1) * 512],
                                                pp[kk % 4][:], AF.Identity,
                                                bias=bcol[:, bb_ + jt:bb_ + jt + 1]),
                                                "qb")
                                            kk += 1

                            @blk.vector
                            def _(v):
                                v = DW(v)
                                v.wait_ge(sems["dp"], BB["dp"] + 16 * 2)
                                v.memset(vext[:], 0.0)
                                ap_ones = vext[:].rearrange(
                                    "p (tau g i) -> p tau g i", tau=NT, g=4)[
                                    :, :, :, 0:1]
                                ap1 = vext[:].rearrange(
                                    "p (tau rest) -> p tau rest", tau=NT)[
                                    :, :, 64:64 + 280 * 3 + 71 * 3 + 1]
                                for ii in range(4):
                                    ap2 = vext[:].rearrange(
                                        "p (tau g rest) -> p tau g rest",
                                        tau=NT, g=4)[:, :, :, 70 * ii + 64 + ii:
                                                     70 * ii + 64 + ii + 1]
                                    v.memset(ap2, 1.0)
                                kk = 0
                                for wi in range(2):
                                    bb_ = [BQ0, BK0][wi]
                                    dstT = [qT, kT][wi]
                                    for jt in range(8):
                                        for tq in range(4):
                                            v.wait_ge(sems["qb"], BB["qb"] + kk + 1)
                                            v.tensor_mul(
                                                t1b[:],
                                                qbb[:, (kk % 4) * 512:(kk % 4 + 1) * 512],
                                                cosb[:, tq * 512:(tq + 1) * 512])
                                            v.wait_ge(sems["jm"], BB["jm"] + kk + 1)
                                            v.tensor_tensor(
                                                t2b[:], pj[kk % 2][:],
                                                sinb[:, tq * 512:(tq + 1) * 512],
                                                ALU.mult)
                                            inc(v.tensor_add(
                                                dstT[:, jt * S + tq * 512:
                                                     jt * S + (tq + 1) * 512],
                                                t1b[:], t2b[:]), "rp")
                                            kk += 1
                                for tau in range(NT):
                                    v.wait_ge(sems["vm"], BB["vm"] + tau + 1)
                                    dst = vext[:, tau * TROW:(tau + 1) * TROW].rearrange(
                                        "p (h c) -> p h c", h=H)[:, :, 0:64]
                                    src = psv[:].rearrange("p (h c) -> p h c", h=H)
                                    bsr = bvb[:].rearrange("p (h c) -> p h c", h=H)
                                    inc(v.tensor_tensor(dst, src, bsr, ALU.add), "ve")

                    with contextlib.ExitStack() as p4a:
                        A6 = lambda sh, dt_: p4a.enter_context(nc.sbuf_tensor(_nm('A6'), sh, dt_))
                        oallT = A6([128, 8 * S], BF16)
                        sums = A6([16, S], F32)
                        with contextlib.ExitStack() as p4c:
                            A7 = lambda sh, dt_: p4c.enter_context(nc.sbuf_tensor(_nm('A7'), sh, dt_))
                            P7 = lambda sh, dt_: p4c.enter_context(nc.psum_tensor(_nm('P7'), sh, dt_))
                            ptb = A7([128, 2 * S], BF16)
                            stage4 = A7([4, 2 * S], F32)
                            pst = [P7([128, 1024], F32) for _ in range(2)]
                            pso = [P7([68, 1024], F32) for _ in range(2)]
                            LN_ = ("A", "B")

                            def job_chunks(g_, qh):
                                ch = []
                                ktmax = 8 * qh + 7
                                for hm, i_ in enumerate((3, 2, 1, 0)):
                                    h = 4 * g_ + i_
                                    for kt in range(ktmax + 1):
                                        q0g = max(kt * 128, qh * 1024)
                                        ch.append(dict(
                                            hm=hm, h=h, i=i_, kt=kt, q0g=q0g,
                                            q0r=q0g - qh * 1024,
                                            L=qh * 1024 + 1024 - q0g,
                                            first=(kt == 0), last=(kt == ktmax),
                                            diag=(q0g == kt * 128)))
                                return ch

                            PAIRS = [((0, 1), (1, 1)), ((2, 1), (3, 1)),
                                     ((0, 0), (1, 0)), ((2, 0), (3, 0))]
                            for (jA, jB) in PAIRS:
                                jobs = (jA, jB)
                                CH = [job_chunks(*jA), job_chunks(*jB)]
                                n = len(CH[0])
                                META = []
                                for li in range(2):
                                    cumE = cumS = cumPV = mcnt = 0
                                    lastE = [0, 0]
                                    cumPVl = []
                                    lst = []
                                    for ci, c in enumerate(CH[li]):
                                        sp = [(off, min(512, c["L"] - off))
                                              for off in range(0, c["L"], 512)]
                                        e_before = cumE
                                        s_before = cumS
                                        swaits = [lastE[off // 512]
                                                  for (off, _) in sp]
                                        for idx, (off, _) in enumerate(sp):
                                            lastE[off // 512] = e_before + idx + 1
                                        cumE += len(sp)
                                        cumS += len(sp)
                                        pv = []
                                        for (pa, pl) in aligned_pieces(
                                                c["q0r"], c["L"]):
                                            k = -(-(pa - c["q0r"] + pl) // 512)
                                            pv.append((pa, pl, e_before + k,
                                                       c["diag"] and
                                                       (pa - c["q0r"]) < 128))
                                        if c["diag"]:
                                            mcnt += 1
                                        pv_before = cumPV
                                        cumPV += len(pv)
                                        cumPVl.append(cumPV)
                                        lst.append(dict(
                                            c=c, sp=sp, swaits=swaits,
                                            s_before=s_before,
                                            e_before=e_before, pv=pv,
                                            mcnt=mcnt, pv_after=cumPV,
                                            pv_upto2=(cumPVl[ci - 2]
                                                      if ci >= 2 else 0)))
                                    META.append(lst)
                                with nc.Block() as blk:
                                    BB = snap()

                                    @blk.tensor
                                    def _(t, META=META, n=n):
                                        for cstep in range(n + 1):
                                            for li in range(2):
                                                if cstep >= n:
                                                    continue
                                                ci = cstep
                                                m = META[li][ci]
                                                c = m["c"]
                                                xe = "xe" + LN_[li]
                                                h = c["h"]
                                                jt, r0 = h // 2, (h % 2) * 64
                                                for (off, Lp), sw in zip(
                                                        m["sp"], m["swaits"]):
                                                    t.wait_ge(sems[xe],
                                                              BB[xe] + sw)
                                                    inc(t.matmul(
                                                        pst[li][:, off:off + Lp],
                                                        kT[r0:r0 + 64,
                                                           jt * S + c["kt"] * 128:
                                                           jt * S + (c["kt"] + 1) * 128],
                                                        qT[r0:r0 + 64,
                                                           jt * S + c["q0g"] + off:
                                                           jt * S + c["q0g"] + off + Lp],
                                                        start=True, stop=True),
                                                        "stm" + LN_[li])
                                            for li in range(2):
                                                if cstep < 1:
                                                    continue
                                                ci = cstep - 1
                                                m = META[li][ci]
                                                c = m["c"]
                                                xe = "xe" + LN_[li]
                                                if c["first"]:
                                                    t.wait_ge(sems["oc" + LN_[li]],
                                                              BB["oc" + LN_[li]]
                                                              + c["hm"])
                                                h, i_, kt = c["h"], c["i"], c["kt"]
                                                base = li * 2048 + (ci % 2) * 1024
                                                for (pa, pl, ereq, needm) in m["pv"]:
                                                    t.wait_ge(sems[xe],
                                                              BB[xe] + ereq)
                                                    if needm:
                                                        t.wait_ge(
                                                            sems["msk" + LN_[li]],
                                                            BB["msk" + LN_[li]]
                                                            + m["mcnt"])
                                                    inc(t.matmul(
                                                        pso[li][0:65 + i_, pa:pa + pl],
                                                        vext[:, kt * TROW + VROW * h:
                                                             kt * TROW + VROW * h
                                                             + 65 + i_],
                                                        ptb[:, base + pa - c["q0r"]:
                                                            base + pa - c["q0r"] + pl],
                                                        start=c["first"],
                                                        stop=c["last"],
                                                        skip_group_check=True),
                                                        "pvm" + LN_[li])

                                    @blk.scalar
                                    def _(s, META=META, n=n):
                                        s = DW(s)
                                        for ci in range(n):
                                            for li in range(2):
                                                m = META[li][ci]
                                                base = li * 2048 + (ci % 2) * 1024
                                                s.wait_ge(sems["pvm" + LN_[li]],
                                                          BB["pvm" + LN_[li]]
                                                          + m["pv_upto2"])
                                                for idx, (off, Lp) in enumerate(
                                                        m["sp"]):
                                                    s.wait_ge(
                                                        sems["stm" + LN_[li]],
                                                        BB["stm" + LN_[li]]
                                                        + m["s_before"] + idx + 1)
                                                    inc(s.activation(
                                                        ptb[:, base + off:
                                                            base + off + Lp],
                                                        pst[li][:, off:off + Lp],
                                                        AF.Exp,
                                                        scale=0.125),
                                                        "xe" + LN_[li])

                                    @blk.vector
                                    def _(v, jobs=jobs, META=META, n=n):
                                        v = DW(v)
                                        for ci in range(n):
                                            for li in range(2):
                                                m = META[li][ci]
                                                c = m["c"]
                                                if c["diag"]:
                                                    v.wait_ge(sems["xe" + LN_[li]],
                                                              BB["xe" + LN_[li]]
                                                              + m["e_before"] + 1)
                                                    base = (li * 2048
                                                            + (ci % 2) * 1024)
                                                    seg = ptb[:, base:base + 128]
                                                    inc(v.tensor_mul(seg, seg,
                                                                     mask_sb[:]),
                                                        "msk" + LN_[li])
                                                if c["last"]:
                                                    g_, qh = jobs[li]
                                                    h = c["h"]
                                                    v.wait_ge(sems["pvm" + LN_[li]],
                                                              BB["pvm" + LN_[li]]
                                                              + m["pv_after"])
                                                    inc(v.tensor_copy(
                                                        oallT[(h % 2) * 64:
                                                              (h % 2) * 64 + 64,
                                                              (h // 2) * S + qh * 1024:
                                                              (h // 2) * S
                                                              + qh * 1024 + 1024],
                                                        pso[li][0:64, :]),
                                                        "oc" + LN_[li])
                                                    if c["hm"] == 3:
                                                        v.wait_ge(
                                                            sems["ssd" + LN_[li]],
                                                            BB["ssd" + LN_[li]])
                                                        inc(v.tensor_copy(
                                                            stage4[0:4, li * 1024:
                                                                   li * 1024 + 1024],
                                                            pso[li][64:68, :]),
                                                            "oc" + LN_[li])

                                    @blk.sync
                                    def _(e, jobs=jobs):
                                        for li in range(2):
                                            g_, qh = jobs[li]
                                            e.wait_ge(sems["oc" + LN_[li]],
                                                      BB["oc" + LN_[li]] + 5)
                                            inc(e.dma_start(
                                                sums[4 * g_:4 * g_ + 4,
                                                     qh * 1024:qh * 1024 + 1024],
                                                stage4[0:4, li * 1024:
                                                       li * 1024 + 1024]),
                                                "ssd" + LN_[li], 16)

                        with contextlib.ExitStack() as p4w:
                            A8 = lambda sh, dt_: p4w.enter_context(nc.sbuf_tensor(_nm('A8'), sh, dt_))
                            P8 = lambda sh, dt_: p4w.enter_context(nc.psum_tensor(_nm('P8'), sh, dt_))
                            wosb = A8([128, 8 * 1024], BF16)
                            sums_bf = A8([16, S], BF16)
                            pw = [P8([128, 512], F32) for _ in range(2)]
                            pnb = P8([128, 512], F32)
                            with nc.Block() as blk:
                                BB = snap()

                                @blk.gpsimd
                                def _(e):
                                    e.wait_ge(sems["ssdA"], BB["ssdA"])
                                    e.wait_ge(sems["ssdB"], BB["ssdB"])
                                    dst = wosb[:].rearrange("p (j c) -> p j c", j=8)
                                    src = wod.ap().rearrange("(j p) e -> p j e", p=128)
                                    inc(e.dma_start(dst, src), "dq", 16)

                                @blk.vector
                                def _(v):
                                    v = DW(v)
                                    v.wait_ge(sems["ssdA"], BB["ssdA"])
                                    v.wait_ge(sems["ssdB"], BB["ssdB"])
                                    v.reciprocal(sums[:], sums[:])
                                    inc(v.tensor_copy(sums_bf[:], sums[:]), "rc")
                                    k = 0
                                    for jt in range(8):
                                        for tq in range(4):
                                            v.wait_ge(sems["nbm"], BB["nbm"] + k + 1)
                                            seg = oallT[:, jt * S + tq * 512:
                                                        jt * S + (tq + 1) * 512]
                                            inc(v.tensor_tensor(seg, seg, pnb[:],
                                                                ALU.mult), "nbc")
                                            k += 1
                                    k = 0
                                    for et in range(8):
                                        for tq in range(4):
                                            v.wait_ge(sems["wm"], BB["wm"] + k + 1)
                                            inc(v.tensor_scalar(
                                                xnt1[:, et * S + tq * 512:
                                                     et * S + (tq + 1) * 512],
                                                pw[k % 2][:],
                                                bcol[:, BO0 + et:BO0 + et + 1], None,
                                                ALU.add), "we")
                                            k += 1

                                @blk.tensor
                                def _(t):
                                    t.wait_ge(sems["rc"], BB["rc"] + 1)
                                    k = 0
                                    for jt in range(8):
                                        for tq in range(4):
                                            if k >= 1:
                                                t.wait_ge(sems["nbc"], BB["nbc"] + k)
                                            inc(t.matmul(
                                                pnb[:], ind_sb[:, jt * 128:(jt + 1) * 128],
                                                sums_bf[:, tq * 512:(tq + 1) * 512],
                                                start=True, stop=True), "nbm")
                                            k += 1
                                    t.wait_ge(sems["dq"], BB["dq"] + 16)
                                    t.wait_ge(sems["nbc"], BB["nbc"] + 32)
                                    k = 0
                                    for et in range(8):
                                        for tq in range(4):
                                            if k >= 2:
                                                t.wait_ge(sems["we"], BB["we"] + k - 1)
                                            last = None
                                            for jj in range(8):
                                                last = t.matmul(
                                                    pw[k % 2][:],
                                                    wosb[:, jj * 1024 + et * 128:
                                                         jj * 1024 + (et + 1) * 128],
                                                    oallT[:, jj * S + tq * 512:
                                                          jj * S + (tq + 1) * 512],
                                                    start=(jj == 0), stop=(jj == 7))
                                            inc(last, "wm")
                                            k += 1

                    with contextlib.ExitStack() as p4e:
                        A9 = lambda sh, dt_: p4e.enter_context(nc.sbuf_tensor(_nm('A9'), sh, dt_))
                        xta = A9([128, 2 * 1024], F32)
                        x2t = A9([128, 2 * 1024], F32)
                        pstr = p4e.enter_context(nc.psum_tensor(_nm('pstr_e'), [128, 2048], BF16))
                        with nc.Block() as blk:
                            BB = snap()

                            @blk.sync
                            def _(e):
                                e.wait_ge(sems["dga"], BB["dga"])
                                e.wait_ge(sems["d10"], BB["d10"])
                                e.wait_ge(sems["d11"], BB["d11"])
                                dst = qT[:].rearrange("p (j s) -> p j s", j=8)
                                src = gtsc.ap()[b, :, :].rearrange(
                                    "(j p) s -> p j s", p=128)
                                inc(e.dma_start(dst, src), "dg", 16)
                                def _store(tau):
                                    p = tau % 2
                                    e.wait_ge(sems["ex"], BB["ex"] + tau + 1)
                                    inc(e.dma_start(
                                        out.ap()[b * S + tau * 128:
                                                 b * S + (tau + 1) * 128, :],
                                        x2t[:, p * 1024:(p + 1) * 1024]),
                                        "st" + str(p), 16)

                                for tau in range(NT):
                                    p = tau % 2
                                    if tau >= 2:
                                        _store(tau - 2)
                                    inc(e.dma_start(
                                        xta[:, p * 1024:(p + 1) * 1024],
                                        x1sc.ap()[b * S + tau * 128:
                                                  b * S + (tau + 1) * 128, :]),
                                        "d2" + str(p), 16)
                                _store(NT - 2)
                                _store(NT - 1)

                            @blk.vector
                            def _(v):
                                v = DW(v)
                                v.wait_ge(sems["dg"], BB["dg"] + 16)
                                for jj in range(8):
                                    seg = xnt1[:, jj * S:(jj + 1) * S]
                                    inc(v.tensor_mul(seg, seg, qT[:, jj * S:(jj + 1) * S]),
                                        "sg")
                                for tau in range(NT):
                                    p = tau % 2
                                    v.wait_ge(sems["et"], BB["et"] + tau + 1)
                                    v.wait_ge(sems["d2" + str(p)],
                                              BB["d2" + str(p)] + 16 * (tau // 2 + 1))
                                    if tau >= 2:
                                        v.wait_ge(sems["st" + str(p)],
                                                  BB["st" + str(p)] + 16 * (tau // 2))
                                    inc(v.tensor_tensor(
                                        x2t[:, p * 1024:(p + 1) * 1024],
                                        xta[:, p * 1024:(p + 1) * 1024],
                                        pstr[:, p * 1024:(p + 1) * 1024],
                                        ALU.add), "ex")

                            @blk.tensor
                            def _(t):
                                t.wait_ge(sems["sg"], BB["sg"] + 8)
                                for tau in range(NT):
                                    p = tau % 2
                                    if tau >= 2:
                                        t.wait_ge(sems["ex"], BB["ex"] + tau - 1)
                                    last = None
                                    for jj in range(8):
                                        last = t.transpose(
                                            pstr[:, p * 1024 + jj * 128:
                                                 p * 1024 + (jj + 1) * 128],
                                            xnt1[:, jj * S + tau * 128:
                                                 jj * S + (tau + 1) * 128],
                                            id_sb[:])
                                    inc(last, "et")

        with nc.Block() as blk:
            @blk.sync
            def _(e):
                e.wait_ge(sems["st0"], semv["st0"])
                e.wait_ge(sems["st1"], semv["st1"])

    return nc


def _consts():
    ident = np.eye(128, dtype=np.float32)
    jt = np.zeros((128, 128), np.float32)
    for blk in range(2):
        o = blk * 64
        jt[o:o + 32, o + 32:o + 64] = np.eye(32)
        jt[o + 32:o + 64, o:o + 32] = -np.eye(32)
    inv = 1.0 / (10000.0 ** (np.arange(0, 64, 2, dtype=np.float64) / 64.0))
    pos = np.arange(S, dtype=np.float64)
    ang = inv[:, None] * pos[None, :]
    cosr = np.tile(np.cos(ang), (4, 1))
    sinr = np.tile(np.sin(ang), (4, 1))
    mask = (np.arange(128)[:, None] <= np.arange(128)[None, :]).astype(np.float32)
    ind16 = np.zeros((16, D), np.float32)
    for h in range(16):
        jt_ = h // 2
        r0 = (h % 2) * 64
        ind16[h, jt_ * 128 + r0: jt_ * 128 + r0 + 64] = 1.0
    import ml_dtypes
    bf = lambda a: np.ascontiguousarray(a).astype(ml_dtypes.bfloat16)
    return dict(identd=bf(ident), jrotd=bf(jt), cosd=bf(cosr), sind=bf(sinr),
                maskd=bf(mask), ind16d=bf(ind16))


def _cols(vec, k):
    return np.ascontiguousarray(vec.reshape(k, 128).T)


def kernel(**inputs):
    f32 = lambda a: np.ascontiguousarray(a, dtype=np.float32)
    i32 = lambda a: np.ascontiguousarray(a, dtype=np.int32)
    x = f32(inputs["L0_attn_x"]).reshape(B_ALL, S, D)
    base = {
        "delta": f32(inputs["L0_attn_delta"].reshape(8, 128, 2)
                     .transpose(1, 0, 2).reshape(128, 16)),
        "alpha": f32(inputs["L0_attn_alpha"].reshape(8, 128, 2)
                     .transpose(1, 0, 2).reshape(128, 16)),
        "betaT": f32(inputs["L0_attn_beta"].reshape(8, 128, 2)
                     .transpose(1, 0, 2).reshape(128, 16)),
        "gammaT": f32(inputs["L0_attn_gamma"].reshape(8, 128, 2)
                      .transpose(1, 0, 2).reshape(128, 16)),
        "omegaT": _cols(f32(inputs["L0_attn_omega"]), 8),
        "w0": f32(inputs["L0_ffn_weight"]),
        "wqd": f32(inputs["L1_attn_Wq"]),
        "wkd": f32(inputs["L1_attn_Wkv"][:, :D]),
        "wvd": f32(inputs["L1_attn_Wkv"][:, D:]),
        "bqc": _cols(f32(inputs["L1_attn_bq"]), 8),
        "bkc": _cols(f32(inputs["L1_attn_bkv"][:D]), 8),
        "bvr": f32(inputs["L1_attn_bkv"][D:]).reshape(1, D),
        "wod": f32(inputs["L1_attn_Wo"]),
        "boc": _cols(f32(inputs["L1_attn_bo"]), 8),
        "lngc": _cols(f32(inputs["L1_ffn_ln_gamma"]), 8),
        "lnbc": _cols(f32(inputs["L1_ffn_ln_beta"]), 8),
        "fc1q": i32(inputs["L1_ffn_fc1_q"]),
        "fc1s": f32(inputs["L1_ffn_fc1_scales"]),
        "fc1bc": _cols(f32(inputs["L1_ffn_fc1_bias"]), 32),
        "fc2q": i32(inputs["L1_ffn_fc2_q"]),
        "fc2s": f32(inputs["L1_ffn_fc2_scales"]),
        "fc2bc": _cols(f32(inputs["L1_ffn_fc2_bias"]), 8),
    }
    base.update(_consts())

    nc = bass.Bass("TRN2", target_bir_lowering=False)
    build(nc)

    in_maps = []
    for c in range(NCORES):
        m = dict(base)
        m["x"] = np.ascontiguousarray(
            x[c * B_PER:(c + 1) * B_PER].reshape(T, D))
        in_maps.append(m)
    res = run_bass_kernel_spmd(nc, in_maps, core_ids=list(range(NCORES)))
    outs = [res.results[c]["out"].reshape(B_PER, S, D) for c in range(NCORES)]
    return np.concatenate(outs, axis=0).astype(np.float32)


# version 14
# speedup vs baseline: 1.0521x; 1.0521x over previous
import math
import contextlib
import numpy as np
import concourse.bass as bass
import concourse.mybir as mybir
from concourse.bass_utils import run_bass_kernel_spmd

F32 = mybir.dt.float32
BF16 = mybir.dt.bfloat16
I32 = mybir.dt.int32
AF = mybir.ActivationFunctionType
ALU = mybir.AluOpType
AXX = mybir.AxisListType.X

B_ALL, S, D = 16, 2048, 1024
NCORES = 8
B_PER = B_ALL // NCORES
T = B_PER * S
NT = S // 128
DT = D // 128
H, DH = 16, 64
F_FFN = 4096
FT = F_FFN // 128

STAGE = "FULL"

_P1RANK = {"P1L": 0, "P1M1": 1, "P1M2": 2, "P1M3": 3, "P1M": 4}


def _p1rank():
    return _P1RANK.get(STAGE, 99)


_OUT_KW = ("accum_out",)


def _ap_rng(a):
    try:
        name = a.tensor.name
        lo = a.offset
        ext = 1
        for st, ct in a.ap:
            if st > 0:
                ext += st * (ct - 1)
        return (name, lo, lo + ext)
    except Exception:
        return None


class DW:

    _OPS = {
        "tensor_tensor", "tensor_add", "tensor_sub", "tensor_mul", "tensor_max",
        "tensor_scalar", "tensor_single_scalar", "tensor_scalar_add",
        "tensor_scalar_sub", "tensor_scalar_mul", "tensor_scalar_max",
        "tensor_scalar_min", "tensor_relu", "tensor_reduce", "reduce_max",
        "reduce_sum", "scalar_tensor_tensor", "tensor_tensor_scan",
        "tensor_copy", "memset", "memzero", "reciprocal", "activation",
        "copy", "add", "mul", "square", "sqrt", "sign", "select",
    }
    _NO_IN = {"memset", "memzero"}

    def __init__(self, eng):
        self._e = eng
        self._w = []
        self._r = []

    def _overlap(self, rng, lst):
        if rng is None:
            return True
        for (n, lo, hi) in lst:
            if n == rng[0] and rng[1] < hi and lo < rng[2]:
                return True
        return False

    def __getattr__(self, n):
        f = getattr(self._e, n)
        if n not in DW._OPS:
            return f

        def g(*a, **k):
            outs, ins = [], []
            if a:
                outs.append(_ap_rng(a[0]))
            for x in a[1:]:
                r = _ap_rng(x)
                if r is not None:
                    ins.append(r)
            for kw, v in k.items():
                r = _ap_rng(v)
                if r is None:
                    continue
                (outs if kw in _OUT_KW else ins).append(r)
            need = False
            for r in ins:
                if self._overlap(r, self._w):
                    need = True
                    break
            if not need:
                for r in outs:
                    if self._overlap(r, self._w) or self._overlap(r, self._r):
                        need = True
                        break
            if need:
                self._e.drain()
                self._w.clear()
                self._r.clear()
            res = f(*a, **k)
            for r in outs:
                self._w.append(r if r is not None else ("?", 0, 1 << 60))
            if n not in DW._NO_IN:
                for r in ins:
                    self._r.append(r)
            return res
        return g
def attn_sched():
    sched = []
    for kt in range(NT):
        q = kt * 128
        first = True
        while q < S:
            ln = min(1024, S - q)
            sched.append((kt, q, ln, first))
            first = False
            q += ln
    return sched


def aligned_pieces(q, ln):
    res = []
    e = q + ln
    while q < e:
        nxt = min(e, (q // 512 + 1) * 512)
        res.append((q, nxt - q))
        q = nxt
    return res


def build(nc):
    _uid = [0]

    def _nm(tag):
        _uid[0] += 1
        return f'{tag}_{_uid[0]}'

    xin = nc.dram_tensor("x", [T, D], F32, kind="ExternalInput")
    delta = nc.dram_tensor("delta", [128, 16], F32, kind="ExternalInput")
    alpha = nc.dram_tensor("alpha", [128, 16], F32, kind="ExternalInput")
    betaT = nc.dram_tensor("betaT", [128, 16], F32, kind="ExternalInput")
    gammaT = nc.dram_tensor("gammaT", [128, 16], F32, kind="ExternalInput")
    omegaT = nc.dram_tensor("omegaT", [128, 8], F32, kind="ExternalInput")
    w0 = nc.dram_tensor("w0", [D, D], F32, kind="ExternalInput")
    wqd = nc.dram_tensor("wqd", [D, D], F32, kind="ExternalInput")
    wkd = nc.dram_tensor("wkd", [D, D], F32, kind="ExternalInput")
    wvd = nc.dram_tensor("wvd", [D, D], F32, kind="ExternalInput")
    bqc = nc.dram_tensor("bqc", [128, 8], F32, kind="ExternalInput")
    bkc = nc.dram_tensor("bkc", [128, 8], F32, kind="ExternalInput")
    bvr = nc.dram_tensor("bvr", [1, D], F32, kind="ExternalInput")
    wod = nc.dram_tensor("wod", [D, D], F32, kind="ExternalInput")
    boc = nc.dram_tensor("boc", [128, 8], F32, kind="ExternalInput")
    lngc = nc.dram_tensor("lngc", [128, 8], F32, kind="ExternalInput")
    lnbc = nc.dram_tensor("lnbc", [128, 8], F32, kind="ExternalInput")
    fc1q = nc.dram_tensor("fc1q", [D, F_FFN], I32, kind="ExternalInput")
    fc1s = nc.dram_tensor("fc1s", [8, F_FFN], F32, kind="ExternalInput")
    fc1bc = nc.dram_tensor("fc1bc", [128, 32], F32, kind="ExternalInput")
    fc2q = nc.dram_tensor("fc2q", [F_FFN, D], I32, kind="ExternalInput")
    fc2s = nc.dram_tensor("fc2s", [32, D], F32, kind="ExternalInput")
    fc2bc = nc.dram_tensor("fc2bc", [128, 8], F32, kind="ExternalInput")
    identd = nc.dram_tensor("identd", [128, 128], BF16, kind="ExternalInput")
    jrotd = nc.dram_tensor("jrotd", [128, 128], BF16, kind="ExternalInput")
    cosd = nc.dram_tensor("cosd", [128, S], BF16, kind="ExternalInput")
    sind = nc.dram_tensor("sind", [128, S], BF16, kind="ExternalInput")
    maskd = nc.dram_tensor("maskd", [128, 128], BF16, kind="ExternalInput")
    ind16d = nc.dram_tensor("ind16d", [16, D], BF16, kind="ExternalInput")
    out = nc.dram_tensor("out", [T, D], F32, kind="ExternalOutput")
    x1sc = nc.dram_tensor("x1sc", [T, D], F32, kind="Internal")
    xn1sc = nc.dram_tensor("xn1sc", [B_PER, D, S], BF16, kind="Internal")
    gtsc = nc.dram_tensor("gtsc", [B_PER, D, S], BF16, kind="Internal")

    est = contextlib.ExitStack()
    with est:
        _ctr = [0]
        def SB(shape, dt_):
            _ctr[0] += 1
            return est.enter_context(nc.sbuf_tensor(f'sb{_ctr[0]}', shape, dt_))

        id_sb = SB([128, 128], BF16)
        j_sb = SB([128, 128], BF16)
        mask_sb = SB([128, 128], BF16)
        ind_sb = SB([16, D], BF16)
        ones1 = SB([1, 128], BF16)
        ones1f = SB([1, 128], F32)
        pcol = SB([128, 48], F32)
        QC, CC, OMC, WSC, RWSC = 0, 16, 32, 40, 41
        bcol = SB([128, 64], F32)
        BQ0, BK0, BO0, LNG0, LNB0, HB20 = 0, 8, 16, 24, 32, 40
        f1bcol = SB([128, 32], F32)
        qcolb = SB([128, 48], BF16)
        m2cb = SB([128, 32], BF16)
        r2cb = SB([128, 32], BF16)
        smc = SB([128, 64], F32)
        wsum = SB([1, 2], F32)
        halfc = SB([128, 1], F32)
        onescol = SB([128, 1], F32)
        halfc2 = SB([128, 1], F32)

        sems, semv = {}, {}

        def mksem(names):
            for n in names:
                sems[n] = est.enter_context(nc.semaphore(n))
                semv[n] = 0
        mksem(["dp", "dw", "dx", "sq", "vr", "ss", "vn", "xs", "tt", "tc",
               "az", "asc", "aq", "cb", "sl", "bm", "bmm", "be", "gs", "da",
               "vb", "bb", "bc", "sg", "et", "d2", "ex", "d1", "q2", "v2",
               "s2", "n2", "a2", "t2", "cs", "d3", "dr", "dq", "ds", "vq",
               "hm", "hs", "f1", "fe", "f2", "ge", "dg", "qm", "qb", "jm",
               "rp", "vm", "ve", "stm", "xe", "msk", "pvm", "oc", "rc",
               "nbm", "nbc", "wm", "we", "st", "dxa", "dga", "vv", "gm",
               "dx0", "dx1", "dx2", "dx3", "d20", "d21", "d10", "d11", "d30", "d31",
               "st0", "st1", "ds0", "ds1", "dq0", "dq1", "dbv", "ssd0", "ssd1",
               "stmA", "stmB", "xeA", "xeB", "mskA", "mskB", "pvmA", "pvmB",
               "ocA", "ocB", "ssdA", "ssdB"])

        def inc(ins, name, amt=1):
            semv[name] += amt
            return ins.then_inc(sems[name], amt)

        def snap():
            return dict(semv)

        SKIP_P0 = (STAGE == "P1B" or STAGE.startswith("MIN") or _p1rank() < 99)
        with nc.Block() as blk:
            @blk.sync
            def _(e):
                if SKIP_P0:
                    return
                for dst, src in [
                    (id_sb[:], identd.ap()), (j_sb[:], jrotd.ap()),
                    (mask_sb[:], maskd.ap()), (ind_sb[:], ind16d.ap()),
                    (pcol[:, 0:16], delta.ap()), (pcol[:, 16:32], alpha.ap()),
                    (smc[:, 0:16], betaT.ap()), (smc[:, 16:32], gammaT.ap()),
                    (pcol[:, OMC:OMC + 8], omegaT.ap()),
                    (bcol[:, BQ0:BQ0 + 8], bqc.ap()), (bcol[:, BK0:BK0 + 8], bkc.ap()),
                    (bcol[:, BO0:BO0 + 8], boc.ap()), (bcol[:, LNG0:LNG0 + 8], lngc.ap()),
                    (bcol[:, LNB0:LNB0 + 8], lnbc.ap()),
                    (bcol[:, HB20:HB20 + 8], fc2bc.ap()), (f1bcol[:], fc1bc.ap()),
                ]:
                    inc(e.dma_start(dst, src), "dp", 16)
                e.wait_ge(sems["dp"], semv["dp"])

            @blk.vector
            def _(v):
                v = DW(v)
                v.memset(ones1[:], 1.0)
                v.memset(ones1f[:], 1.0)
                v.memset(onescol[:], 1.0)
                v.memset(halfc[:], 0.5)
                if SKIP_P0:
                    return

        with nc.Block() as blk:
            @blk.scalar
            def _(s):
                s = DW(s)
                if SKIP_P0:
                    return
                s.activation(pcol[:, 0:16], pcol[:, 0:16], AF.Sigmoid)
                s.activation(pcol[:, 16:32], pcol[:, 16:32], AF.Sigmoid)

        with nc.Block() as blk:
            @blk.vector
            def _(v):
                v = DW(v)
                if SKIP_P0:
                    return
                v.tensor_mul(smc[:, 32:48], pcol[:, 0:16], pcol[:, 16:32])
                v.tensor_mul(pcol[:, CC:CC + 16], pcol[:, 0:16], smc[:, 0:16])
                v.tensor_mul(pcol[:, CC:CC + 16], pcol[:, CC:CC + 16], smc[:, 16:32])
                v.tensor_scalar_mul(pcol[:, CC:CC + 16], pcol[:, CC:CC + 16],
                                    math.sqrt(0.5))
                v.tensor_scalar(pcol[:, QC:QC + 16], smc[:, 32:48], -1.0, 1.0,
                                ALU.mult, ALU.add)
                v.tensor_scalar_mul(bcol[:, HB20:HB20 + 8], bcol[:, HB20:HB20 + 8], 0.5)

        if STAGE == "XCOPY":
            with nc.sbuf_tensor("xc", [128, 1024], F32) as xc:
                with nc.Block() as blk:
                    @blk.sync
                    def _(e):
                        for tau in range(32):
                            if tau >= 2:
                                e.wait_ge(sems["st"], semv["st"] - 16)
                            inc(e.dma_start(xc[:], xin.ap()[tau * 128:(tau + 1) * 128, :]),
                                "dx", 16)
                            e.wait_ge(sems["dx"], semv["dx"])
                            inc(e.dma_start(out.ap()[tau * 128:(tau + 1) * 128, :], xc[:]),
                                "st", 16)
                        e.wait_ge(sems["st"], semv["st"])
            return nc

        if STAGE == "P0X":
            with nc.Block() as blk:
                @blk.sync
                def _(e):
                    inc(e.dma_start(out.ap()[0:128, 0:48], pcol[:]), "st", 16)
                    e.wait_ge(sems["st"], semv["st"])
            return nc

        if STAGE == "MINPE":
            with (
                nc.sbuf_tensor("mpa", [128, 8], F32) as mpa,
                nc.sbuf_tensor("mpres", [128, 8], F32) as mpres,
                nc.psum_tensor("mpps", [128, 128], F32) as mpps,
            ):
                with nc.Block() as blk:
                    @blk.sync
                    def _(e):
                        inc(e.dma_start(mpa[:], xin.ap()[0:128, 0:8]), "dx", 16)

                    @blk.vector
                    def _(v):
                        v.wait_ge(sems["dx"], semv["dx"])

                with nc.Block() as blk:
                    @blk.tensor
                    def _(t):
                        inc(t.matmul(mpps[0:1, 0:1], mpa[:, 0:1], onescol[:],
                                     start=True, stop=True), "bm")

                    @blk.vector
                    def _(v):
                        v.wait_ge(sems["bm"], semv["bm"])
                        v.tensor_copy(mpres[0:1, 0:1], mpps[0:1, 0:1])
                        v.drain()

                with nc.Block() as blk:
                    @blk.sync
                    def _(e):
                        inc(e.dma_start(out.ap()[0:128, 0:8], mpres[:]), "st", 16)
                        e.wait_ge(sems["st"], semv["st"])
            return nc

        if STAGE.startswith("MINP2"):
            with (
                nc.sbuf_tensor("m2w", [128, 8 * 1024], F32) as m2w,
                nc.sbuf_tensor("m2res", [128, 16], F32) as m2res,
                nc.psum_tensor("m2ps", [128, 128], F32) as m2ps,
                nc.psum_tensor("m2pb", [128, 1024], BF16) as m2pb,
            ):
                with nc.Block() as blk:
                    @blk.sync
                    def _(e):
                        for j in range(8):
                            inc(e.dma_start(m2w[:, j * 1024:(j + 1) * 1024],
                                            w0.ap()[j * 128:(j + 1) * 128, :]),
                                "dw", 16)

                    @blk.vector
                    def _(v):
                        v = DW(v)
                        v.wait_ge(sems["dw"], semv["dw"])
                        if STAGE != "MINP2N":
                            for j in range(8):
                                v.tensor_reduce(m2res[:, j:j + 1],
                                                m2w[:, j * 1024:(j + 1) * 1024],
                                                AXX, ALU.add,
                                                apply_absolute_value=True)
                            v.reduce_sum(m2res[:, 8:9], m2res[:, 0:8], AXX)

                with nc.Block() as blk:
                    @blk.tensor
                    def _(t):
                        inc(t.matmul(m2ps[0:1, 0:1], m2res[:, 8:9], onescol[:],
                                     start=True, stop=True), "bm")

                    @blk.vector
                    def _(v):
                        v.wait_ge(sems["bm"], semv["bm"])
                        v.tensor_copy(m2res[0:1, 9:10], m2ps[0:1, 0:1])
                        v.drain()

                with nc.Block() as blk:
                    @blk.sync
                    def _(e):
                        inc(e.dma_start(out.ap()[0:128, 0:16], m2res[:]), "st", 16)
                        e.wait_ge(sems["st"], semv["st"])

                if STAGE == "MINP2D":
                    with nc.Block() as blk:
                        @blk.gpsimd
                        def _(g):
                            inc(g.dma_start(out.ap()[128:256, 0:48], pcol[:]),
                                "st", 16)
                            inc(g.dma_start(out.ap()[256:257, 0:2], wsum[:]),
                                "st", 16)
                            inc(g.dma_start(out.ap()[384:512, 0:1024],
                                            m2w[:, 0:1024]), "st", 16)
                            inc(g.dma_start(out.ap()[768:896, 0:64], smc[:]),
                                "st", 16)
                            g.wait_ge(sems["st"], semv["st"])
            return nc

        with nc.sbuf_tensor([128, 8 * 1024], BF16) as wqT:
            with (
                nc.sbuf_tensor([128, 8 * 1024], F32) as wfull,
                nc.sbuf_tensor([128, 1024], F32) as tmpa,
                nc.sbuf_tensor([128, 1024], F32) as tmpb,
                nc.sbuf_tensor([128, 1024], I32) as tmpi,
                nc.sbuf_tensor([128, 8 * 1024], BF16) as wqo,
                nc.psum_tensor([128, 512], F32) as ps1,
                nc.psum_tensor([128, 2048], BF16) as pstb,
            ):
                with nc.Block() as blk:
                    BB = snap()

                    @blk.sync
                    def _(e):
                        for j in range(8):
                            inc(e.dma_start(wfull[:, j * 1024:(j + 1) * 1024],
                                            w0.ap()[j * 128:(j + 1) * 128, :]), "dw", 16)

                    @blk.vector
                    def _(v):
                        v = DW(v)
                        v.wait_ge(sems["dw"], BB["dw"] + 16 * 8)
                        for j in range(8):
                            v.tensor_reduce(smc[:, 32 + j:33 + j],
                                            wfull[:, j * 1024:(j + 1) * 1024],
                                            AXX, ALU.add, apply_absolute_value=True)
                        v.reduce_sum(smc[:, 40:41], smc[:, 32:40], AXX)

                if STAGE in ("P1L", "P1M1"):
                    with nc.Block() as blk:
                        @blk.gpsimd
                        def _(g):
                            g.wait_ge(sems["dw"], semv["dw"])
                            inc(g.dma_start(out.ap()[768:896, 0:64], smc[:]),
                                "st", 16)
                            inc(g.dma_start(out.ap()[896:1024, 0:1024],
                                            wfull[:, 0:1024]), "st", 16)
                            g.wait_ge(sems["st"], semv["st"])

                with nc.Block() as blk:
                    @blk.vector
                    def _(v):
                        v = DW(v)
                        if _p1rank() < 1:
                            return
                        v.tensor_copy(halfc2[:], smc[:, 40:41])

                with nc.Block() as blk:
                    @blk.tensor
                    def _(t):
                        if _p1rank() < 1:
                            return
                        import os as _os
                        mmv = _os.environ.get("MMVAR", "tiny")
                        if mmv == "tiny":
                            t.matmul(ps1[0:1, 0:1], halfc2[:], onescol[:],
                                     start=True, stop=True)
                        elif mmv == "full":
                            t.matmul(ps1[0:128, 0:128], wfull[:, 0:128],
                                     wfull[:, 128:256], start=True, stop=True)
                        elif mmv == "mid":
                            t.matmul(ps1[0:1, 0:128], halfc2[:],
                                     wfull[:, 0:128], start=True, stop=True)

                with nc.Block() as blk:
                    @blk.vector
                    def _(v):
                        v = DW(v)
                        if _p1rank() < 2:
                            return
                        v.tensor_copy(wsum[0:1, 0:1], ps1[0:1, 0:1])

                with nc.Block() as blk:
                    @blk.tensor
                    def _(t):
                        if _p1rank() < 3:
                            return
                        t.matmul(ps1[0:128, 0:1], ones1f[:], wsum[0:1, 0:1],
                                 start=True, stop=True)

                with nc.Block() as blk:
                    @blk.vector
                    def _(v):
                        v = DW(v)
                        if _p1rank() < 4:
                            return
                        v.tensor_copy(pcol[:, WSC:WSC + 1], ps1[0:128, 0:1])

                with nc.Block() as blk:
                    @blk.vector
                    def _(v):
                        v = DW(v)
                        if _p1rank() < 4:
                            return
                        v.tensor_scalar(pcol[:, WSC:WSC + 1], pcol[:, WSC:WSC + 1],
                                        1.0 / float(D * D), 1e-8, ALU.mult, ALU.add)
                        v.reciprocal(pcol[:, RWSC:RWSC + 1], pcol[:, WSC:WSC + 1])
                        if STAGE == "P1A" or _p1rank() < 99:
                            return
                        for j in range(8):
                            wj = wfull[:, j * 1024:(j + 1) * 1024]
                            v.tensor_scalar(tmpa[:], wj, pcol[:, RWSC:RWSC + 1], 0.5,
                                            ALU.mult, ALU.add)
                            v.tensor_copy(tmpi[:], tmpa[:])
                            v.tensor_tensor(tmpb[:], tmpi[:], tmpa[:], ALU.is_gt)
                            v.tensor_sub(tmpb[:], tmpi[:], tmpb[:])
                            v.tensor_scalar(wqo[:, j * 1024:(j + 1) * 1024], tmpb[:],
                                            -1.0, 1.0, ALU.max, ALU.min)

                with nc.Block() as blk:
                    BB = snap()

                    @blk.tensor
                    def _(t):
                        if STAGE == "P1B" or _p1rank() < 99:
                            return
                        for a in range(8):
                            for g4 in range(2):
                                gidx = a * 2 + g4
                                h_ = (gidx % 2) * 1024
                                if gidx >= 2:
                                    t.wait_ge(sems["tc"], BB["tc"] + gidx - 1)
                                last = None
                                for k in range(4):
                                    b_ = g4 * 4 + k
                                    last = t.transpose(
                                        pstb[:, h_ + k * 128:h_ + (k + 1) * 128],
                                        wqo[:, a * 1024 + b_ * 128:
                                            a * 1024 + (b_ + 1) * 128],
                                        id_sb[:])
                                inc(last, "tt")

                    @blk.vector
                    def _(v):
                        v = DW(v)
                        if STAGE == "P1B" or _p1rank() < 99:
                            return
                        for a in range(8):
                            for g4 in range(2):
                                gidx = a * 2 + g4
                                h_ = (gidx % 2) * 1024
                                v.wait_ge(sems["tt"], BB["tt"] + gidx + 1)
                                dst = wqT[:].rearrange("p (i c) -> p i c", i=8)[
                                    :, g4 * 4:g4 * 4 + 4, a * 128:(a + 1) * 128]
                                src = pstb[:, h_:h_ + 512].rearrange(
                                    "p (k c) -> p k c", k=4)
                                inc(v.tensor_copy(dst, src), "tc")

            if STAGE == "P1A":
                with nc.Block() as blk:
                    @blk.sync
                    def _(e):
                        inc(e.dma_start(out.ap()[0:128, 0:48], pcol[:]), "st", 16)
                        e.wait_ge(sems["st"], semv["st"])
                return nc

            if STAGE == "P1X":
                with nc.sbuf_tensor("dmp", [128, 1024], F32) as dmp:
                    with nc.Block() as blk:
                        @blk.vector
                        def _(v):
                            v = DW(v)
                            inc(v.tensor_copy(dmp[:], wqT[:, 0:1024]), "vb")

                    with nc.Block() as blk:
                        @blk.sync
                        def _(e):
                            inc(e.dma_start(out.ap()[0:128, :], dmp[:]), "st", 16)
                            e.wait_ge(sems["st"], semv["st"])
                return nc

                if STAGE == "P1B" or _p1rank() < 99:
                    with nc.Block() as blk:
                        @blk.gpsimd
                        def _(g):
                            g.wait_ge(sems["dw"], semv["dw"])
                            inc(g.dma_start(out.ap()[0:128, 0:1024], wqo[:, 0:1024]),
                                "st", 16)
                            inc(g.dma_start(out.ap()[128:256, 0:48], pcol[:]),
                                "st", 16)
                            inc(g.dma_start(out.ap()[256:257, 0:2], wsum[:]),
                                "st", 16)
                            inc(g.dma_start(out.ap()[384:512, 0:1024],
                                            wfull[:, 0:1024]), "st", 16)
                            inc(g.dma_start(out.ap()[512:640, 0:1024],
                                            wfull[:, 1024:2048]), "st", 16)
                            inc(g.dma_start(out.ap()[768:896, 0:64], smc[:]),
                                "st", 16)
                            inc(g.dma_start(out.ap()[640:768, 0:1024],
                                            wfull[:, 7168:8192]), "st", 16)
                            g.wait_ge(sems["st"], semv["st"])

            if STAGE == "P1B" or _p1rank() < 99:
                return nc

            with contextlib.ExitStack() as p2s:
                A2 = lambda sh, dt_: p2s.enter_context(nc.sbuf_tensor(_nm('A2'), sh, dt_))
                P2 = lambda sh, dt_: p2s.enter_context(nc.psum_tensor(_nm('P2'), sh, dt_))
                xt = A2([128, 4 * 1024], F32)
                xt2 = xt
                x1t = A2([128, 2 * 1024], F32)
                xnt = A2([128, 2 * 1024], BF16)
                xn1t = A2([128, 2 * 1024], BF16)
                stg = xnt
                xnT = A2([128, 8 * 2048], BF16)
                aqT = A2([128, 8 * 2048], BF16)
                gT = A2([128, 8 * 2048], BF16)
                tf = A2([128, 2048], F32)
                tf2 = A2([128, 2048], F32)
                h0 = tf
                h1 = tf2
                ti = x1t[:].bitcast(I32)
                gtb = xn1t
                rsb = A2([128, 2048], BF16)
                ysb = A2([128, 2048], F32)
                sc4 = A2([128, 32], F32)
                sqd = ysb[:, 0:1024]
                pstr1 = P2([128, 2048], BF16)
                pstr2 = P2([128, 2048], BF16)
                psb0 = P2([128, 512], F32)
                psb1 = P2([128, 512], F32)
                psbc = P2([128, 512], F32)
                psbit = [psb0, psb1]
                for b in range(B_PER):
                    with nc.Block() as blk:
                        BB = snap()

                        @blk.sync
                        def _(e):
                            for tau in range(NT):
                                p4 = tau % 4
                                if tau >= 4:
                                    e.wait_ge(sems["xs"], BB["xs"] + tau - 3)
                                inc(e.dma_start(
                                    xt[:, p4 * 1024:(p4 + 1) * 1024],
                                    xin.ap()[b * S + tau * 128:b * S + (tau + 1) * 128, :]),
                                    "dx" + str(p4), 16)

                        @blk.scalar
                        def _(s):
                            s = DW(s)
                            s.wait_ge(sems["d30"], BB["d30"])
                            s.wait_ge(sems["d31"], BB["d31"])
                            for tau in range(NT):
                                p = tau % 2
                                p4 = tau % 4
                                s.wait_ge(sems["dx" + str(p4)],
                                          BB["dx" + str(p4)] + 16 * (tau // 4 + 1))
                                inc(s.activation(sqd, xt[:, p4 * 1024:(p4 + 1) * 1024],
                                                 AF.Square,
                                                 accum_out=sc4[:, 4 + p4:5 + p4]), "sq")
                                s.wait_ge(sems["vr"], BB["vr"] + tau + 1)
                                inc(s.activation(sc4[:, 16 + p4:17 + p4],
                                                 sc4[:, 16 + p4:17 + p4],
                                                 AF.Sqrt), "ss")
                                s.wait_ge(sems["vn"], BB["vn"] + tau + 1)
                                if tau >= 2:
                                    s.wait_ge(sems["tc"], BB["tc"] + tau - 1)
                                inc(s.activation(xnt[:, p * 1024:(p + 1) * 1024],
                                                 xt[:, p4 * 1024:(p4 + 1) * 1024],
                                                 AF.Identity,
                                                 bias=sc4[:, 20 + p4:21 + p4],
                                                 scale=sc4[:, 16 + p4:17 + p4]), "xs")

                        @blk.vector
                        def _(v):
                            v = DW(v)
                            for tau in range(NT):
                                p = tau % 2
                                p4 = tau % 4
                                v.wait_ge(sems["dx" + str(p4)],
                                          BB["dx" + str(p4)] + 16 * (tau // 4 + 1))
                                v.reduce_sum(sc4[:, 0 + p4:1 + p4],
                                             xt[:, p4 * 1024:(p4 + 1) * 1024], AXX)
                                v.tensor_scalar_mul(sc4[:, 8 + p4:9 + p4],
                                                    sc4[:, 0 + p4:1 + p4], 1.0 / D)
                                v.wait_ge(sems["sq"], BB["sq"] + tau + 1)
                                v.tensor_mul(sc4[:, 24 + p4:25 + p4],
                                             sc4[:, 8 + p4:9 + p4],
                                             sc4[:, 8 + p4:9 + p4])
                                v.tensor_scalar(sc4[:, 12 + p4:13 + p4],
                                                sc4[:, 4 + p4:5 + p4],
                                                1.0 / D, None, ALU.mult)
                                v.tensor_sub(sc4[:, 12 + p4:13 + p4],
                                             sc4[:, 12 + p4:13 + p4],
                                             sc4[:, 24 + p4:25 + p4])
                                v.tensor_scalar_add(sc4[:, 12 + p4:13 + p4],
                                                    sc4[:, 12 + p4:13 + p4], 1e-5)
                                inc(v.reciprocal(sc4[:, 16 + p4:17 + p4],
                                                 sc4[:, 12 + p4:13 + p4]), "vr")
                                v.wait_ge(sems["ss"], BB["ss"] + tau + 1)
                                inc(v.scalar_tensor_tensor(
                                    sc4[:, 20 + p4:21 + p4], sc4[:, 8 + p4:9 + p4],
                                    -1.0, sc4[:, 16 + p4:17 + p4],
                                    ALU.mult, ALU.mult), "vn")
                                v.wait_ge(sems["xs"], BB["xs"] + tau + 1)
                                v.tensor_reduce(smc[:, 16 + tau:17 + tau],
                                                xnt[:, p * 1024:(p + 1) * 1024],
                                                AXX, ALU.max, apply_absolute_value=True)
                                v.wait_ge(sems["tt"], BB["tt"] + tau + 1)
                                dst = xnT[:].rearrange("p (j s) -> p j s", j=8)[
                                    :, :, tau * 128:(tau + 1) * 128]
                                src = pstr1[:, p * 1024:(p + 1) * 1024].rearrange(
                                    "p (j c) -> p j c", j=8)
                                inc(v.tensor_copy(dst, src), "tc")

                        @blk.tensor
                        def _(t):
                            for tau in range(NT):
                                p = tau % 2
                                t.wait_ge(sems["xs"], BB["xs"] + tau + 1)
                                if tau >= 2:
                                    t.wait_ge(sems["tc"], BB["tc"] + tau - 1)
                                last = None
                                for jj in range(8):
                                    last = t.transpose(
                                        pstr1[:, p * 1024 + jj * 128:
                                              p * 1024 + (jj + 1) * 128],
                                        xnt[:, p * 1024 + jj * 128:
                                            p * 1024 + (jj + 1) * 128],
                                        id_sb[:])
                                inc(last, "tt")

                    if STAGE == "L0A" and b == 0:
                        break

                    with nc.Block() as blk:
                        BB = snap()

                        @blk.vector
                        def _(v):
                            v = DW(v)
                            v.tensor_scalar_add(smc[:, 16:32], smc[:, 16:32], 1e-8)
                            v.tensor_scalar(qcolb[:, 16:32], smc[:, 16:32],
                                            pcol[:, WSC:WSC + 1], 1.0 / 127.0,
                                            ALU.mult, ALU.mult)
                            v.reciprocal(smc[:, 16:32], smc[:, 16:32])
                            v.tensor_scalar_mul(smc[:, 16:32], smc[:, 16:32], 127.0)
                            inc(v.tensor_copy(qcolb[:, 0:16], smc[:, 16:32]), "vb")
                            for tq in range(16):
                                v.wait_ge(sems["bm"], BB["bm"] + 2 * tq + 1)
                                inc(v.tensor_copy(rsb[:, tq * 128:(tq + 1) * 128],
                                                  psbc[:, 0:128]), "bc")
                                v.wait_ge(sems["bm"], BB["bm"] + 2 * tq + 2)
                                inc(v.tensor_copy(ysb[:, tq * 128:(tq + 1) * 128],
                                                  psbc[:, 128:256]), "bc")
                            for jj in range(8):
                                xj = xnT[:, jj * 2048:(jj + 1) * 2048]
                                inc(v.tensor_mul(tf[:], xj, rsb[:]), "az")
                                v.wait_ge(sems["asc"], BB["asc"] + 2 * (jj + 1))
                                v.tensor_tensor(gtb[:], ti, tf2[:], ALU.is_gt)
                                inc(v.tensor_sub(aqT[:, jj * 2048:(jj + 1) * 2048],
                                                 ti, gtb[:]), "aq")
                            for jj in range(8):
                                xj = xnT[:, jj * 2048:(jj + 1) * 2048]
                                v.tensor_tensor_scan(
                                    h0[:], pcol[:, QC + 2 * jj:QC + 2 * jj + 1]
                                    .broadcast_to([128, 2048]), xj, 0.0,
                                    ALU.mult, ALU.add)
                                v.tensor_tensor_scan(
                                    h1[:], pcol[:, QC + 2 * jj + 1:QC + 2 * jj + 2]
                                    .broadcast_to([128, 2048]), xj, 0.0,
                                    ALU.mult, ALU.add)
                                v.tensor_scalar_mul(gtb[:], h0[:],
                                                    pcol[:, CC + 2 * jj:CC + 2 * jj + 1])
                                v.scalar_tensor_tensor(
                                    gtb[:], h1[:],
                                    pcol[:, CC + 2 * jj + 1:CC + 2 * jj + 2],
                                    gtb[:], ALU.mult, ALU.add)
                                inc(v.scalar_tensor_tensor(
                                    xj, xj, pcol[:, OMC + jj:OMC + jj + 1], gtb[:],
                                    ALU.mult, ALU.add), "cb")
                            for k in range(32):
                                tq, ot = k // 8, k % 8
                                v.wait_ge(sems["bmm"], BB["bmm"] + k + 1)
                                inc(v.tensor_mul(
                                    gT[:, ot * 2048 + tq * 512:ot * 2048 + (tq + 1) * 512],
                                    psbit[k % 2][:],
                                    ysb[:, tq * 512:(tq + 1) * 512]), "be")

                        @blk.scalar
                        def _(s):
                            s = DW(s)
                            s.wait_ge(sems["d10"], BB["d10"])
                            s.wait_ge(sems["d11"], BB["d11"])
                            s.wait_ge(sems["st0"], BB["st0"])
                            s.wait_ge(sems["st1"], BB["st1"])
                            for jj in range(8):
                                s.wait_ge(sems["az"], BB["az"] + jj + 1)
                                if jj >= 1:
                                    s.wait_ge(sems["aq"], BB["aq"] + jj)
                                s.activation(tf2[:], tf[:], AF.Identity,
                                             bias=halfc[:])
                                inc(s.copy(ti, tf2[:]), "asc", 2)
                            for jj in range(8):
                                s.wait_ge(sems["cb"], BB["cb"] + jj + 1)
                                seg = xnT[:, jj * 2048:(jj + 1) * 2048]
                                inc(s.activation(seg, seg, AF.Silu), "sl")
                            for k in range(32):
                                tq, ot = k // 8, k % 8
                                s.wait_ge(sems["be"], BB["be"] + k + 1)
                                seg = gT[:, ot * 2048 + tq * 512:
                                         ot * 2048 + (tq + 1) * 512]
                                inc(s.activation(seg, seg, AF.Silu), "gs")

                        @blk.tensor
                        def _(t):
                            t.wait_ge(sems["vb"], BB["vb"] + 1)
                            for tq in range(16):
                                if tq >= 1:
                                    t.wait_ge(sems["bc"], BB["bc"] + 2 * tq)
                                inc(t.matmul(psbc[:, 0:128],
                                             qcolb[:, tq:tq + 1].broadcast_to([128, 128]),
                                             id_sb[:], start=True, stop=True), "bm")
                                t.wait_ge(sems["bc"], BB["bc"] + 2 * tq + 1)
                                inc(t.matmul(psbc[:, 128:256],
                                             qcolb[:, 16 + tq:17 + tq]
                                             .broadcast_to([128, 128]),
                                             id_sb[:], start=True, stop=True), "bm")
                            t.wait_ge(sems["aq"], BB["aq"] + 8)
                            for k in range(32):
                                tq, ot = k // 8, k % 8
                                if k >= 2:
                                    t.wait_ge(sems["be"], BB["be"] + k - 1)
                                last = None
                                for it in range(8):
                                    last = t.matmul(
                                        psbit[k % 2][:],
                                        wqT[:, it * 1024 + ot * 128:
                                            it * 1024 + (ot + 1) * 128],
                                        aqT[:, it * 2048 + tq * 512:
                                            it * 2048 + (tq + 1) * 512],
                                        start=(it == 0), stop=(it == 7))
                                inc(last, "bmm")

                    if STAGE == "L0B" and b == 0:
                        break

                    with nc.Block() as blk:
                        BB = snap()

                        @blk.vector
                        def _(v):
                            v = DW(v)
                            for jj in range(8):
                                sl_ = xnT[:, jj * 2048:(jj + 1) * 2048]
                                inc(v.tensor_mul(sl_, sl_,
                                                 gT[:, jj * 2048:(jj + 1) * 2048]), "sg")
                            for tau in range(NT):
                                p = tau % 2
                                v.wait_ge(sems["et"], BB["et"] + tau + 1)
                                v.wait_ge(sems["d2" + str(p)],
                                          BB["d2" + str(p)] + 16 * (tau // 2 + 1))
                                if tau >= 2:
                                    v.wait_ge(sems["d1" + str(p)],
                                              BB["d1" + str(p)] + 16 * (tau // 2))
                                    if STAGE == "L0":
                                        v.wait_ge(sems["st" + str(p)],
                                                  BB["st" + str(p)] + 16 * (tau // 2))
                                inc(v.tensor_tensor(x1t[:, p * 1024:(p + 1) * 1024],
                                                    xt2[:, p * 1024:(p + 1) * 1024],
                                                    pstr1[:, p * 1024:(p + 1) * 1024],
                                                    ALU.add), "ex")
                                v.reduce_sum(smc[:, 0 + p:1 + p],
                                             x1t[:, p * 1024:(p + 1) * 1024], AXX)
                                v.tensor_scalar_mul(smc[:, 4 + p:5 + p],
                                                    smc[:, 0 + p:1 + p], 1.0 / D)
                                v.wait_ge(sems["q2"], BB["q2"] + tau + 1)
                                v.tensor_mul(smc[:, 12 + p:13 + p], smc[:, 4 + p:5 + p],
                                             smc[:, 4 + p:5 + p])
                                v.tensor_scalar(smc[:, 6 + p:7 + p], smc[:, 2 + p:3 + p],
                                                1.0 / D, None, ALU.mult)
                                v.tensor_sub(smc[:, 6 + p:7 + p], smc[:, 6 + p:7 + p],
                                             smc[:, 12 + p:13 + p])
                                v.tensor_scalar_add(smc[:, 6 + p:7 + p],
                                                    smc[:, 6 + p:7 + p], 1e-5)
                                inc(v.reciprocal(smc[:, 8 + p:9 + p],
                                                 smc[:, 6 + p:7 + p]), "v2")
                                v.wait_ge(sems["s2"], BB["s2"] + 2 * tau + 1)
                                inc(v.scalar_tensor_tensor(
                                    smc[:, 10 + p:11 + p], smc[:, 4 + p:5 + p], -1.0,
                                    smc[:, 8 + p:9 + p], ALU.mult, ALU.mult), "n2")
                                v.wait_ge(sems["a2"], BB["a2"] + 2 * (tau + 1))
                                v.reduce_sum(smc[:, 32 + tau:33 + tau],
                                             xn1t[:, p * 1024:(p + 1) * 1024], AXX)
                                v.tensor_scalar_mul(smc[:, 32 + tau:33 + tau],
                                                    smc[:, 32 + tau:33 + tau], 1.0 / D)
                                v.tensor_mul(smc[:, 12 + p:13 + p],
                                             smc[:, 32 + tau:33 + tau],
                                             smc[:, 32 + tau:33 + tau])
                                v.tensor_scalar(smc[:, 48 + tau:49 + tau],
                                                smc[:, 14 + p:15 + p], 1.0 / D,
                                                None, ALU.mult)
                                v.tensor_sub(smc[:, 48 + tau:49 + tau],
                                             smc[:, 48 + tau:49 + tau],
                                             smc[:, 12 + p:13 + p])
                                v.tensor_scalar_add(smc[:, 48 + tau:49 + tau],
                                                    smc[:, 48 + tau:49 + tau], 1e-5)
                                inc(v.reciprocal(smc[:, 48 + tau:49 + tau],
                                                 smc[:, 48 + tau:49 + tau]), "vv")
                                v.wait_ge(sems["t2"], BB["t2"] + tau + 1)
                                if tau >= 2:
                                    v.wait_ge(sems["d3" + str(p)],
                                              BB["d3" + str(p)] + 16 * (tau // 2))
                                inc(v.tensor_copy(stg[:, p * 1024:(p + 1) * 1024],
                                                  pstr2[:, p * 1024:(p + 1) * 1024]),
                                    "cs")

                        @blk.scalar
                        def _(s):
                            s = DW(s)
                            for tau in range(NT):
                                p = tau % 2
                                s.wait_ge(sems["ex"], BB["ex"] + tau + 1)
                                inc(s.activation(sqd, x1t[:, p * 1024:(p + 1) * 1024],
                                                 AF.Square,
                                                 accum_out=smc[:, 2 + p:3 + p]), "q2")
                                s.wait_ge(sems["v2"], BB["v2"] + tau + 1)
                                inc(s.activation(smc[:, 8 + p:9 + p],
                                                 smc[:, 8 + p:9 + p], AF.Sqrt), "s2")
                                s.wait_ge(sems["n2"], BB["n2"] + tau + 1)
                                if tau >= 2:
                                    s.wait_ge(sems["cs"], BB["cs"] + tau - 1)
                                s.activation(xn1t[:, p * 1024:(p + 1) * 1024],
                                             x1t[:, p * 1024:(p + 1) * 1024],
                                             AF.Identity, bias=smc[:, 10 + p:11 + p],
                                             scale=smc[:, 8 + p:9 + p])
                                inc(s.activation(sqd, xn1t[:, p * 1024:(p + 1) * 1024],
                                                 AF.Square,
                                                 accum_out=smc[:, 14 + p:15 + p]),
                                    "a2", 2)
                                s.wait_ge(sems["vv"], BB["vv"] + tau + 1)
                                inc(s.activation(smc[:, 48 + tau:49 + tau],
                                                 smc[:, 48 + tau:49 + tau], AF.Sqrt),
                                    "s2")

                        @blk.tensor
                        def _(t):
                            t.wait_ge(sems["sg"], BB["sg"] + 8)
                            for tau in range(NT):
                                p = tau % 2
                                if tau >= 2:
                                    t.wait_ge(sems["ex"], BB["ex"] + tau - 1)
                                last = None
                                for jj in range(8):
                                    last = t.transpose(
                                        pstr1[:, p * 1024 + jj * 128:
                                              p * 1024 + (jj + 1) * 128],
                                        xnT[:, jj * 2048 + tau * 128:
                                            jj * 2048 + (tau + 1) * 128],
                                        id_sb[:])
                                inc(last, "et")
                                t.wait_ge(sems["a2"], BB["a2"] + 2 * (tau + 1))
                                if tau >= 2:
                                    t.wait_ge(sems["cs"], BB["cs"] + tau - 1)
                                last = None
                                for jj in range(8):
                                    last = t.transpose(
                                        pstr2[:, p * 1024 + jj * 128:
                                              p * 1024 + (jj + 1) * 128],
                                        xn1t[:, p * 1024 + jj * 128:
                                            p * 1024 + (jj + 1) * 128],
                                        id_sb[:])
                                inc(last, "t2")

                        def _store(e, tau):
                            p = tau % 2
                            e.wait_ge(sems["ex"], BB["ex"] + tau + 1)
                            inc(e.dma_start(
                                x1sc.ap()[b * S + tau * 128:b * S + (tau + 1) * 128, :],
                                x1t[:, p * 1024:(p + 1) * 1024]), "d1" + str(p), 16)
                            if STAGE == "L0":
                                inc(e.dma_start(
                                    out.ap()[b * S + tau * 128:
                                             b * S + (tau + 1) * 128, :],
                                    x1t[:, p * 1024:(p + 1) * 1024]),
                                    "st" + str(p), 16)
                            e.wait_ge(sems["cs"], BB["cs"] + tau + 1)
                            dst = xn1sc.ap()[b, :, :].rearrange(
                                "(j p) s -> p j s", p=128)[:, :,
                                                           tau * 128:(tau + 1) * 128]
                            src = stg[:, p * 1024:(p + 1) * 1024].rearrange(
                                "p (j c) -> p j c", j=8)
                            inc(e.dma_start(dst, src), "d3" + str(p), 16)

                        @blk.sync
                        def _(e):
                            for tau in range(NT):
                                p = tau % 2
                                if tau >= 2:
                                    _store(e, tau - 2)
                                inc(e.dma_start(
                                    xt2[:, p * 1024:(p + 1) * 1024],
                                    xin.ap()[b * S + tau * 128:b * S + (tau + 1) * 128, :]),
                                    "d2" + str(p), 16)
                            _store(e, NT - 2)
                            _store(e, NT - 1)

                    with nc.Block() as blk:
                        @blk.vector
                        def _(v):
                            v = DW(v)
                            v.tensor_copy(r2cb[:, b * 16:(b + 1) * 16], smc[:, 48:64])
                            v.tensor_mul(smc[:, 32:48], smc[:, 32:48], smc[:, 48:64])
                            v.tensor_copy(m2cb[:, b * 16:(b + 1) * 16], smc[:, 32:48])

            if STAGE in ("L0A", "L0B"):
                with nc.Block() as blk:
                    @blk.gpsimd
                    def _(g):
                        dst = out.ap().rearrange("(x p) d -> p x d", p=128)[:, 0:16, :]
                        src = (xnT[:] if STAGE == "L0A" else gT[:]).rearrange(
                            "p (j s) -> p j s", j=16)
                        inc(g.dma_start(dst, src), "st", 16)
                        g.wait_ge(sems["st"], semv["st"])
                return nc

        if STAGE == "L0":
            with nc.Block() as blk:
                @blk.sync
                def _(e):
                    e.wait_ge(sems["st0"], semv["st0"])
                    e.wait_ge(sems["st1"], semv["st1"])
            return nc

        with (
            nc.sbuf_tensor([128, 8 * F_FFN], BF16) as w1sb,
            nc.sbuf_tensor([128, FT * 1024], BF16) as w2sb,
        ):
            with nc.sbuf_tensor([128, 2 * F_FFN], F32) as scb:
                with nc.Block() as blk:
                    BB = snap()

                    @blk.gpsimd
                    def _(g):
                        for nm in ("d10", "d11", "d30", "d31"):
                            g.wait_ge(sems[nm], BB[nm])
                        for gidx in range(8 + FT):
                            p = gidx % 2
                            if gidx >= 2:
                                g.wait_ge(sems["vq"], BB["vq"] + gidx - 1)
                            if gidx < 8:
                                scsrc = fc1s.ap()[gidx:gidx + 1, :]
                                wdst = w1sb[:, gidx * F_FFN:(gidx + 1) * F_FFN]
                                wsrc = fc1q.ap()[gidx * 128:(gidx + 1) * 128, :]
                                scw = F_FFN
                            else:
                                gg = gidx - 8
                                scsrc = fc2s.ap()[gg:gg + 1, :]
                                wdst = w2sb[:, gg * 1024:(gg + 1) * 1024]
                                wsrc = fc2q.ap()[gg * 128:(gg + 1) * 128, :]
                                scw = 1024
                            inc(g.dma_start(scb[:, p * F_FFN:p * F_FFN + scw],
                                            scsrc.partition_broadcast(128)),
                                "ds" + str(p), 16)
                            inc(g.dma_start(wdst, wsrc), "dq" + str(p), 16)

                    @blk.vector
                    def _(v):
                        v = DW(v)
                        for gidx in range(8 + FT):
                            p = gidx % 2
                            v.wait_ge(sems["ds" + str(p)],
                                      BB["ds" + str(p)] + 16 * (gidx // 2 + 1))
                            v.wait_ge(sems["dq" + str(p)],
                                      BB["dq" + str(p)] + 16 * (gidx // 2 + 1))
                            if gidx < 8:
                                seg = w1sb[:, gidx * F_FFN:(gidx + 1) * F_FFN]
                                scw = F_FFN
                            else:
                                gg = gidx - 8
                                seg = w2sb[:, gg * 1024:(gg + 1) * 1024]
                                scw = 1024
                            inc(v.scalar_tensor_tensor(
                                seg, seg, -128.0, scb[:, p * F_FFN:p * F_FFN + scw],
                                ALU.add, ALU.mult), "vq")

            with contextlib.ExitStack() as p3s:
                A3 = lambda sh, dt_: p3s.enter_context(nc.sbuf_tensor(_nm('A3'), sh, dt_))
                P3 = lambda sh, dt_: p3s.enter_context(nc.psum_tensor(_nm('P3'), sh, dt_))
                TCW = 512
                NTC = S // TCW
                NCHK = B_PER * NTC
                XW = 8 * TCW
                xsl = A3([128, 2 * XW], BF16)
                hsl = A3([128, XW], BF16)
                y1s = A3([128, FT * TCW], BF16)
                r2b = A3([128, TCW], F32)
                m2b = A3([128, TCW], F32)
                htmp = A3([128, 2 * TCW], F32)
                tgs = A3([128, 2 * TCW], F32)
                pf = [P3([128, 512], F32) for _ in range(4)]
                pg = [P3([128, 512], F32) for _ in range(2)]
                psbc2 = P3([128, 1024], F32)

                def _dxa(e, ch, BB3):
                    b, tc = ch // NTC, ch % NTC
                    if ch >= 2:
                        e.wait_ge(sems["vm"], BB3["vm"] + 8 * (ch - 1))
                        e.wait_ge(sems["dga"], BB3["dga"] + 16 * (ch - 1))
                    dst = xsl[:, (ch % 2) * XW:(ch % 2 + 1) * XW].rearrange(
                        "p (j c) -> p j c", j=8)
                    src = xn1sc.ap()[b, :, :].rearrange(
                        "(j p) s -> p j s", p=128)[:, :,
                                                   tc * TCW:(tc + 1) * TCW]
                    inc(e.dma_start(dst, src), "dxa", 16)

                def _bb(t, ch, BB3):
                    b, tc = ch // NTC, ch % NTC
                    if ch >= 1:
                        t.wait_ge(sems["bc"], BB3["bc"] + 2 * ch)
                    last = None
                    for u in range(4):
                        tau = tc * 4 + u
                        last = t.matmul(
                            psbc2[:, u * 128:(u + 1) * 128],
                            m2cb[:, b * 16 + tau:b * 16 + tau + 1]
                            .broadcast_to([128, 128]),
                            id_sb[:], start=True, stop=True)
                    for u in range(4):
                        tau = tc * 4 + u
                        last = t.matmul(
                            psbc2[:, 512 + u * 128:512 + (u + 1) * 128],
                            r2cb[:, b * 16 + tau:b * 16 + tau + 1]
                            .broadcast_to([128, 128]),
                            id_sb[:], start=True, stop=True)
                    inc(last, "bb", 2)

                def _ln_v(v, ch, BB3):
                    v.wait_ge(sems["bb"], BB3["bb"] + 2 * (ch + 1))
                    inc(v.tensor_copy(m2b[:], psbc2[:, 0:TCW]), "bc")
                    inc(v.tensor_copy(r2b[:], psbc2[:, 512:512 + TCW]), "bc")
                    v.wait_ge(sems["dxa"], BB3["dxa"] + 16 * (ch + 1))
                    for jj in range(8):
                        hp = jj % 2
                        if jj >= 2:
                            v.wait_ge(sems["hs"], BB3["hs"] + 8 * ch + jj - 1)
                        elif ch >= 1:
                            v.wait_ge(sems["hs"], BB3["hs"] + 8 * ch)
                        xj = xsl[:, (ch % 2) * XW + jj * TCW:
                                 (ch % 2) * XW + (jj + 1) * TCW]
                        v.tensor_tensor(
                            htmp[:, hp * TCW:(hp + 1) * TCW],
                            xj, r2b[:], ALU.mult)
                        inc(v.tensor_sub(
                            htmp[:, hp * TCW:(hp + 1) * TCW],
                            htmp[:, hp * TCW:(hp + 1) * TCW],
                            m2b[:]), "hm")

                def _ln_s(s, ch, BB3):
                    if ch >= 1:
                        s.wait_ge(sems["f1"], BB3["f1"] + FT * ch)
                    for jj in range(8):
                        hp = jj % 2
                        s.wait_ge(sems["hm"], BB3["hm"] + 8 * ch + jj + 1)
                        inc(s.activation(
                            hsl[:, jj * TCW:(jj + 1) * TCW],
                            htmp[:, hp * TCW:(hp + 1) * TCW],
                            AF.Identity,
                            bias=bcol[:, LNB0 + jj:LNB0 + jj + 1],
                            scale=bcol[:, LNG0 + jj:LNG0 + jj + 1]), "hs")

                with nc.Block() as blk:
                    BB3 = snap()

                    @blk.sync
                    def _(e):
                        e.wait_ge(sems["d30"], BB3["d30"])
                        e.wait_ge(sems["d31"], BB3["d31"])
                        for ch in range(NCHK):
                            _dxa(e, ch, BB3)

                    @blk.tensor
                    def _(t):
                        _bb(t, 0, BB3)
                        for ch in range(NCHK):
                            if ch + 1 < NCHK:
                                _bb(t, ch + 1, BB3)
                            t.wait_ge(sems["hs"], BB3["hs"] + 8 * (ch + 1))
                            for ft in range(FT):
                                if ch >= 1 or ft >= 4:
                                    t.wait_ge(sems["fe"],
                                              BB3["fe"] + FT * ch + ft - 3)
                                last = None
                                for jj in range(8):
                                    last = t.matmul(
                                        pf[ft % 4][:, 0:TCW],
                                        w1sb[:, jj * F_FFN + ft * 128:
                                             jj * F_FFN + (ft + 1) * 128],
                                        hsl[:, jj * TCW:(jj + 1) * TCW],
                                        start=(jj == 0), stop=(jj == 7))
                                inc(last, "f1")
                            t.wait_ge(sems["fe"], BB3["fe"] + FT * (ch + 1))
                            for ht in range(8):
                                if 8 * ch + ht - 1 > 0:
                                    t.wait_ge(sems["vm"],
                                              BB3["vm"] + 8 * ch + ht - 1)
                                last = None
                                for ft in range(FT):
                                    last = t.matmul(
                                        pg[ht % 2][:, 0:TCW],
                                        w2sb[:, ft * 1024 + ht * 128:
                                             ft * 1024 + (ht + 1) * 128],
                                        y1s[:, ft * TCW:(ft + 1) * TCW],
                                        start=(ft == 0), stop=(ft == FT - 1))
                                inc(last, "f2")

                    @blk.vector
                    def _(v):
                        v = DW(v)
                        _ln_v(v, 0, BB3)
                        for ch in range(NCHK):
                            if ch + 1 < NCHK:
                                _ln_v(v, ch + 1, BB3)
                            for ht in range(8):
                                gp = ht % 2
                                v.wait_ge(sems["f2"],
                                          BB3["f2"] + 8 * ch + ht + 1)
                                if 8 * ch + ht - 1 > 0:
                                    v.wait_ge(sems["ge"],
                                              BB3["ge"] + 8 * ch + ht - 1)
                                xj = xsl[:, (ch % 2) * XW + ht * TCW:
                                         (ch % 2) * XW + (ht + 1) * TCW]
                                inc(v.scalar_tensor_tensor(
                                    tgs[:, gp * TCW:(gp + 1) * TCW],
                                    pg[ht % 2][:, 0:TCW], 0.5, xj,
                                    ALU.mult, ALU.add), "vm")

                    @blk.scalar
                    def _(s):
                        s = DW(s)
                        _ln_s(s, 0, BB3)
                        for ch in range(NCHK):
                            if ch >= 1:
                                s.wait_ge(sems["f2"], BB3["f2"] + 8 * ch)
                            for ft in range(FT):
                                s.wait_ge(sems["f1"],
                                          BB3["f1"] + FT * ch + ft + 1)
                                inc(s.activation(y1s[:, ft * TCW:(ft + 1) * TCW],
                                                 pf[ft % 4][:, 0:TCW], AF.Silu,
                                                 bias=f1bcol[:, ft:ft + 1]), "fe")
                            if ch + 1 < NCHK:
                                _ln_s(s, ch + 1, BB3)
                            for ht in range(8):
                                gp = ht % 2
                                s.wait_ge(sems["vm"],
                                          BB3["vm"] + 8 * ch + ht + 1)
                                inc(s.activation(
                                    xsl[:, (ch % 2) * XW + ht * TCW:
                                        (ch % 2) * XW + (ht + 1) * TCW],
                                    tgs[:, gp * TCW:(gp + 1) * TCW], AF.Silu,
                                    bias=bcol[:, HB20 + ht:HB20 + ht + 1]), "ge")

                    @blk.gpsimd
                    def _(g):
                        for ch in range(NCHK):
                            b, tc = ch // NTC, ch % NTC
                            g.wait_ge(sems["ge"], BB3["ge"] + 8 * (ch + 1))
                            dst = gtsc.ap()[b, :, :].rearrange(
                                "(j p) s -> p j s", p=128)[:, :,
                                                           tc * TCW:(tc + 1) * TCW]
                            src = xsl[:, (ch % 2) * XW:(ch % 2 + 1) * XW].rearrange(
                                "p (j c) -> p j c", j=8)
                            inc(g.dma_start(dst, src), "dga", 16)

        if True:
            VROW = 70
            TROW = H * VROW
            SCHED = attn_sched()
            NCH = len(SCHED)
            CB = [0] * (NT + 1)
            for (kt_, q_, l_, f_) in SCHED:
                CB[kt_ + 1] += 1
            for i in range(NT):
                CB[i + 1] += CB[i]

            for b in range(B_PER):
                with contextlib.ExitStack() as p4b:
                    A4 = lambda sh, dt_: p4b.enter_context(nc.sbuf_tensor(_nm('A4'), sh, dt_))
                    xnt1 = A4([128, 8 * S], BF16)
                    qT = A4([128, 8 * S], BF16)
                    kT = A4([128, 8 * S], BF16)
                    vext = A4([128, NT * TROW], BF16)
                    with nc.Block() as blk:
                        @blk.sync
                        def _(e):
                            e.wait_ge(sems["d30"], semv["d30"])
                            e.wait_ge(sems["d31"], semv["d31"])
                            e.wait_ge(sems["dga"], semv["dga"])
                            e.wait_ge(sems["st0"], semv["st0"])
                            e.wait_ge(sems["st1"], semv["st1"])
                            dst = xnt1[:].rearrange("p (j s) -> p j s", j=8)
                            src = xn1sc.ap()[b, :, :].rearrange("(j p) s -> p j s", p=128)
                            inc(e.dma_start(dst, src), "dxa", 16)
                            e.wait_ge(sems["dxa"], semv["dxa"])

                    with contextlib.ExitStack() as p4p:
                        A5 = lambda sh, dt_: p4p.enter_context(nc.sbuf_tensor(_nm('A5'), sh, dt_))
                        P5 = lambda sh, dt_: p4p.enter_context(nc.psum_tensor(_nm('P5'), sh, dt_))
                        wvsb = A5([128, 8 * 1024], BF16)
                        wqsb = A5([128, 8 * 1024], BF16)
                        wksb = A5([128, 8 * 1024], BF16)
                        qbb = A5([128, 4 * 512], BF16)
                        t1b = A5([128, 512], BF16)
                        t2b = A5([128, 512], BF16)
                        cosb = A5([128, S], BF16)
                        sinb = A5([128, S], BF16)
                        bvb = A5([128, 1024], F32)
                        pp = [P5([128, 512], F32) for _ in range(4)]
                        pj = [P5([128, 512], F32) for _ in range(2)]
                        psv = P5([128, 1024], F32)
                        with nc.Block() as blk:
                            BB = snap()

                            @blk.sync
                            def _(e):
                                inc(e.dma_start(cosb[:], cosd.ap()), "dp", 16)
                                inc(e.dma_start(sinb[:], sind.ap()), "dp", 16)

                            @blk.gpsimd
                            def _(g):
                                inc(g.dma_start(
                                    bvb[:], bvr.ap().partition_broadcast(128)),
                                    "dbv", 16)
                                for wi, wd in enumerate([wqd, wkd]):
                                    dst = (wqsb if wi == 0 else wksb)[:].rearrange(
                                        "p (j c) -> p j c", j=8)
                                    src = wd.ap().rearrange(
                                        "(j p) hd -> p j hd", p=128)
                                    inc(g.dma_start(dst, src),
                                        "dq" + str(wi), 16)
                                dst = wvsb[:].rearrange("p (j c) -> p j c", j=8)
                                src = wvd.ap().rearrange("(j p) hd -> p j hd", p=128)
                                inc(g.dma_start(dst, src), "dq", 16)

                            @blk.tensor
                            def _(t):
                                kk = 0
                                for wi in range(2):
                                    wsb = wqsb if wi == 0 else wksb
                                    t.wait_ge(sems["dq" + str(wi)],
                                              BB["dq" + str(wi)] + 16)
                                    for jt in range(8):
                                        for tq in range(4):
                                            if kk >= 4:
                                                t.wait_ge(sems["qb"], BB["qb"] + kk - 3)
                                            last = None
                                            for jj in range(8):
                                                last = t.matmul(
                                                    pp[kk % 4][:],
                                                    wsb[:, jj * 1024 + jt * 128:
                                                        jj * 1024 + (jt + 1) * 128],
                                                    xnt1[:, jj * S + tq * 512:
                                                         jj * S + (tq + 1) * 512],
                                                    start=(jj == 0), stop=(jj == 7))
                                            inc(last, "qm")
                                            t.wait_ge(sems["qb"], BB["qb"] + kk + 1)
                                            if kk >= 2:
                                                t.wait_ge(sems["rp"], BB["rp"] + kk - 1)
                                            inc(t.matmul(
                                                pj[kk % 2][:], j_sb[:],
                                                qbb[:, (kk % 4) * 512:(kk % 4 + 1) * 512],
                                                start=True, stop=True), "jm")
                                            kk += 1
                                t.wait_ge(sems["dq"], BB["dq"] + 16)
                                t.wait_ge(sems["dbv"], BB["dbv"] + 16)
                                for tau in range(NT):
                                    if tau >= 1:
                                        t.wait_ge(sems["ve"], BB["ve"] + tau)
                                    last = None
                                    for jj in range(8):
                                        for hh in range(2):
                                            last = t.matmul(
                                                psv[:, hh * 512:(hh + 1) * 512],
                                                xnt1[:, jj * S + tau * 128:
                                                     jj * S + (tau + 1) * 128],
                                                wvsb[:, jj * 1024 + hh * 512:
                                                     jj * 1024 + (hh + 1) * 512],
                                                start=(jj == 0), stop=(jj == 7))
                                    inc(last, "vm")

                            @blk.scalar
                            def _(s):
                                s = DW(s)
                                kk = 0
                                for wi in range(2):
                                    bb_ = [BQ0, BK0][wi]
                                    for jt in range(8):
                                        for tq in range(4):
                                            s.wait_ge(sems["qm"],
                                                      BB["qm"] + kk + 1)
                                            if kk >= 4:
                                                s.wait_ge(sems["jm"],
                                                          BB["jm"] + kk - 3)
                                                s.wait_ge(sems["rp"],
                                                          BB["rp"] + kk - 3)
                                            inc(s.activation(
                                                qbb[:, (kk % 4) * 512:
                                                    (kk % 4 + 1) * 512],
                                                pp[kk % 4][:], AF.Identity,
                                                bias=bcol[:, bb_ + jt:bb_ + jt + 1]),
                                                "qb")
                                            kk += 1

                            @blk.vector
                            def _(v):
                                v = DW(v)
                                v.wait_ge(sems["dp"], BB["dp"] + 16 * 2)
                                v.memset(vext[:], 0.0)
                                ap_ones = vext[:].rearrange(
                                    "p (tau g i) -> p tau g i", tau=NT, g=4)[
                                    :, :, :, 0:1]
                                ap1 = vext[:].rearrange(
                                    "p (tau rest) -> p tau rest", tau=NT)[
                                    :, :, 64:64 + 280 * 3 + 71 * 3 + 1]
                                for ii in range(4):
                                    ap2 = vext[:].rearrange(
                                        "p (tau g rest) -> p tau g rest",
                                        tau=NT, g=4)[:, :, :, 70 * ii + 64 + ii:
                                                     70 * ii + 64 + ii + 1]
                                    v.memset(ap2, 1.0)
                                kk = 0
                                for wi in range(2):
                                    bb_ = [BQ0, BK0][wi]
                                    dstT = [qT, kT][wi]
                                    for jt in range(8):
                                        for tq in range(4):
                                            v.wait_ge(sems["qb"], BB["qb"] + kk + 1)
                                            v.tensor_mul(
                                                t1b[:],
                                                qbb[:, (kk % 4) * 512:(kk % 4 + 1) * 512],
                                                cosb[:, tq * 512:(tq + 1) * 512])
                                            v.wait_ge(sems["jm"], BB["jm"] + kk + 1)
                                            v.tensor_tensor(
                                                t2b[:], pj[kk % 2][:],
                                                sinb[:, tq * 512:(tq + 1) * 512],
                                                ALU.mult)
                                            inc(v.tensor_add(
                                                dstT[:, jt * S + tq * 512:
                                                     jt * S + (tq + 1) * 512],
                                                t1b[:], t2b[:]), "rp")
                                            kk += 1
                                for tau in range(NT):
                                    v.wait_ge(sems["vm"], BB["vm"] + tau + 1)
                                    dst = vext[:, tau * TROW:(tau + 1) * TROW].rearrange(
                                        "p (h c) -> p h c", h=H)[:, :, 0:64]
                                    src = psv[:].rearrange("p (h c) -> p h c", h=H)
                                    bsr = bvb[:].rearrange("p (h c) -> p h c", h=H)
                                    inc(v.tensor_tensor(dst, src, bsr, ALU.add), "ve")

                    with contextlib.ExitStack() as p4a:
                        A6 = lambda sh, dt_: p4a.enter_context(nc.sbuf_tensor(_nm('A6'), sh, dt_))
                        oallT = A6([128, 8 * S], BF16)
                        sums = A6([16, S], F32)
                        with contextlib.ExitStack() as p4c:
                            A7 = lambda sh, dt_: p4c.enter_context(nc.sbuf_tensor(_nm('A7'), sh, dt_))
                            P7 = lambda sh, dt_: p4c.enter_context(nc.psum_tensor(_nm('P7'), sh, dt_))
                            ptb = A7([128, 2 * S], BF16)
                            stage4 = A7([4, 2 * S], F32)
                            pst = [P7([128, 1024], F32) for _ in range(2)]
                            pso = [P7([68, 1024], F32) for _ in range(2)]
                            LN_ = ("A", "B")

                            def job_chunks(g_, qh):
                                ch = []
                                ktmax = 8 * qh + 7
                                for hm, i_ in enumerate((3, 2, 1, 0)):
                                    h = 4 * g_ + i_
                                    for kt in range(ktmax + 1):
                                        q0g = max(kt * 128, qh * 1024)
                                        ch.append(dict(
                                            hm=hm, h=h, i=i_, kt=kt, q0g=q0g,
                                            q0r=q0g - qh * 1024,
                                            L=qh * 1024 + 1024 - q0g,
                                            first=(kt == 0), last=(kt == ktmax),
                                            diag=(q0g == kt * 128)))
                                return ch

                            PAIRS = [((0, 1), (1, 1)), ((2, 1), (3, 1)),
                                     ((0, 0), (1, 0)), ((2, 0), (3, 0))]
                            for (jA, jB) in PAIRS:
                                jobs = (jA, jB)
                                CH = [job_chunks(*jA), job_chunks(*jB)]
                                n = len(CH[0])
                                META = []
                                for li in range(2):
                                    cumE = cumS = cumPV = mcnt = 0
                                    lastE = [0, 0]
                                    cumPVl = []
                                    lst = []
                                    for ci, c in enumerate(CH[li]):
                                        sp = [(off, min(512, c["L"] - off))
                                              for off in range(0, c["L"], 512)]
                                        e_before = cumE
                                        s_before = cumS
                                        swaits = [lastE[off // 512]
                                                  for (off, _) in sp]
                                        for idx, (off, _) in enumerate(sp):
                                            lastE[off // 512] = e_before + idx + 1
                                        cumE += len(sp)
                                        cumS += len(sp)
                                        pv = []
                                        for (pa, pl) in aligned_pieces(
                                                c["q0r"], c["L"]):
                                            k = -(-(pa - c["q0r"] + pl) // 512)
                                            pv.append((pa, pl, e_before + k,
                                                       c["diag"] and
                                                       (pa - c["q0r"]) < 128))
                                        if c["diag"]:
                                            mcnt += 1
                                        pv_before = cumPV
                                        cumPV += len(pv)
                                        cumPVl.append(cumPV)
                                        lst.append(dict(
                                            c=c, sp=sp, swaits=swaits,
                                            s_before=s_before,
                                            e_before=e_before, pv=pv,
                                            mcnt=mcnt, pv_after=cumPV,
                                            pv_upto2=(cumPVl[ci - 2]
                                                      if ci >= 2 else 0)))
                                    META.append(lst)
                                with nc.Block() as blk:
                                    BB = snap()

                                    @blk.tensor
                                    def _(t, META=META, n=n):
                                        for cstep in range(n + 1):
                                            for li in range(2):
                                                if cstep >= n:
                                                    continue
                                                ci = cstep
                                                m = META[li][ci]
                                                c = m["c"]
                                                xe = "xe" + LN_[li]
                                                h = c["h"]
                                                jt, r0 = h // 2, (h % 2) * 64
                                                for (off, Lp), sw in zip(
                                                        m["sp"], m["swaits"]):
                                                    t.wait_ge(sems[xe],
                                                              BB[xe] + sw)
                                                    inc(t.matmul(
                                                        pst[li][:, off:off + Lp],
                                                        kT[r0:r0 + 64,
                                                           jt * S + c["kt"] * 128:
                                                           jt * S + (c["kt"] + 1) * 128],
                                                        qT[r0:r0 + 64,
                                                           jt * S + c["q0g"] + off:
                                                           jt * S + c["q0g"] + off + Lp],
                                                        start=True, stop=True),
                                                        "stm" + LN_[li])
                                            for li in range(2):
                                                if cstep < 1:
                                                    continue
                                                ci = cstep - 1
                                                m = META[li][ci]
                                                c = m["c"]
                                                xe = "xe" + LN_[li]
                                                if c["first"]:
                                                    t.wait_ge(sems["oc" + LN_[li]],
                                                              BB["oc" + LN_[li]]
                                                              + c["hm"])
                                                h, i_, kt = c["h"], c["i"], c["kt"]
                                                base = li * 2048 + (ci % 2) * 1024
                                                for (pa, pl, ereq, needm) in m["pv"]:
                                                    t.wait_ge(sems[xe],
                                                              BB[xe] + ereq)
                                                    if needm:
                                                        t.wait_ge(
                                                            sems["msk" + LN_[li]],
                                                            BB["msk" + LN_[li]]
                                                            + m["mcnt"])
                                                    inc(t.matmul(
                                                        pso[li][0:65 + i_, pa:pa + pl],
                                                        vext[:, kt * TROW + VROW * h:
                                                             kt * TROW + VROW * h
                                                             + 65 + i_],
                                                        ptb[:, base + pa - c["q0r"]:
                                                            base + pa - c["q0r"] + pl],
                                                        start=c["first"],
                                                        stop=c["last"],
                                                        skip_group_check=True),
                                                        "pvm" + LN_[li])

                                    @blk.scalar
                                    def _(s, META=META, n=n):
                                        s = DW(s)
                                        for ci in range(n):
                                            for li in range(2):
                                                m = META[li][ci]
                                                base = li * 2048 + (ci % 2) * 1024
                                                s.wait_ge(sems["pvm" + LN_[li]],
                                                          BB["pvm" + LN_[li]]
                                                          + m["pv_upto2"])
                                                for idx, (off, Lp) in enumerate(
                                                        m["sp"]):
                                                    s.wait_ge(
                                                        sems["stm" + LN_[li]],
                                                        BB["stm" + LN_[li]]
                                                        + m["s_before"] + idx + 1)
                                                    inc(s.activation(
                                                        ptb[:, base + off:
                                                            base + off + Lp],
                                                        pst[li][:, off:off + Lp],
                                                        AF.Exp,
                                                        scale=0.125),
                                                        "xe" + LN_[li])

                                    @blk.vector
                                    def _(v, jobs=jobs, META=META, n=n):
                                        v = DW(v)
                                        for ci in range(n):
                                            for li in range(2):
                                                m = META[li][ci]
                                                c = m["c"]
                                                if c["diag"]:
                                                    v.wait_ge(sems["xe" + LN_[li]],
                                                              BB["xe" + LN_[li]]
                                                              + m["e_before"] + 1)
                                                    base = (li * 2048
                                                            + (ci % 2) * 1024)
                                                    seg = ptb[:, base:base + 128]
                                                    inc(v.tensor_mul(seg, seg,
                                                                     mask_sb[:]),
                                                        "msk" + LN_[li])
                                                if c["last"]:
                                                    g_, qh = jobs[li]
                                                    h = c["h"]
                                                    v.wait_ge(sems["pvm" + LN_[li]],
                                                              BB["pvm" + LN_[li]]
                                                              + m["pv_after"])
                                                    inc(v.tensor_copy(
                                                        oallT[(h % 2) * 64:
                                                              (h % 2) * 64 + 64,
                                                              (h // 2) * S + qh * 1024:
                                                              (h // 2) * S
                                                              + qh * 1024 + 1024],
                                                        pso[li][0:64, :]),
                                                        "oc" + LN_[li])
                                                    if c["hm"] == 3:
                                                        v.wait_ge(
                                                            sems["ssd" + LN_[li]],
                                                            BB["ssd" + LN_[li]])
                                                        inc(v.tensor_copy(
                                                            stage4[0:4, li * 1024:
                                                                   li * 1024 + 1024],
                                                            pso[li][64:68, :]),
                                                            "oc" + LN_[li])

                                    @blk.sync
                                    def _(e, jobs=jobs):
                                        for li in range(2):
                                            g_, qh = jobs[li]
                                            e.wait_ge(sems["oc" + LN_[li]],
                                                      BB["oc" + LN_[li]] + 5)
                                            inc(e.dma_start(
                                                sums[4 * g_:4 * g_ + 4,
                                                     qh * 1024:qh * 1024 + 1024],
                                                stage4[0:4, li * 1024:
                                                       li * 1024 + 1024]),
                                                "ssd" + LN_[li], 16)

                        with contextlib.ExitStack() as p4w:
                            A8 = lambda sh, dt_: p4w.enter_context(nc.sbuf_tensor(_nm('A8'), sh, dt_))
                            P8 = lambda sh, dt_: p4w.enter_context(nc.psum_tensor(_nm('P8'), sh, dt_))
                            wosb = A8([128, 8 * 1024], BF16)
                            sums_bf = A8([16, S], BF16)
                            pw = [P8([128, 512], F32) for _ in range(2)]
                            pnb = P8([128, 512], F32)
                            with nc.Block() as blk:
                                BB = snap()

                                @blk.gpsimd
                                def _(e):
                                    e.wait_ge(sems["ssdA"], BB["ssdA"])
                                    e.wait_ge(sems["ssdB"], BB["ssdB"])
                                    dst = wosb[:].rearrange("p (j c) -> p j c", j=8)
                                    src = wod.ap().rearrange("(j p) e -> p j e", p=128)
                                    inc(e.dma_start(dst, src), "dq", 16)

                                @blk.vector
                                def _(v):
                                    v = DW(v)
                                    v.wait_ge(sems["ssdA"], BB["ssdA"])
                                    v.wait_ge(sems["ssdB"], BB["ssdB"])
                                    v.reciprocal(sums[:], sums[:])
                                    inc(v.tensor_copy(sums_bf[:], sums[:]), "rc")
                                    k = 0
                                    for jt in range(8):
                                        for tq in range(4):
                                            v.wait_ge(sems["nbm"], BB["nbm"] + k + 1)
                                            seg = oallT[:, jt * S + tq * 512:
                                                        jt * S + (tq + 1) * 512]
                                            inc(v.tensor_tensor(seg, seg, pnb[:],
                                                                ALU.mult), "nbc")
                                            k += 1
                                    k = 0
                                    for et in range(8):
                                        for tq in range(4):
                                            v.wait_ge(sems["wm"], BB["wm"] + k + 1)
                                            inc(v.tensor_scalar(
                                                xnt1[:, et * S + tq * 512:
                                                     et * S + (tq + 1) * 512],
                                                pw[k % 2][:],
                                                bcol[:, BO0 + et:BO0 + et + 1], None,
                                                ALU.add), "we")
                                            k += 1

                                @blk.tensor
                                def _(t):
                                    t.wait_ge(sems["rc"], BB["rc"] + 1)
                                    k = 0
                                    for jt in range(8):
                                        for tq in range(4):
                                            if k >= 1:
                                                t.wait_ge(sems["nbc"], BB["nbc"] + k)
                                            inc(t.matmul(
                                                pnb[:], ind_sb[:, jt * 128:(jt + 1) * 128],
                                                sums_bf[:, tq * 512:(tq + 1) * 512],
                                                start=True, stop=True), "nbm")
                                            k += 1
                                    t.wait_ge(sems["dq"], BB["dq"] + 16)
                                    t.wait_ge(sems["nbc"], BB["nbc"] + 32)
                                    k = 0
                                    for et in range(8):
                                        for tq in range(4):
                                            if k >= 2:
                                                t.wait_ge(sems["we"], BB["we"] + k - 1)
                                            last = None
                                            for jj in range(8):
                                                last = t.matmul(
                                                    pw[k % 2][:],
                                                    wosb[:, jj * 1024 + et * 128:
                                                         jj * 1024 + (et + 1) * 128],
                                                    oallT[:, jj * S + tq * 512:
                                                          jj * S + (tq + 1) * 512],
                                                    start=(jj == 0), stop=(jj == 7))
                                            inc(last, "wm")
                                            k += 1

                    with contextlib.ExitStack() as p4e:
                        A9 = lambda sh, dt_: p4e.enter_context(nc.sbuf_tensor(_nm('A9'), sh, dt_))
                        xta = A9([128, 2 * 1024], F32)
                        x2t = A9([128, 2 * 1024], F32)
                        pstr = p4e.enter_context(nc.psum_tensor(_nm('pstr_e'), [128, 2048], BF16))
                        with nc.Block() as blk:
                            BB = snap()

                            @blk.sync
                            def _(e):
                                e.wait_ge(sems["dga"], BB["dga"])
                                e.wait_ge(sems["d10"], BB["d10"])
                                e.wait_ge(sems["d11"], BB["d11"])
                                dst = qT[:].rearrange("p (j s) -> p j s", j=8)
                                src = gtsc.ap()[b, :, :].rearrange(
                                    "(j p) s -> p j s", p=128)
                                inc(e.dma_start(dst, src), "dg", 16)
                                def _store(tau):
                                    p = tau % 2
                                    e.wait_ge(sems["ex"], BB["ex"] + tau + 1)
                                    inc(e.dma_start(
                                        out.ap()[b * S + tau * 128:
                                                 b * S + (tau + 1) * 128, :],
                                        x2t[:, p * 1024:(p + 1) * 1024]),
                                        "st" + str(p), 16)

                                for tau in range(NT):
                                    p = tau % 2
                                    if tau >= 2:
                                        _store(tau - 2)
                                    inc(e.dma_start(
                                        xta[:, p * 1024:(p + 1) * 1024],
                                        x1sc.ap()[b * S + tau * 128:
                                                  b * S + (tau + 1) * 128, :]),
                                        "d2" + str(p), 16)
                                _store(NT - 2)
                                _store(NT - 1)

                            @blk.vector
                            def _(v):
                                v = DW(v)
                                v.wait_ge(sems["dg"], BB["dg"] + 16)
                                for jj in range(8):
                                    seg = xnt1[:, jj * S:(jj + 1) * S]
                                    inc(v.tensor_mul(seg, seg, qT[:, jj * S:(jj + 1) * S]),
                                        "sg")
                                for tau in range(NT):
                                    p = tau % 2
                                    v.wait_ge(sems["et"], BB["et"] + tau + 1)
                                    v.wait_ge(sems["d2" + str(p)],
                                              BB["d2" + str(p)] + 16 * (tau // 2 + 1))
                                    if tau >= 2:
                                        v.wait_ge(sems["st" + str(p)],
                                                  BB["st" + str(p)] + 16 * (tau // 2))
                                    inc(v.tensor_tensor(
                                        x2t[:, p * 1024:(p + 1) * 1024],
                                        xta[:, p * 1024:(p + 1) * 1024],
                                        pstr[:, p * 1024:(p + 1) * 1024],
                                        ALU.add), "ex")

                            @blk.tensor
                            def _(t):
                                t.wait_ge(sems["sg"], BB["sg"] + 8)
                                for tau in range(NT):
                                    p = tau % 2
                                    if tau >= 2:
                                        t.wait_ge(sems["ex"], BB["ex"] + tau - 1)
                                    last = None
                                    for jj in range(8):
                                        last = t.transpose(
                                            pstr[:, p * 1024 + jj * 128:
                                                 p * 1024 + (jj + 1) * 128],
                                            xnt1[:, jj * S + tau * 128:
                                                 jj * S + (tau + 1) * 128],
                                            id_sb[:])
                                    inc(last, "et")

        with nc.Block() as blk:
            @blk.sync
            def _(e):
                e.wait_ge(sems["st0"], semv["st0"])
                e.wait_ge(sems["st1"], semv["st1"])

    return nc


def _consts():
    ident = np.eye(128, dtype=np.float32)
    jt = np.zeros((128, 128), np.float32)
    for blk in range(2):
        o = blk * 64
        jt[o:o + 32, o + 32:o + 64] = np.eye(32)
        jt[o + 32:o + 64, o:o + 32] = -np.eye(32)
    inv = 1.0 / (10000.0 ** (np.arange(0, 64, 2, dtype=np.float64) / 64.0))
    pos = np.arange(S, dtype=np.float64)
    ang = inv[:, None] * pos[None, :]
    cosr = np.tile(np.cos(ang), (4, 1))
    sinr = np.tile(np.sin(ang), (4, 1))
    mask = (np.arange(128)[:, None] <= np.arange(128)[None, :]).astype(np.float32)
    ind16 = np.zeros((16, D), np.float32)
    for h in range(16):
        jt_ = h // 2
        r0 = (h % 2) * 64
        ind16[h, jt_ * 128 + r0: jt_ * 128 + r0 + 64] = 1.0
    import ml_dtypes
    bf = lambda a: np.ascontiguousarray(a).astype(ml_dtypes.bfloat16)
    return dict(identd=bf(ident), jrotd=bf(jt), cosd=bf(cosr), sind=bf(sinr),
                maskd=bf(mask), ind16d=bf(ind16))


def _cols(vec, k):
    return np.ascontiguousarray(vec.reshape(k, 128).T)


def kernel(**inputs):
    f32 = lambda a: np.ascontiguousarray(a, dtype=np.float32)
    i32 = lambda a: np.ascontiguousarray(a, dtype=np.int32)
    x = f32(inputs["L0_attn_x"]).reshape(B_ALL, S, D)
    base = {
        "delta": f32(inputs["L0_attn_delta"].reshape(8, 128, 2)
                     .transpose(1, 0, 2).reshape(128, 16)),
        "alpha": f32(inputs["L0_attn_alpha"].reshape(8, 128, 2)
                     .transpose(1, 0, 2).reshape(128, 16)),
        "betaT": f32(inputs["L0_attn_beta"].reshape(8, 128, 2)
                     .transpose(1, 0, 2).reshape(128, 16)),
        "gammaT": f32(inputs["L0_attn_gamma"].reshape(8, 128, 2)
                      .transpose(1, 0, 2).reshape(128, 16)),
        "omegaT": _cols(f32(inputs["L0_attn_omega"]), 8),
        "w0": f32(inputs["L0_ffn_weight"]),
        "wqd": f32(inputs["L1_attn_Wq"]),
        "wkd": f32(inputs["L1_attn_Wkv"][:, :D]),
        "wvd": f32(inputs["L1_attn_Wkv"][:, D:]),
        "bqc": _cols(f32(inputs["L1_attn_bq"]), 8),
        "bkc": _cols(f32(inputs["L1_attn_bkv"][:D]), 8),
        "bvr": f32(inputs["L1_attn_bkv"][D:]).reshape(1, D),
        "wod": f32(inputs["L1_attn_Wo"]),
        "boc": _cols(f32(inputs["L1_attn_bo"]), 8),
        "lngc": _cols(f32(inputs["L1_ffn_ln_gamma"]), 8),
        "lnbc": _cols(f32(inputs["L1_ffn_ln_beta"]), 8),
        "fc1q": i32(inputs["L1_ffn_fc1_q"]),
        "fc1s": f32(inputs["L1_ffn_fc1_scales"]),
        "fc1bc": _cols(f32(inputs["L1_ffn_fc1_bias"]), 32),
        "fc2q": i32(inputs["L1_ffn_fc2_q"]),
        "fc2s": f32(inputs["L1_ffn_fc2_scales"]),
        "fc2bc": _cols(f32(inputs["L1_ffn_fc2_bias"]), 8),
    }
    base.update(_consts())

    nc = bass.Bass("TRN2", target_bir_lowering=False)
    build(nc)

    in_maps = []
    for c in range(NCORES):
        m = dict(base)
        m["x"] = np.ascontiguousarray(
            x[c * B_PER:(c + 1) * B_PER].reshape(T, D))
        in_maps.append(m)
    res = run_bass_kernel_spmd(nc, in_maps, core_ids=list(range(NCORES)))
    outs = [res.results[c]["out"].reshape(B_PER, S, D) for c in range(NCORES)]
    return np.concatenate(outs, axis=0).astype(np.float32)


# version 15 (final)
# speedup vs baseline: 1.0599x; 1.0075x over previous
import math
import contextlib
import numpy as np
import concourse.bass as bass
import concourse.mybir as mybir
from concourse.bass_utils import run_bass_kernel_spmd

F32 = mybir.dt.float32
BF16 = mybir.dt.bfloat16
I32 = mybir.dt.int32
AF = mybir.ActivationFunctionType
ALU = mybir.AluOpType
AXX = mybir.AxisListType.X

B_ALL, S, D = 16, 2048, 1024
NCORES = 8
B_PER = B_ALL // NCORES
T = B_PER * S
NT = S // 128
DT = D // 128
H, DH = 16, 64
F_FFN = 4096
FT = F_FFN // 128

STAGE = "FULL"

_P1RANK = {"P1L": 0, "P1M1": 1, "P1M2": 2, "P1M3": 3, "P1M": 4}


def _p1rank():
    return _P1RANK.get(STAGE, 99)


_OUT_KW = ("accum_out",)


def _ap_rng(a):
    try:
        name = a.tensor.name
        lo = a.offset
        ext = 1
        for st, ct in a.ap:
            if st > 0:
                ext += st * (ct - 1)
        return (name, lo, lo + ext)
    except Exception:
        return None


class DW:

    _OPS = {
        "tensor_tensor", "tensor_add", "tensor_sub", "tensor_mul", "tensor_max",
        "tensor_scalar", "tensor_single_scalar", "tensor_scalar_add",
        "tensor_scalar_sub", "tensor_scalar_mul", "tensor_scalar_max",
        "tensor_scalar_min", "tensor_relu", "tensor_reduce", "reduce_max",
        "reduce_sum", "scalar_tensor_tensor", "tensor_tensor_scan",
        "tensor_copy", "memset", "memzero", "reciprocal", "activation",
        "copy", "add", "mul", "square", "sqrt", "sign", "select",
    }
    _NO_IN = {"memset", "memzero"}

    def __init__(self, eng):
        self._e = eng
        self._w = []
        self._r = []

    def _overlap(self, rng, lst):
        if rng is None:
            return True
        for (n, lo, hi) in lst:
            if n == rng[0] and rng[1] < hi and lo < rng[2]:
                return True
        return False

    def __getattr__(self, n):
        f = getattr(self._e, n)
        if n not in DW._OPS:
            return f

        def g(*a, **k):
            outs, ins = [], []
            if a:
                outs.append(_ap_rng(a[0]))
            for x in a[1:]:
                r = _ap_rng(x)
                if r is not None:
                    ins.append(r)
            for kw, v in k.items():
                r = _ap_rng(v)
                if r is None:
                    continue
                (outs if kw in _OUT_KW else ins).append(r)
            need = False
            for r in ins:
                if self._overlap(r, self._w):
                    need = True
                    break
            if not need:
                for r in outs:
                    if self._overlap(r, self._w) or self._overlap(r, self._r):
                        need = True
                        break
            if need:
                self._e.drain()
                self._w.clear()
                self._r.clear()
            res = f(*a, **k)
            for r in outs:
                self._w.append(r if r is not None else ("?", 0, 1 << 60))
            if n not in DW._NO_IN:
                for r in ins:
                    self._r.append(r)
            return res
        return g
def attn_sched():
    sched = []
    for kt in range(NT):
        q = kt * 128
        first = True
        while q < S:
            ln = min(1024, S - q)
            sched.append((kt, q, ln, first))
            first = False
            q += ln
    return sched


def aligned_pieces(q, ln):
    res = []
    e = q + ln
    while q < e:
        nxt = min(e, (q // 512 + 1) * 512)
        res.append((q, nxt - q))
        q = nxt
    return res


def build(nc):
    _uid = [0]

    def _nm(tag):
        _uid[0] += 1
        return f'{tag}_{_uid[0]}'

    xin = nc.dram_tensor("x", [T, D], F32, kind="ExternalInput")
    delta = nc.dram_tensor("delta", [128, 16], F32, kind="ExternalInput")
    alpha = nc.dram_tensor("alpha", [128, 16], F32, kind="ExternalInput")
    betaT = nc.dram_tensor("betaT", [128, 16], F32, kind="ExternalInput")
    gammaT = nc.dram_tensor("gammaT", [128, 16], F32, kind="ExternalInput")
    omegaT = nc.dram_tensor("omegaT", [128, 8], F32, kind="ExternalInput")
    w0 = nc.dram_tensor("w0", [D, D], F32, kind="ExternalInput")
    wqd = nc.dram_tensor("wqd", [D, D], F32, kind="ExternalInput")
    wkd = nc.dram_tensor("wkd", [D, D], F32, kind="ExternalInput")
    wvd = nc.dram_tensor("wvd", [D, D], F32, kind="ExternalInput")
    bqc = nc.dram_tensor("bqc", [128, 8], F32, kind="ExternalInput")
    bkc = nc.dram_tensor("bkc", [128, 8], F32, kind="ExternalInput")
    bvr = nc.dram_tensor("bvr", [1, D], F32, kind="ExternalInput")
    wod = nc.dram_tensor("wod", [D, D], F32, kind="ExternalInput")
    boc = nc.dram_tensor("boc", [128, 8], F32, kind="ExternalInput")
    lngc = nc.dram_tensor("lngc", [128, 8], F32, kind="ExternalInput")
    lnbc = nc.dram_tensor("lnbc", [128, 8], F32, kind="ExternalInput")
    fc1q = nc.dram_tensor("fc1q", [D, F_FFN], I32, kind="ExternalInput")
    fc1s = nc.dram_tensor("fc1s", [8, F_FFN], F32, kind="ExternalInput")
    fc1bc = nc.dram_tensor("fc1bc", [128, 32], F32, kind="ExternalInput")
    fc2q = nc.dram_tensor("fc2q", [F_FFN, D], I32, kind="ExternalInput")
    fc2s = nc.dram_tensor("fc2s", [32, D], F32, kind="ExternalInput")
    fc2bc = nc.dram_tensor("fc2bc", [128, 8], F32, kind="ExternalInput")
    identd = nc.dram_tensor("identd", [128, 128], BF16, kind="ExternalInput")
    jrotd = nc.dram_tensor("jrotd", [128, 128], BF16, kind="ExternalInput")
    cosd = nc.dram_tensor("cosd", [128, S], BF16, kind="ExternalInput")
    sind = nc.dram_tensor("sind", [128, S], BF16, kind="ExternalInput")
    maskd = nc.dram_tensor("maskd", [128, 128], BF16, kind="ExternalInput")
    ind16d = nc.dram_tensor("ind16d", [16, D], BF16, kind="ExternalInput")
    out = nc.dram_tensor("out", [T, D], F32, kind="ExternalOutput")
    x1sc = nc.dram_tensor("x1sc", [T, D], F32, kind="Internal")
    xn1sc = nc.dram_tensor("xn1sc", [B_PER, D, S], BF16, kind="Internal")
    gtsc = nc.dram_tensor("gtsc", [B_PER, D, S], BF16, kind="Internal")

    est = contextlib.ExitStack()
    with est:
        _ctr = [0]
        def SB(shape, dt_):
            _ctr[0] += 1
            return est.enter_context(nc.sbuf_tensor(f'sb{_ctr[0]}', shape, dt_))

        id_sb = SB([128, 128], BF16)
        j_sb = SB([128, 128], BF16)
        mask_sb = SB([128, 128], BF16)
        ind_sb = SB([16, D], BF16)
        ones1 = SB([1, 128], BF16)
        ones1f = SB([1, 128], F32)
        pcol = SB([128, 48], F32)
        QC, CC, OMC, WSC, RWSC = 0, 16, 32, 40, 41
        bcol = SB([128, 64], F32)
        BQ0, BK0, BO0, LNG0, LNB0, HB20 = 0, 8, 16, 24, 32, 40
        f1bcol = SB([128, 32], F32)
        qcolb = SB([128, 48], BF16)
        m2cb = SB([128, 32], BF16)
        r2cb = SB([128, 32], BF16)
        smc = SB([128, 64], F32)
        wsum = SB([1, 2], F32)
        halfc = SB([128, 1], F32)
        onescol = SB([128, 1], F32)
        halfc2 = SB([128, 1], F32)

        sems, semv = {}, {}

        def mksem(names):
            for n in names:
                sems[n] = est.enter_context(nc.semaphore(n))
                semv[n] = 0
        mksem(["dp", "dw", "dx", "sq", "vr", "ss", "vn", "xs", "tt", "tc",
               "az", "asc", "aq", "cb", "sl", "bm", "bmm", "be", "gs", "da",
               "vb", "bb", "bc", "sg", "et", "d2", "ex", "d1", "q2", "v2",
               "s2", "n2", "a2", "t2", "cs", "d3", "dr", "dq", "ds", "vq",
               "hm", "hs", "f1", "fe", "f2", "ge", "dg", "qm", "qb", "jm",
               "rp", "vm", "ve", "stm", "xe", "msk", "pvm", "oc", "rc",
               "nbm", "nbc", "wm", "we", "st", "dxa", "dga", "vv", "gm",
               "dx0", "dx1", "dx2", "dx3", "d20", "d21", "d10", "d11", "d30", "d31",
               "st0", "st1", "ds0", "ds1", "dq0", "dq1", "dbv", "ssd0", "ssd1",
               "stmA", "stmB", "xeA", "xeB", "mskA", "mskB", "pvmA", "pvmB",
               "ocA", "ocB", "ssdA", "ssdB"])

        def inc(ins, name, amt=1):
            semv[name] += amt
            return ins.then_inc(sems[name], amt)

        def snap():
            return dict(semv)

        SKIP_P0 = (STAGE == "P1B" or STAGE.startswith("MIN") or _p1rank() < 99)
        with nc.Block() as blk:
            @blk.sync
            def _(e):
                if SKIP_P0:
                    return
                for dst, src in [
                    (id_sb[:], identd.ap()), (j_sb[:], jrotd.ap()),
                    (mask_sb[:], maskd.ap()), (ind_sb[:], ind16d.ap()),
                    (pcol[:, 0:16], delta.ap()), (pcol[:, 16:32], alpha.ap()),
                    (smc[:, 0:16], betaT.ap()), (smc[:, 16:32], gammaT.ap()),
                    (pcol[:, OMC:OMC + 8], omegaT.ap()),
                    (bcol[:, BQ0:BQ0 + 8], bqc.ap()), (bcol[:, BK0:BK0 + 8], bkc.ap()),
                    (bcol[:, BO0:BO0 + 8], boc.ap()), (bcol[:, LNG0:LNG0 + 8], lngc.ap()),
                    (bcol[:, LNB0:LNB0 + 8], lnbc.ap()),
                    (bcol[:, HB20:HB20 + 8], fc2bc.ap()), (f1bcol[:], fc1bc.ap()),
                ]:
                    inc(e.dma_start(dst, src), "dp", 16)
                e.wait_ge(sems["dp"], semv["dp"])

            @blk.vector
            def _(v):
                v = DW(v)
                v.memset(ones1[:], 1.0)
                v.memset(ones1f[:], 1.0)
                v.memset(onescol[:], 1.0)
                v.memset(halfc[:], 0.5)
                if SKIP_P0:
                    return

        with nc.Block() as blk:
            @blk.scalar
            def _(s):
                s = DW(s)
                if SKIP_P0:
                    return
                s.activation(pcol[:, 0:16], pcol[:, 0:16], AF.Sigmoid)
                s.activation(pcol[:, 16:32], pcol[:, 16:32], AF.Sigmoid)

        with nc.Block() as blk:
            @blk.vector
            def _(v):
                v = DW(v)
                if SKIP_P0:
                    return
                v.tensor_mul(smc[:, 32:48], pcol[:, 0:16], pcol[:, 16:32])
                v.tensor_mul(pcol[:, CC:CC + 16], pcol[:, 0:16], smc[:, 0:16])
                v.tensor_mul(pcol[:, CC:CC + 16], pcol[:, CC:CC + 16], smc[:, 16:32])
                v.tensor_scalar_mul(pcol[:, CC:CC + 16], pcol[:, CC:CC + 16],
                                    math.sqrt(0.5))
                v.tensor_scalar(pcol[:, QC:QC + 16], smc[:, 32:48], -1.0, 1.0,
                                ALU.mult, ALU.add)
                v.tensor_scalar_mul(bcol[:, HB20:HB20 + 8], bcol[:, HB20:HB20 + 8], 0.5)

        if STAGE == "XCOPY":
            with nc.sbuf_tensor("xc", [128, 1024], F32) as xc:
                with nc.Block() as blk:
                    @blk.sync
                    def _(e):
                        for tau in range(32):
                            if tau >= 2:
                                e.wait_ge(sems["st"], semv["st"] - 16)
                            inc(e.dma_start(xc[:], xin.ap()[tau * 128:(tau + 1) * 128, :]),
                                "dx", 16)
                            e.wait_ge(sems["dx"], semv["dx"])
                            inc(e.dma_start(out.ap()[tau * 128:(tau + 1) * 128, :], xc[:]),
                                "st", 16)
                        e.wait_ge(sems["st"], semv["st"])
            return nc

        if STAGE == "P0X":
            with nc.Block() as blk:
                @blk.sync
                def _(e):
                    inc(e.dma_start(out.ap()[0:128, 0:48], pcol[:]), "st", 16)
                    e.wait_ge(sems["st"], semv["st"])
            return nc

        if STAGE == "MINPE":
            with (
                nc.sbuf_tensor("mpa", [128, 8], F32) as mpa,
                nc.sbuf_tensor("mpres", [128, 8], F32) as mpres,
                nc.psum_tensor("mpps", [128, 128], F32) as mpps,
            ):
                with nc.Block() as blk:
                    @blk.sync
                    def _(e):
                        inc(e.dma_start(mpa[:], xin.ap()[0:128, 0:8]), "dx", 16)

                    @blk.vector
                    def _(v):
                        v.wait_ge(sems["dx"], semv["dx"])

                with nc.Block() as blk:
                    @blk.tensor
                    def _(t):
                        inc(t.matmul(mpps[0:1, 0:1], mpa[:, 0:1], onescol[:],
                                     start=True, stop=True), "bm")

                    @blk.vector
                    def _(v):
                        v.wait_ge(sems["bm"], semv["bm"])
                        v.tensor_copy(mpres[0:1, 0:1], mpps[0:1, 0:1])
                        v.drain()

                with nc.Block() as blk:
                    @blk.sync
                    def _(e):
                        inc(e.dma_start(out.ap()[0:128, 0:8], mpres[:]), "st", 16)
                        e.wait_ge(sems["st"], semv["st"])
            return nc

        if STAGE.startswith("MINP2"):
            with (
                nc.sbuf_tensor("m2w", [128, 8 * 1024], F32) as m2w,
                nc.sbuf_tensor("m2res", [128, 16], F32) as m2res,
                nc.psum_tensor("m2ps", [128, 128], F32) as m2ps,
                nc.psum_tensor("m2pb", [128, 1024], BF16) as m2pb,
            ):
                with nc.Block() as blk:
                    @blk.sync
                    def _(e):
                        for j in range(8):
                            inc(e.dma_start(m2w[:, j * 1024:(j + 1) * 1024],
                                            w0.ap()[j * 128:(j + 1) * 128, :]),
                                "dw", 16)

                    @blk.vector
                    def _(v):
                        v = DW(v)
                        v.wait_ge(sems["dw"], semv["dw"])
                        if STAGE != "MINP2N":
                            for j in range(8):
                                v.tensor_reduce(m2res[:, j:j + 1],
                                                m2w[:, j * 1024:(j + 1) * 1024],
                                                AXX, ALU.add,
                                                apply_absolute_value=True)
                            v.reduce_sum(m2res[:, 8:9], m2res[:, 0:8], AXX)

                with nc.Block() as blk:
                    @blk.tensor
                    def _(t):
                        inc(t.matmul(m2ps[0:1, 0:1], m2res[:, 8:9], onescol[:],
                                     start=True, stop=True), "bm")

                    @blk.vector
                    def _(v):
                        v.wait_ge(sems["bm"], semv["bm"])
                        v.tensor_copy(m2res[0:1, 9:10], m2ps[0:1, 0:1])
                        v.drain()

                with nc.Block() as blk:
                    @blk.sync
                    def _(e):
                        inc(e.dma_start(out.ap()[0:128, 0:16], m2res[:]), "st", 16)
                        e.wait_ge(sems["st"], semv["st"])

                if STAGE == "MINP2D":
                    with nc.Block() as blk:
                        @blk.gpsimd
                        def _(g):
                            inc(g.dma_start(out.ap()[128:256, 0:48], pcol[:]),
                                "st", 16)
                            inc(g.dma_start(out.ap()[256:257, 0:2], wsum[:]),
                                "st", 16)
                            inc(g.dma_start(out.ap()[384:512, 0:1024],
                                            m2w[:, 0:1024]), "st", 16)
                            inc(g.dma_start(out.ap()[768:896, 0:64], smc[:]),
                                "st", 16)
                            g.wait_ge(sems["st"], semv["st"])
            return nc

        with nc.sbuf_tensor([128, 8 * 1024], BF16) as wqT:
            with (
                nc.sbuf_tensor([128, 8 * 1024], F32) as wfull,
                nc.sbuf_tensor([128, 1024], F32) as tmpa,
                nc.sbuf_tensor([128, 1024], F32) as tmpb,
                nc.sbuf_tensor([128, 1024], I32) as tmpi,
                nc.sbuf_tensor([128, 8 * 1024], BF16) as wqo,
                nc.psum_tensor([128, 512], F32) as ps1,
                nc.psum_tensor([128, 2048], BF16) as pstb,
            ):
                with nc.Block() as blk:
                    BB = snap()

                    @blk.sync
                    def _(e):
                        for j in range(8):
                            inc(e.dma_start(wfull[:, j * 1024:(j + 1) * 1024],
                                            w0.ap()[j * 128:(j + 1) * 128, :]), "dw", 16)

                    @blk.vector
                    def _(v):
                        v = DW(v)
                        v.wait_ge(sems["dw"], BB["dw"] + 16 * 8)
                        for j in range(8):
                            v.tensor_reduce(smc[:, 32 + j:33 + j],
                                            wfull[:, j * 1024:(j + 1) * 1024],
                                            AXX, ALU.add, apply_absolute_value=True)
                        v.reduce_sum(smc[:, 40:41], smc[:, 32:40], AXX)

                if STAGE in ("P1L", "P1M1"):
                    with nc.Block() as blk:
                        @blk.gpsimd
                        def _(g):
                            g.wait_ge(sems["dw"], semv["dw"])
                            inc(g.dma_start(out.ap()[768:896, 0:64], smc[:]),
                                "st", 16)
                            inc(g.dma_start(out.ap()[896:1024, 0:1024],
                                            wfull[:, 0:1024]), "st", 16)
                            g.wait_ge(sems["st"], semv["st"])

                with nc.Block() as blk:
                    @blk.vector
                    def _(v):
                        v = DW(v)
                        if _p1rank() < 1:
                            return
                        v.tensor_copy(halfc2[:], smc[:, 40:41])

                with nc.Block() as blk:
                    @blk.tensor
                    def _(t):
                        if _p1rank() < 1:
                            return
                        import os as _os
                        mmv = _os.environ.get("MMVAR", "tiny")
                        if mmv == "tiny":
                            t.matmul(ps1[0:1, 0:1], halfc2[:], onescol[:],
                                     start=True, stop=True)
                        elif mmv == "full":
                            t.matmul(ps1[0:128, 0:128], wfull[:, 0:128],
                                     wfull[:, 128:256], start=True, stop=True)
                        elif mmv == "mid":
                            t.matmul(ps1[0:1, 0:128], halfc2[:],
                                     wfull[:, 0:128], start=True, stop=True)

                with nc.Block() as blk:
                    @blk.vector
                    def _(v):
                        v = DW(v)
                        if _p1rank() < 2:
                            return
                        v.tensor_copy(wsum[0:1, 0:1], ps1[0:1, 0:1])

                with nc.Block() as blk:
                    @blk.tensor
                    def _(t):
                        if _p1rank() < 3:
                            return
                        t.matmul(ps1[0:128, 0:1], ones1f[:], wsum[0:1, 0:1],
                                 start=True, stop=True)

                with nc.Block() as blk:
                    @blk.vector
                    def _(v):
                        v = DW(v)
                        if _p1rank() < 4:
                            return
                        v.tensor_copy(pcol[:, WSC:WSC + 1], ps1[0:128, 0:1])

                with nc.Block() as blk:
                    @blk.vector
                    def _(v):
                        v = DW(v)
                        if _p1rank() < 4:
                            return
                        v.tensor_scalar(pcol[:, WSC:WSC + 1], pcol[:, WSC:WSC + 1],
                                        1.0 / float(D * D), 1e-8, ALU.mult, ALU.add)
                        v.reciprocal(pcol[:, RWSC:RWSC + 1], pcol[:, WSC:WSC + 1])
                        if STAGE == "P1A" or _p1rank() < 99:
                            return
                        for j in range(8):
                            wj = wfull[:, j * 1024:(j + 1) * 1024]
                            v.tensor_scalar(tmpa[:], wj, pcol[:, RWSC:RWSC + 1], 0.5,
                                            ALU.mult, ALU.add)
                            v.tensor_copy(tmpi[:], tmpa[:])
                            v.tensor_tensor(tmpb[:], tmpi[:], tmpa[:], ALU.is_gt)
                            v.tensor_sub(tmpb[:], tmpi[:], tmpb[:])
                            v.tensor_scalar(wqo[:, j * 1024:(j + 1) * 1024], tmpb[:],
                                            -1.0, 1.0, ALU.max, ALU.min)

                with nc.Block() as blk:
                    BB = snap()

                    @blk.tensor
                    def _(t):
                        if STAGE == "P1B" or _p1rank() < 99:
                            return
                        for a in range(8):
                            for g4 in range(2):
                                gidx = a * 2 + g4
                                h_ = (gidx % 2) * 1024
                                if gidx >= 2:
                                    t.wait_ge(sems["tc"], BB["tc"] + gidx - 1)
                                last = None
                                for k in range(4):
                                    b_ = g4 * 4 + k
                                    last = t.transpose(
                                        pstb[:, h_ + k * 128:h_ + (k + 1) * 128],
                                        wqo[:, a * 1024 + b_ * 128:
                                            a * 1024 + (b_ + 1) * 128],
                                        id_sb[:])
                                inc(last, "tt")

                    @blk.vector
                    def _(v):
                        v = DW(v)
                        if STAGE == "P1B" or _p1rank() < 99:
                            return
                        for a in range(8):
                            for g4 in range(2):
                                gidx = a * 2 + g4
                                h_ = (gidx % 2) * 1024
                                v.wait_ge(sems["tt"], BB["tt"] + gidx + 1)
                                dst = wqT[:].rearrange("p (i c) -> p i c", i=8)[
                                    :, g4 * 4:g4 * 4 + 4, a * 128:(a + 1) * 128]
                                src = pstb[:, h_:h_ + 512].rearrange(
                                    "p (k c) -> p k c", k=4)
                                inc(v.tensor_copy(dst, src), "tc")

            if STAGE == "P1A":
                with nc.Block() as blk:
                    @blk.sync
                    def _(e):
                        inc(e.dma_start(out.ap()[0:128, 0:48], pcol[:]), "st", 16)
                        e.wait_ge(sems["st"], semv["st"])
                return nc

            if STAGE == "P1X":
                with nc.sbuf_tensor("dmp", [128, 1024], F32) as dmp:
                    with nc.Block() as blk:
                        @blk.vector
                        def _(v):
                            v = DW(v)
                            inc(v.tensor_copy(dmp[:], wqT[:, 0:1024]), "vb")

                    with nc.Block() as blk:
                        @blk.sync
                        def _(e):
                            inc(e.dma_start(out.ap()[0:128, :], dmp[:]), "st", 16)
                            e.wait_ge(sems["st"], semv["st"])
                return nc

                if STAGE == "P1B" or _p1rank() < 99:
                    with nc.Block() as blk:
                        @blk.gpsimd
                        def _(g):
                            g.wait_ge(sems["dw"], semv["dw"])
                            inc(g.dma_start(out.ap()[0:128, 0:1024], wqo[:, 0:1024]),
                                "st", 16)
                            inc(g.dma_start(out.ap()[128:256, 0:48], pcol[:]),
                                "st", 16)
                            inc(g.dma_start(out.ap()[256:257, 0:2], wsum[:]),
                                "st", 16)
                            inc(g.dma_start(out.ap()[384:512, 0:1024],
                                            wfull[:, 0:1024]), "st", 16)
                            inc(g.dma_start(out.ap()[512:640, 0:1024],
                                            wfull[:, 1024:2048]), "st", 16)
                            inc(g.dma_start(out.ap()[768:896, 0:64], smc[:]),
                                "st", 16)
                            inc(g.dma_start(out.ap()[640:768, 0:1024],
                                            wfull[:, 7168:8192]), "st", 16)
                            g.wait_ge(sems["st"], semv["st"])

            if STAGE == "P1B" or _p1rank() < 99:
                return nc

            with contextlib.ExitStack() as p2s:
                A2 = lambda sh, dt_: p2s.enter_context(nc.sbuf_tensor(_nm('A2'), sh, dt_))
                P2 = lambda sh, dt_: p2s.enter_context(nc.psum_tensor(_nm('P2'), sh, dt_))
                xt = A2([128, 4 * 1024], F32)
                xt2 = xt
                x1t = A2([128, 2 * 1024], F32)
                xnt = A2([128, 2 * 1024], BF16)
                xn1t = A2([128, 2 * 1024], BF16)
                stg = xnt
                xnT = A2([128, 8 * 2048], BF16)
                aqT = A2([128, 8 * 2048], BF16)
                gT = A2([128, 8 * 2048], BF16)
                tf = A2([128, 2048], F32)
                tf2 = A2([128, 2048], F32)
                h0 = tf
                h1 = tf2
                ti = x1t[:].bitcast(I32)
                gtb = xn1t
                rsb = A2([128, 2048], BF16)
                ysb = A2([128, 2048], F32)
                sc4 = A2([128, 32], F32)
                sqd = ysb[:, 0:1024]
                pstr1 = P2([128, 2048], BF16)
                pstr2 = P2([128, 2048], BF16)
                psb0 = P2([128, 512], F32)
                psb1 = P2([128, 512], F32)
                psbc = P2([128, 512], F32)
                psbit = [psb0, psb1]
                for b in range(B_PER):
                    with nc.Block() as blk:
                        BB = snap()

                        @blk.sync
                        def _(e):
                            for tau in range(NT):
                                p4 = tau % 4
                                if tau >= 4:
                                    e.wait_ge(sems["xs"], BB["xs"] + tau - 3)
                                inc(e.dma_start(
                                    xt[:, p4 * 1024:(p4 + 1) * 1024],
                                    xin.ap()[b * S + tau * 128:b * S + (tau + 1) * 128, :]),
                                    "dx" + str(p4), 16)

                        @blk.scalar
                        def _(s):
                            s = DW(s)
                            s.wait_ge(sems["d30"], BB["d30"])
                            s.wait_ge(sems["d31"], BB["d31"])
                            for tau in range(NT):
                                p = tau % 2
                                p4 = tau % 4
                                s.wait_ge(sems["dx" + str(p4)],
                                          BB["dx" + str(p4)] + 16 * (tau // 4 + 1))
                                inc(s.activation(sqd, xt[:, p4 * 1024:(p4 + 1) * 1024],
                                                 AF.Square,
                                                 accum_out=sc4[:, 4 + p4:5 + p4]), "sq")
                                s.wait_ge(sems["vr"], BB["vr"] + tau + 1)
                                inc(s.activation(sc4[:, 16 + p4:17 + p4],
                                                 sc4[:, 16 + p4:17 + p4],
                                                 AF.Sqrt), "ss")
                                s.wait_ge(sems["vn"], BB["vn"] + tau + 1)
                                if tau >= 2:
                                    s.wait_ge(sems["tc"], BB["tc"] + tau - 1)
                                inc(s.activation(xnt[:, p * 1024:(p + 1) * 1024],
                                                 xt[:, p4 * 1024:(p4 + 1) * 1024],
                                                 AF.Identity,
                                                 bias=sc4[:, 20 + p4:21 + p4],
                                                 scale=sc4[:, 16 + p4:17 + p4]), "xs")

                        @blk.vector
                        def _(v):
                            v = DW(v)
                            for tau in range(NT):
                                p = tau % 2
                                p4 = tau % 4
                                v.wait_ge(sems["dx" + str(p4)],
                                          BB["dx" + str(p4)] + 16 * (tau // 4 + 1))
                                v.reduce_sum(sc4[:, 0 + p4:1 + p4],
                                             xt[:, p4 * 1024:(p4 + 1) * 1024], AXX)
                                v.tensor_scalar_mul(sc4[:, 8 + p4:9 + p4],
                                                    sc4[:, 0 + p4:1 + p4], 1.0 / D)
                                v.wait_ge(sems["sq"], BB["sq"] + tau + 1)
                                v.tensor_mul(sc4[:, 24 + p4:25 + p4],
                                             sc4[:, 8 + p4:9 + p4],
                                             sc4[:, 8 + p4:9 + p4])
                                v.tensor_scalar(sc4[:, 12 + p4:13 + p4],
                                                sc4[:, 4 + p4:5 + p4],
                                                1.0 / D, None, ALU.mult)
                                v.tensor_sub(sc4[:, 12 + p4:13 + p4],
                                             sc4[:, 12 + p4:13 + p4],
                                             sc4[:, 24 + p4:25 + p4])
                                v.tensor_scalar_add(sc4[:, 12 + p4:13 + p4],
                                                    sc4[:, 12 + p4:13 + p4], 1e-5)
                                inc(v.reciprocal(sc4[:, 16 + p4:17 + p4],
                                                 sc4[:, 12 + p4:13 + p4]), "vr")
                                v.wait_ge(sems["ss"], BB["ss"] + tau + 1)
                                inc(v.scalar_tensor_tensor(
                                    sc4[:, 20 + p4:21 + p4], sc4[:, 8 + p4:9 + p4],
                                    -1.0, sc4[:, 16 + p4:17 + p4],
                                    ALU.mult, ALU.mult), "vn")
                                v.wait_ge(sems["xs"], BB["xs"] + tau + 1)
                                v.tensor_reduce(smc[:, 16 + tau:17 + tau],
                                                xnt[:, p * 1024:(p + 1) * 1024],
                                                AXX, ALU.max, apply_absolute_value=True)
                                v.wait_ge(sems["tt"], BB["tt"] + tau + 1)
                                dst = xnT[:].rearrange("p (j s) -> p j s", j=8)[
                                    :, :, tau * 128:(tau + 1) * 128]
                                src = pstr1[:, p * 1024:(p + 1) * 1024].rearrange(
                                    "p (j c) -> p j c", j=8)
                                inc(v.tensor_copy(dst, src), "tc")

                        @blk.tensor
                        def _(t):
                            for tau in range(NT):
                                p = tau % 2
                                t.wait_ge(sems["xs"], BB["xs"] + tau + 1)
                                if tau >= 2:
                                    t.wait_ge(sems["tc"], BB["tc"] + tau - 1)
                                last = None
                                for jj in range(8):
                                    last = t.transpose(
                                        pstr1[:, p * 1024 + jj * 128:
                                              p * 1024 + (jj + 1) * 128],
                                        xnt[:, p * 1024 + jj * 128:
                                            p * 1024 + (jj + 1) * 128],
                                        id_sb[:])
                                inc(last, "tt")

                    if STAGE == "L0A" and b == 0:
                        break

                    with nc.Block() as blk:
                        BB = snap()

                        @blk.vector
                        def _(v):
                            v = DW(v)
                            v.tensor_scalar_add(smc[:, 16:32], smc[:, 16:32], 1e-8)
                            v.tensor_scalar(qcolb[:, 16:32], smc[:, 16:32],
                                            pcol[:, WSC:WSC + 1], 1.0 / 127.0,
                                            ALU.mult, ALU.mult)
                            v.reciprocal(smc[:, 16:32], smc[:, 16:32])
                            v.tensor_scalar_mul(smc[:, 16:32], smc[:, 16:32], 127.0)
                            inc(v.tensor_copy(qcolb[:, 0:16], smc[:, 16:32]), "vb")
                            for tq in range(16):
                                v.wait_ge(sems["bm"], BB["bm"] + 2 * tq + 1)
                                inc(v.tensor_copy(rsb[:, tq * 128:(tq + 1) * 128],
                                                  psbc[:, 0:128]), "bc")
                                v.wait_ge(sems["bm"], BB["bm"] + 2 * tq + 2)
                                inc(v.tensor_copy(ysb[:, tq * 128:(tq + 1) * 128],
                                                  psbc[:, 128:256]), "bc")
                            for jj in range(8):
                                xj = xnT[:, jj * 2048:(jj + 1) * 2048]
                                inc(v.tensor_mul(tf[:], xj, rsb[:]), "az")
                                v.wait_ge(sems["asc"], BB["asc"] + 2 * (jj + 1))
                                v.tensor_tensor(gtb[:], ti, tf2[:], ALU.is_gt)
                                inc(v.tensor_sub(aqT[:, jj * 2048:(jj + 1) * 2048],
                                                 ti, gtb[:]), "aq")
                            for jj in range(8):
                                xj = xnT[:, jj * 2048:(jj + 1) * 2048]
                                inc(v.tensor_tensor_scan(
                                    h0[:], pcol[:, QC + 2 * jj:QC + 2 * jj + 1]
                                    .broadcast_to([128, 2048]), xj, 0.0,
                                    ALU.mult, ALU.add), "da")
                                v.tensor_tensor_scan(
                                    h1[:], pcol[:, QC + 2 * jj + 1:QC + 2 * jj + 2]
                                    .broadcast_to([128, 2048]), xj, 0.0,
                                    ALU.mult, ALU.add)
                                v.wait_ge(sems["gm"], BB["gm"] + jj + 1)
                                v.scalar_tensor_tensor(
                                    gtb[:], h1[:],
                                    pcol[:, CC + 2 * jj + 1:CC + 2 * jj + 2],
                                    gtb[:], ALU.mult, ALU.add)
                                inc(v.scalar_tensor_tensor(
                                    xj, xj, pcol[:, OMC + jj:OMC + jj + 1], gtb[:],
                                    ALU.mult, ALU.add), "cb")
                            for k in range(32):
                                tq, ot = k // 8, k % 8
                                v.wait_ge(sems["bmm"], BB["bmm"] + k + 1)
                                inc(v.tensor_mul(
                                    gT[:, ot * 2048 + tq * 512:ot * 2048 + (tq + 1) * 512],
                                    psbit[k % 2][:],
                                    ysb[:, tq * 512:(tq + 1) * 512]), "be")

                        @blk.scalar
                        def _(s):
                            s = DW(s)
                            s.wait_ge(sems["d10"], BB["d10"])
                            s.wait_ge(sems["d11"], BB["d11"])
                            s.wait_ge(sems["st0"], BB["st0"])
                            s.wait_ge(sems["st1"], BB["st1"])
                            for jj in range(8):
                                s.wait_ge(sems["az"], BB["az"] + jj + 1)
                                if jj >= 1:
                                    s.wait_ge(sems["aq"], BB["aq"] + jj)
                                s.activation(tf2[:], tf[:], AF.Identity,
                                             bias=halfc[:])
                                inc(s.copy(ti, tf2[:]), "asc", 2)
                            s.wait_ge(sems["aq"], BB["aq"] + 8)
                            for jj in range(8):
                                s.wait_ge(sems["da"], BB["da"] + jj + 1)
                                if jj >= 1:
                                    s.wait_ge(sems["cb"], BB["cb"] + jj)
                                inc(s.activation(
                                    gtb[:], h0[:], AF.Identity,
                                    scale=pcol[:, CC + 2 * jj:CC + 2 * jj + 1]),
                                    "gm")
                            for jj in range(8):
                                s.wait_ge(sems["cb"], BB["cb"] + jj + 1)
                                seg = xnT[:, jj * 2048:(jj + 1) * 2048]
                                inc(s.activation(seg, seg, AF.Silu), "sl")
                            for k in range(32):
                                tq, ot = k // 8, k % 8
                                s.wait_ge(sems["be"], BB["be"] + k + 1)
                                seg = gT[:, ot * 2048 + tq * 512:
                                         ot * 2048 + (tq + 1) * 512]
                                inc(s.activation(seg, seg, AF.Silu), "gs")

                        @blk.tensor
                        def _(t):
                            t.wait_ge(sems["vb"], BB["vb"] + 1)
                            for tq in range(16):
                                if tq >= 1:
                                    t.wait_ge(sems["bc"], BB["bc"] + 2 * tq)
                                inc(t.matmul(psbc[:, 0:128],
                                             qcolb[:, tq:tq + 1].broadcast_to([128, 128]),
                                             id_sb[:], start=True, stop=True), "bm")
                                t.wait_ge(sems["bc"], BB["bc"] + 2 * tq + 1)
                                inc(t.matmul(psbc[:, 128:256],
                                             qcolb[:, 16 + tq:17 + tq]
                                             .broadcast_to([128, 128]),
                                             id_sb[:], start=True, stop=True), "bm")
                            t.wait_ge(sems["aq"], BB["aq"] + 8)
                            for k in range(32):
                                tq, ot = k // 8, k % 8
                                if k >= 2:
                                    t.wait_ge(sems["be"], BB["be"] + k - 1)
                                last = None
                                for it in range(8):
                                    last = t.matmul(
                                        psbit[k % 2][:],
                                        wqT[:, it * 1024 + ot * 128:
                                            it * 1024 + (ot + 1) * 128],
                                        aqT[:, it * 2048 + tq * 512:
                                            it * 2048 + (tq + 1) * 512],
                                        start=(it == 0), stop=(it == 7))
                                inc(last, "bmm")

                    if STAGE == "L0B" and b == 0:
                        break

                    with nc.Block() as blk:
                        BB = snap()

                        @blk.vector
                        def _(v):
                            v = DW(v)
                            for jj in range(8):
                                sl_ = xnT[:, jj * 2048:(jj + 1) * 2048]
                                inc(v.tensor_mul(sl_, sl_,
                                                 gT[:, jj * 2048:(jj + 1) * 2048]), "sg")
                            for tau in range(NT):
                                p = tau % 2
                                v.wait_ge(sems["et"], BB["et"] + tau + 1)
                                v.wait_ge(sems["d2" + str(p)],
                                          BB["d2" + str(p)] + 16 * (tau // 2 + 1))
                                if tau >= 2:
                                    v.wait_ge(sems["d1" + str(p)],
                                              BB["d1" + str(p)] + 16 * (tau // 2))
                                    if STAGE == "L0":
                                        v.wait_ge(sems["st" + str(p)],
                                                  BB["st" + str(p)] + 16 * (tau // 2))
                                inc(v.tensor_tensor(x1t[:, p * 1024:(p + 1) * 1024],
                                                    xt2[:, p * 1024:(p + 1) * 1024],
                                                    pstr1[:, p * 1024:(p + 1) * 1024],
                                                    ALU.add), "ex")
                                v.reduce_sum(smc[:, 0 + p:1 + p],
                                             x1t[:, p * 1024:(p + 1) * 1024], AXX)
                                v.tensor_scalar_mul(smc[:, 4 + p:5 + p],
                                                    smc[:, 0 + p:1 + p], 1.0 / D)
                                v.wait_ge(sems["q2"], BB["q2"] + tau + 1)
                                v.tensor_mul(smc[:, 12 + p:13 + p], smc[:, 4 + p:5 + p],
                                             smc[:, 4 + p:5 + p])
                                v.tensor_scalar(smc[:, 6 + p:7 + p], smc[:, 2 + p:3 + p],
                                                1.0 / D, None, ALU.mult)
                                v.tensor_sub(smc[:, 6 + p:7 + p], smc[:, 6 + p:7 + p],
                                             smc[:, 12 + p:13 + p])
                                v.tensor_scalar_add(smc[:, 6 + p:7 + p],
                                                    smc[:, 6 + p:7 + p], 1e-5)
                                inc(v.reciprocal(smc[:, 8 + p:9 + p],
                                                 smc[:, 6 + p:7 + p]), "v2")
                                v.wait_ge(sems["s2"], BB["s2"] + 2 * tau + 1)
                                inc(v.scalar_tensor_tensor(
                                    smc[:, 10 + p:11 + p], smc[:, 4 + p:5 + p], -1.0,
                                    smc[:, 8 + p:9 + p], ALU.mult, ALU.mult), "n2")
                                v.wait_ge(sems["a2"], BB["a2"] + 2 * (tau + 1))
                                v.reduce_sum(smc[:, 32 + tau:33 + tau],
                                             xn1t[:, p * 1024:(p + 1) * 1024], AXX)
                                v.tensor_scalar_mul(smc[:, 32 + tau:33 + tau],
                                                    smc[:, 32 + tau:33 + tau], 1.0 / D)
                                v.tensor_mul(smc[:, 12 + p:13 + p],
                                             smc[:, 32 + tau:33 + tau],
                                             smc[:, 32 + tau:33 + tau])
                                v.tensor_scalar(smc[:, 48 + tau:49 + tau],
                                                smc[:, 14 + p:15 + p], 1.0 / D,
                                                None, ALU.mult)
                                v.tensor_sub(smc[:, 48 + tau:49 + tau],
                                             smc[:, 48 + tau:49 + tau],
                                             smc[:, 12 + p:13 + p])
                                v.tensor_scalar_add(smc[:, 48 + tau:49 + tau],
                                                    smc[:, 48 + tau:49 + tau], 1e-5)
                                inc(v.reciprocal(smc[:, 48 + tau:49 + tau],
                                                 smc[:, 48 + tau:49 + tau]), "vv")
                                v.wait_ge(sems["t2"], BB["t2"] + tau + 1)
                                if tau >= 2:
                                    v.wait_ge(sems["d3" + str(p)],
                                              BB["d3" + str(p)] + 16 * (tau // 2))
                                inc(v.tensor_copy(stg[:, p * 1024:(p + 1) * 1024],
                                                  pstr2[:, p * 1024:(p + 1) * 1024]),
                                    "cs")

                        @blk.scalar
                        def _(s):
                            s = DW(s)
                            for tau in range(NT):
                                p = tau % 2
                                s.wait_ge(sems["ex"], BB["ex"] + tau + 1)
                                inc(s.activation(sqd, x1t[:, p * 1024:(p + 1) * 1024],
                                                 AF.Square,
                                                 accum_out=smc[:, 2 + p:3 + p]), "q2")
                                s.wait_ge(sems["v2"], BB["v2"] + tau + 1)
                                inc(s.activation(smc[:, 8 + p:9 + p],
                                                 smc[:, 8 + p:9 + p], AF.Sqrt), "s2")
                                s.wait_ge(sems["n2"], BB["n2"] + tau + 1)
                                if tau >= 2:
                                    s.wait_ge(sems["cs"], BB["cs"] + tau - 1)
                                s.activation(xn1t[:, p * 1024:(p + 1) * 1024],
                                             x1t[:, p * 1024:(p + 1) * 1024],
                                             AF.Identity, bias=smc[:, 10 + p:11 + p],
                                             scale=smc[:, 8 + p:9 + p])
                                inc(s.activation(sqd, xn1t[:, p * 1024:(p + 1) * 1024],
                                                 AF.Square,
                                                 accum_out=smc[:, 14 + p:15 + p]),
                                    "a2", 2)
                                s.wait_ge(sems["vv"], BB["vv"] + tau + 1)
                                inc(s.activation(smc[:, 48 + tau:49 + tau],
                                                 smc[:, 48 + tau:49 + tau], AF.Sqrt),
                                    "s2")

                        @blk.tensor
                        def _(t):
                            t.wait_ge(sems["sg"], BB["sg"] + 8)
                            for tau in range(NT):
                                p = tau % 2
                                if tau >= 2:
                                    t.wait_ge(sems["ex"], BB["ex"] + tau - 1)
                                last = None
                                for jj in range(8):
                                    last = t.transpose(
                                        pstr1[:, p * 1024 + jj * 128:
                                              p * 1024 + (jj + 1) * 128],
                                        xnT[:, jj * 2048 + tau * 128:
                                            jj * 2048 + (tau + 1) * 128],
                                        id_sb[:])
                                inc(last, "et")
                                t.wait_ge(sems["a2"], BB["a2"] + 2 * (tau + 1))
                                if tau >= 2:
                                    t.wait_ge(sems["cs"], BB["cs"] + tau - 1)
                                last = None
                                for jj in range(8):
                                    last = t.transpose(
                                        pstr2[:, p * 1024 + jj * 128:
                                              p * 1024 + (jj + 1) * 128],
                                        xn1t[:, p * 1024 + jj * 128:
                                            p * 1024 + (jj + 1) * 128],
                                        id_sb[:])
                                inc(last, "t2")

                        def _store(e, tau):
                            p = tau % 2
                            e.wait_ge(sems["ex"], BB["ex"] + tau + 1)
                            inc(e.dma_start(
                                x1sc.ap()[b * S + tau * 128:b * S + (tau + 1) * 128, :],
                                x1t[:, p * 1024:(p + 1) * 1024]), "d1" + str(p), 16)
                            if STAGE == "L0":
                                inc(e.dma_start(
                                    out.ap()[b * S + tau * 128:
                                             b * S + (tau + 1) * 128, :],
                                    x1t[:, p * 1024:(p + 1) * 1024]),
                                    "st" + str(p), 16)
                            e.wait_ge(sems["cs"], BB["cs"] + tau + 1)
                            dst = xn1sc.ap()[b, :, :].rearrange(
                                "(j p) s -> p j s", p=128)[:, :,
                                                           tau * 128:(tau + 1) * 128]
                            src = stg[:, p * 1024:(p + 1) * 1024].rearrange(
                                "p (j c) -> p j c", j=8)
                            inc(e.dma_start(dst, src), "d3" + str(p), 16)

                        @blk.sync
                        def _(e):
                            for tau in range(NT):
                                p = tau % 2
                                if tau >= 2:
                                    _store(e, tau - 2)
                                inc(e.dma_start(
                                    xt2[:, p * 1024:(p + 1) * 1024],
                                    xin.ap()[b * S + tau * 128:b * S + (tau + 1) * 128, :]),
                                    "d2" + str(p), 16)
                            _store(e, NT - 2)
                            _store(e, NT - 1)

                    with nc.Block() as blk:
                        @blk.vector
                        def _(v):
                            v = DW(v)
                            v.tensor_copy(r2cb[:, b * 16:(b + 1) * 16], smc[:, 48:64])
                            v.tensor_mul(smc[:, 32:48], smc[:, 32:48], smc[:, 48:64])
                            v.tensor_copy(m2cb[:, b * 16:(b + 1) * 16], smc[:, 32:48])

            if STAGE in ("L0A", "L0B"):
                with nc.Block() as blk:
                    @blk.gpsimd
                    def _(g):
                        dst = out.ap().rearrange("(x p) d -> p x d", p=128)[:, 0:16, :]
                        src = (xnT[:] if STAGE == "L0A" else gT[:]).rearrange(
                            "p (j s) -> p j s", j=16)
                        inc(g.dma_start(dst, src), "st", 16)
                        g.wait_ge(sems["st"], semv["st"])
                return nc

        if STAGE == "L0":
            with nc.Block() as blk:
                @blk.sync
                def _(e):
                    e.wait_ge(sems["st0"], semv["st0"])
                    e.wait_ge(sems["st1"], semv["st1"])
            return nc

        with (
            nc.sbuf_tensor([128, 8 * F_FFN], BF16) as w1sb,
            nc.sbuf_tensor([128, FT * 1024], BF16) as w2sb,
        ):
            with nc.sbuf_tensor([128, 2 * F_FFN], F32) as scb:
                with nc.Block() as blk:
                    BB = snap()

                    @blk.gpsimd
                    def _(g):
                        for nm in ("d10", "d11", "d30", "d31"):
                            g.wait_ge(sems[nm], BB[nm])
                        for gidx in range(8 + FT):
                            p = gidx % 2
                            if gidx >= 2:
                                g.wait_ge(sems["vq"], BB["vq"] + gidx - 1)
                            if gidx < 8:
                                scsrc = fc1s.ap()[gidx:gidx + 1, :]
                                wdst = w1sb[:, gidx * F_FFN:(gidx + 1) * F_FFN]
                                wsrc = fc1q.ap()[gidx * 128:(gidx + 1) * 128, :]
                                scw = F_FFN
                            else:
                                gg = gidx - 8
                                scsrc = fc2s.ap()[gg:gg + 1, :]
                                wdst = w2sb[:, gg * 1024:(gg + 1) * 1024]
                                wsrc = fc2q.ap()[gg * 128:(gg + 1) * 128, :]
                                scw = 1024
                            inc(g.dma_start(scb[:, p * F_FFN:p * F_FFN + scw],
                                            scsrc.partition_broadcast(128)),
                                "ds" + str(p), 16)
                            inc(g.dma_start(wdst, wsrc), "dq" + str(p), 16)

                    @blk.vector
                    def _(v):
                        v = DW(v)
                        for gidx in range(8 + FT):
                            p = gidx % 2
                            v.wait_ge(sems["ds" + str(p)],
                                      BB["ds" + str(p)] + 16 * (gidx // 2 + 1))
                            v.wait_ge(sems["dq" + str(p)],
                                      BB["dq" + str(p)] + 16 * (gidx // 2 + 1))
                            if gidx < 8:
                                seg = w1sb[:, gidx * F_FFN:(gidx + 1) * F_FFN]
                                scw = F_FFN
                            else:
                                gg = gidx - 8
                                seg = w2sb[:, gg * 1024:(gg + 1) * 1024]
                                scw = 1024
                            inc(v.scalar_tensor_tensor(
                                seg, seg, -128.0, scb[:, p * F_FFN:p * F_FFN + scw],
                                ALU.add, ALU.mult), "vq")

            with contextlib.ExitStack() as p3s:
                A3 = lambda sh, dt_: p3s.enter_context(nc.sbuf_tensor(_nm('A3'), sh, dt_))
                P3 = lambda sh, dt_: p3s.enter_context(nc.psum_tensor(_nm('P3'), sh, dt_))
                TCW = 512
                NTC = S // TCW
                NCHK = B_PER * NTC
                XW = 8 * TCW
                xsl = A3([128, 2 * XW], BF16)
                hsl = A3([128, XW], BF16)
                y1s = A3([128, FT * TCW], BF16)
                r2b = A3([128, TCW], F32)
                m2b = A3([128, TCW], F32)
                htmp = A3([128, 2 * TCW], F32)
                tgs = A3([128, 2 * TCW], F32)
                pf = [P3([128, 512], F32) for _ in range(4)]
                pg = [P3([128, 512], F32) for _ in range(2)]
                psbc2 = P3([128, 1024], F32)

                def _dxa(e, ch, BB3):
                    b, tc = ch // NTC, ch % NTC
                    if ch >= 2:
                        e.wait_ge(sems["vm"], BB3["vm"] + 8 * (ch - 1))
                        e.wait_ge(sems["dga"], BB3["dga"] + 16 * (ch - 1))
                    dst = xsl[:, (ch % 2) * XW:(ch % 2 + 1) * XW].rearrange(
                        "p (j c) -> p j c", j=8)
                    src = xn1sc.ap()[b, :, :].rearrange(
                        "(j p) s -> p j s", p=128)[:, :,
                                                   tc * TCW:(tc + 1) * TCW]
                    inc(e.dma_start(dst, src), "dxa", 16)

                def _bb(t, ch, BB3):
                    b, tc = ch // NTC, ch % NTC
                    if ch >= 1:
                        t.wait_ge(sems["bc"], BB3["bc"] + 2 * ch)
                    last = None
                    for u in range(4):
                        tau = tc * 4 + u
                        last = t.matmul(
                            psbc2[:, u * 128:(u + 1) * 128],
                            m2cb[:, b * 16 + tau:b * 16 + tau + 1]
                            .broadcast_to([128, 128]),
                            id_sb[:], start=True, stop=True)
                    for u in range(4):
                        tau = tc * 4 + u
                        last = t.matmul(
                            psbc2[:, 512 + u * 128:512 + (u + 1) * 128],
                            r2cb[:, b * 16 + tau:b * 16 + tau + 1]
                            .broadcast_to([128, 128]),
                            id_sb[:], start=True, stop=True)
                    inc(last, "bb", 2)

                def _ln_v(v, ch, BB3):
                    v.wait_ge(sems["bb"], BB3["bb"] + 2 * (ch + 1))
                    inc(v.tensor_copy(m2b[:], psbc2[:, 0:TCW]), "bc")
                    inc(v.tensor_copy(r2b[:], psbc2[:, 512:512 + TCW]), "bc")
                    v.wait_ge(sems["dxa"], BB3["dxa"] + 16 * (ch + 1))
                    for jj in range(8):
                        hp = jj % 2
                        if jj >= 2:
                            v.wait_ge(sems["hs"], BB3["hs"] + 8 * ch + jj - 1)
                        elif ch >= 1:
                            v.wait_ge(sems["hs"], BB3["hs"] + 8 * ch)
                        xj = xsl[:, (ch % 2) * XW + jj * TCW:
                                 (ch % 2) * XW + (jj + 1) * TCW]
                        v.tensor_tensor(
                            htmp[:, hp * TCW:(hp + 1) * TCW],
                            xj, r2b[:], ALU.mult)
                        inc(v.tensor_sub(
                            htmp[:, hp * TCW:(hp + 1) * TCW],
                            htmp[:, hp * TCW:(hp + 1) * TCW],
                            m2b[:]), "hm")

                def _ln_s(s, ch, BB3):
                    if ch >= 1:
                        s.wait_ge(sems["f1"], BB3["f1"] + FT * ch)
                    for jj in range(8):
                        hp = jj % 2
                        s.wait_ge(sems["hm"], BB3["hm"] + 8 * ch + jj + 1)
                        inc(s.activation(
                            hsl[:, jj * TCW:(jj + 1) * TCW],
                            htmp[:, hp * TCW:(hp + 1) * TCW],
                            AF.Identity,
                            bias=bcol[:, LNB0 + jj:LNB0 + jj + 1],
                            scale=bcol[:, LNG0 + jj:LNG0 + jj + 1]), "hs")

                with nc.Block() as blk:
                    BB3 = snap()

                    @blk.sync
                    def _(e):
                        e.wait_ge(sems["d30"], BB3["d30"])
                        e.wait_ge(sems["d31"], BB3["d31"])
                        for ch in range(NCHK):
                            _dxa(e, ch, BB3)

                    @blk.tensor
                    def _(t):
                        _bb(t, 0, BB3)
                        for ch in range(NCHK):
                            if ch + 1 < NCHK:
                                _bb(t, ch + 1, BB3)
                            t.wait_ge(sems["hs"], BB3["hs"] + 8 * (ch + 1))
                            for ft in range(FT):
                                if ch >= 1 or ft >= 4:
                                    t.wait_ge(sems["fe"],
                                              BB3["fe"] + FT * ch + ft - 3)
                                last = None
                                for jj in range(8):
                                    last = t.matmul(
                                        pf[ft % 4][:, 0:TCW],
                                        w1sb[:, jj * F_FFN + ft * 128:
                                             jj * F_FFN + (ft + 1) * 128],
                                        hsl[:, jj * TCW:(jj + 1) * TCW],
                                        start=(jj == 0), stop=(jj == 7))
                                inc(last, "f1")
                            t.wait_ge(sems["fe"], BB3["fe"] + FT * (ch + 1))
                            for ht in range(8):
                                if 8 * ch + ht - 1 > 0:
                                    t.wait_ge(sems["vm"],
                                              BB3["vm"] + 8 * ch + ht - 1)
                                last = None
                                for ft in range(FT):
                                    last = t.matmul(
                                        pg[ht % 2][:, 0:TCW],
                                        w2sb[:, ft * 1024 + ht * 128:
                                             ft * 1024 + (ht + 1) * 128],
                                        y1s[:, ft * TCW:(ft + 1) * TCW],
                                        start=(ft == 0), stop=(ft == FT - 1))
                                inc(last, "f2")

                    @blk.vector
                    def _(v):
                        v = DW(v)
                        _ln_v(v, 0, BB3)
                        for ch in range(NCHK):
                            if ch + 1 < NCHK:
                                _ln_v(v, ch + 1, BB3)
                            for ht in range(8):
                                gp = ht % 2
                                v.wait_ge(sems["f2"],
                                          BB3["f2"] + 8 * ch + ht + 1)
                                if 8 * ch + ht - 1 > 0:
                                    v.wait_ge(sems["ge"],
                                              BB3["ge"] + 8 * ch + ht - 1)
                                xj = xsl[:, (ch % 2) * XW + ht * TCW:
                                         (ch % 2) * XW + (ht + 1) * TCW]
                                inc(v.scalar_tensor_tensor(
                                    tgs[:, gp * TCW:(gp + 1) * TCW],
                                    pg[ht % 2][:, 0:TCW], 0.5, xj,
                                    ALU.mult, ALU.add), "vm")

                    @blk.scalar
                    def _(s):
                        s = DW(s)
                        _ln_s(s, 0, BB3)
                        for ch in range(NCHK):
                            if ch >= 1:
                                s.wait_ge(sems["f2"], BB3["f2"] + 8 * ch)
                            for ft in range(FT):
                                s.wait_ge(sems["f1"],
                                          BB3["f1"] + FT * ch + ft + 1)
                                inc(s.activation(y1s[:, ft * TCW:(ft + 1) * TCW],
                                                 pf[ft % 4][:, 0:TCW], AF.Silu,
                                                 bias=f1bcol[:, ft:ft + 1]), "fe")
                            if ch + 1 < NCHK:
                                _ln_s(s, ch + 1, BB3)
                            for ht in range(8):
                                gp = ht % 2
                                s.wait_ge(sems["vm"],
                                          BB3["vm"] + 8 * ch + ht + 1)
                                inc(s.activation(
                                    xsl[:, (ch % 2) * XW + ht * TCW:
                                        (ch % 2) * XW + (ht + 1) * TCW],
                                    tgs[:, gp * TCW:(gp + 1) * TCW], AF.Silu,
                                    bias=bcol[:, HB20 + ht:HB20 + ht + 1]), "ge")

                    @blk.gpsimd
                    def _(g):
                        for ch in range(NCHK):
                            b, tc = ch // NTC, ch % NTC
                            g.wait_ge(sems["ge"], BB3["ge"] + 8 * (ch + 1))
                            dst = gtsc.ap()[b, :, :].rearrange(
                                "(j p) s -> p j s", p=128)[:, :,
                                                           tc * TCW:(tc + 1) * TCW]
                            src = xsl[:, (ch % 2) * XW:(ch % 2 + 1) * XW].rearrange(
                                "p (j c) -> p j c", j=8)
                            inc(g.dma_start(dst, src), "dga", 16)

        if True:
            VROW = 70
            TROW = H * VROW
            SCHED = attn_sched()
            NCH = len(SCHED)
            CB = [0] * (NT + 1)
            for (kt_, q_, l_, f_) in SCHED:
                CB[kt_ + 1] += 1
            for i in range(NT):
                CB[i + 1] += CB[i]

            for b in range(B_PER):
                with contextlib.ExitStack() as p4b:
                    A4 = lambda sh, dt_: p4b.enter_context(nc.sbuf_tensor(_nm('A4'), sh, dt_))
                    xnt1 = A4([128, 8 * S], BF16)
                    qT = A4([128, 8 * S], BF16)
                    kT = A4([128, 8 * S], BF16)
                    vext = A4([128, NT * TROW], BF16)
                    with nc.Block() as blk:
                        @blk.sync
                        def _(e):
                            e.wait_ge(sems["d30"], semv["d30"])
                            e.wait_ge(sems["d31"], semv["d31"])
                            e.wait_ge(sems["dga"], semv["dga"])
                            e.wait_ge(sems["st0"], semv["st0"])
                            e.wait_ge(sems["st1"], semv["st1"])
                            dst = xnt1[:].rearrange("p (j s) -> p j s", j=8)
                            src = xn1sc.ap()[b, :, :].rearrange("(j p) s -> p j s", p=128)
                            inc(e.dma_start(dst, src), "dxa", 16)
                            e.wait_ge(sems["dxa"], semv["dxa"])

                    with contextlib.ExitStack() as p4p:
                        A5 = lambda sh, dt_: p4p.enter_context(nc.sbuf_tensor(_nm('A5'), sh, dt_))
                        P5 = lambda sh, dt_: p4p.enter_context(nc.psum_tensor(_nm('P5'), sh, dt_))
                        wvsb = A5([128, 8 * 1024], BF16)
                        wqsb = A5([128, 8 * 1024], BF16)
                        wksb = A5([128, 8 * 1024], BF16)
                        qbb = A5([128, 4 * 512], BF16)
                        t1b = A5([128, 512], BF16)
                        t2b = A5([128, 512], BF16)
                        cosb = A5([128, S], BF16)
                        sinb = A5([128, S], BF16)
                        bvb = A5([128, 1024], F32)
                        pp = [P5([128, 512], F32) for _ in range(4)]
                        pj = [P5([128, 512], F32) for _ in range(2)]
                        psv = P5([128, 1024], F32)
                        with nc.Block() as blk:
                            BB = snap()

                            @blk.sync
                            def _(e):
                                inc(e.dma_start(cosb[:], cosd.ap()), "dp", 16)
                                inc(e.dma_start(sinb[:], sind.ap()), "dp", 16)

                            @blk.gpsimd
                            def _(g):
                                inc(g.dma_start(
                                    bvb[:], bvr.ap().partition_broadcast(128)),
                                    "dbv", 16)
                                for wi, wd in enumerate([wqd, wkd]):
                                    dst = (wqsb if wi == 0 else wksb)[:].rearrange(
                                        "p (j c) -> p j c", j=8)
                                    src = wd.ap().rearrange(
                                        "(j p) hd -> p j hd", p=128)
                                    inc(g.dma_start(dst, src),
                                        "dq" + str(wi), 16)
                                dst = wvsb[:].rearrange("p (j c) -> p j c", j=8)
                                src = wvd.ap().rearrange("(j p) hd -> p j hd", p=128)
                                inc(g.dma_start(dst, src), "dq", 16)

                            @blk.tensor
                            def _(t):
                                kk = 0
                                for wi in range(2):
                                    wsb = wqsb if wi == 0 else wksb
                                    t.wait_ge(sems["dq" + str(wi)],
                                              BB["dq" + str(wi)] + 16)
                                    for jt in range(8):
                                        for tq in range(4):
                                            if kk >= 4:
                                                t.wait_ge(sems["qb"], BB["qb"] + kk - 3)
                                            last = None
                                            for jj in range(8):
                                                last = t.matmul(
                                                    pp[kk % 4][:],
                                                    wsb[:, jj * 1024 + jt * 128:
                                                        jj * 1024 + (jt + 1) * 128],
                                                    xnt1[:, jj * S + tq * 512:
                                                         jj * S + (tq + 1) * 512],
                                                    start=(jj == 0), stop=(jj == 7))
                                            inc(last, "qm")
                                            t.wait_ge(sems["qb"], BB["qb"] + kk + 1)
                                            if kk >= 2:
                                                t.wait_ge(sems["rp"], BB["rp"] + kk - 1)
                                            inc(t.matmul(
                                                pj[kk % 2][:], j_sb[:],
                                                qbb[:, (kk % 4) * 512:(kk % 4 + 1) * 512],
                                                start=True, stop=True), "jm")
                                            kk += 1
                                t.wait_ge(sems["dq"], BB["dq"] + 16)
                                t.wait_ge(sems["dbv"], BB["dbv"] + 16)
                                for tau in range(NT):
                                    if tau >= 1:
                                        t.wait_ge(sems["ve"], BB["ve"] + tau)
                                    last = None
                                    for jj in range(8):
                                        for hh in range(2):
                                            last = t.matmul(
                                                psv[:, hh * 512:(hh + 1) * 512],
                                                xnt1[:, jj * S + tau * 128:
                                                     jj * S + (tau + 1) * 128],
                                                wvsb[:, jj * 1024 + hh * 512:
                                                     jj * 1024 + (hh + 1) * 512],
                                                start=(jj == 0), stop=(jj == 7))
                                    inc(last, "vm")

                            @blk.scalar
                            def _(s):
                                s = DW(s)
                                kk = 0
                                for wi in range(2):
                                    bb_ = [BQ0, BK0][wi]
                                    for jt in range(8):
                                        for tq in range(4):
                                            s.wait_ge(sems["qm"],
                                                      BB["qm"] + kk + 1)
                                            if kk >= 4:
                                                s.wait_ge(sems["jm"],
                                                          BB["jm"] + kk - 3)
                                                s.wait_ge(sems["rp"],
                                                          BB["rp"] + kk - 3)
                                            inc(s.activation(
                                                qbb[:, (kk % 4) * 512:
                                                    (kk % 4 + 1) * 512],
                                                pp[kk % 4][:], AF.Identity,
                                                bias=bcol[:, bb_ + jt:bb_ + jt + 1]),
                                                "qb")
                                            kk += 1

                            @blk.vector
                            def _(v):
                                v = DW(v)
                                v.wait_ge(sems["dp"], BB["dp"] + 16 * 2)
                                v.memset(vext[:], 0.0)
                                ap_ones = vext[:].rearrange(
                                    "p (tau g i) -> p tau g i", tau=NT, g=4)[
                                    :, :, :, 0:1]
                                ap1 = vext[:].rearrange(
                                    "p (tau rest) -> p tau rest", tau=NT)[
                                    :, :, 64:64 + 280 * 3 + 71 * 3 + 1]
                                for ii in range(4):
                                    ap2 = vext[:].rearrange(
                                        "p (tau g rest) -> p tau g rest",
                                        tau=NT, g=4)[:, :, :, 70 * ii + 64 + ii:
                                                     70 * ii + 64 + ii + 1]
                                    v.memset(ap2, 1.0)
                                kk = 0
                                for wi in range(2):
                                    bb_ = [BQ0, BK0][wi]
                                    dstT = [qT, kT][wi]
                                    for jt in range(8):
                                        for tq in range(4):
                                            v.wait_ge(sems["qb"], BB["qb"] + kk + 1)
                                            v.tensor_mul(
                                                t1b[:],
                                                qbb[:, (kk % 4) * 512:(kk % 4 + 1) * 512],
                                                cosb[:, tq * 512:(tq + 1) * 512])
                                            v.wait_ge(sems["jm"], BB["jm"] + kk + 1)
                                            v.tensor_tensor(
                                                t2b[:], pj[kk % 2][:],
                                                sinb[:, tq * 512:(tq + 1) * 512],
                                                ALU.mult)
                                            inc(v.tensor_add(
                                                dstT[:, jt * S + tq * 512:
                                                     jt * S + (tq + 1) * 512],
                                                t1b[:], t2b[:]), "rp")
                                            kk += 1
                                for tau in range(NT):
                                    v.wait_ge(sems["vm"], BB["vm"] + tau + 1)
                                    dst = vext[:, tau * TROW:(tau + 1) * TROW].rearrange(
                                        "p (h c) -> p h c", h=H)[:, :, 0:64]
                                    src = psv[:].rearrange("p (h c) -> p h c", h=H)
                                    bsr = bvb[:].rearrange("p (h c) -> p h c", h=H)
                                    inc(v.tensor_tensor(dst, src, bsr, ALU.add), "ve")

                    with contextlib.ExitStack() as p4a:
                        A6 = lambda sh, dt_: p4a.enter_context(nc.sbuf_tensor(_nm('A6'), sh, dt_))
                        oallT = A6([128, 8 * S], BF16)
                        sums = A6([16, S], F32)
                        with contextlib.ExitStack() as p4c:
                            A7 = lambda sh, dt_: p4c.enter_context(nc.sbuf_tensor(_nm('A7'), sh, dt_))
                            P7 = lambda sh, dt_: p4c.enter_context(nc.psum_tensor(_nm('P7'), sh, dt_))
                            ptb = A7([128, 2 * S], BF16)
                            stage4 = A7([4, 2 * S], F32)
                            pst = [P7([128, 1024], F32) for _ in range(2)]
                            pso = [P7([68, 1024], F32) for _ in range(2)]
                            LN_ = ("A", "B")

                            def job_chunks(g_, qh):
                                ch = []
                                ktmax = 8 * qh + 7
                                for hm, i_ in enumerate((3, 2, 1, 0)):
                                    h = 4 * g_ + i_
                                    for kt in range(ktmax + 1):
                                        q0g = max(kt * 128, qh * 1024)
                                        ch.append(dict(
                                            hm=hm, h=h, i=i_, kt=kt, q0g=q0g,
                                            q0r=q0g - qh * 1024,
                                            L=qh * 1024 + 1024 - q0g,
                                            first=(kt == 0), last=(kt == ktmax),
                                            diag=(q0g == kt * 128)))
                                return ch

                            PAIRS = [((0, 1), (1, 1)), ((2, 1), (3, 1)),
                                     ((0, 0), (1, 0)), ((2, 0), (3, 0))]
                            for (jA, jB) in PAIRS:
                                jobs = (jA, jB)
                                CH = [job_chunks(*jA), job_chunks(*jB)]
                                n = len(CH[0])
                                META = []
                                for li in range(2):
                                    cumE = cumS = cumPV = mcnt = 0
                                    lastE = [0, 0]
                                    cumPVl = []
                                    lst = []
                                    for ci, c in enumerate(CH[li]):
                                        sp = [(off, min(512, c["L"] - off))
                                              for off in range(0, c["L"], 512)]
                                        e_before = cumE
                                        s_before = cumS
                                        swaits = [lastE[off // 512]
                                                  for (off, _) in sp]
                                        for idx, (off, _) in enumerate(sp):
                                            lastE[off // 512] = e_before + idx + 1
                                        cumE += len(sp)
                                        cumS += len(sp)
                                        pv = []
                                        for (pa, pl) in aligned_pieces(
                                                c["q0r"], c["L"]):
                                            k = -(-(pa - c["q0r"] + pl) // 512)
                                            pv.append((pa, pl, e_before + k,
                                                       c["diag"] and
                                                       (pa - c["q0r"]) < 128))
                                        if c["diag"]:
                                            mcnt += 1
                                        pv_before = cumPV
                                        cumPV += len(pv)
                                        cumPVl.append(cumPV)
                                        lst.append(dict(
                                            c=c, sp=sp, swaits=swaits,
                                            s_before=s_before,
                                            e_before=e_before, pv=pv,
                                            mcnt=mcnt, pv_after=cumPV,
                                            pv_upto2=(cumPVl[ci - 2]
                                                      if ci >= 2 else 0)))
                                    META.append(lst)
                                with nc.Block() as blk:
                                    BB = snap()

                                    @blk.tensor
                                    def _(t, META=META, n=n):
                                        for cstep in range(n + 1):
                                            for li in range(2):
                                                if cstep >= n:
                                                    continue
                                                ci = cstep
                                                m = META[li][ci]
                                                c = m["c"]
                                                xe = "xe" + LN_[li]
                                                h = c["h"]
                                                jt, r0 = h // 2, (h % 2) * 64
                                                for (off, Lp), sw in zip(
                                                        m["sp"], m["swaits"]):
                                                    t.wait_ge(sems[xe],
                                                              BB[xe] + sw)
                                                    inc(t.matmul(
                                                        pst[li][:, off:off + Lp],
                                                        kT[r0:r0 + 64,
                                                           jt * S + c["kt"] * 128:
                                                           jt * S + (c["kt"] + 1) * 128],
                                                        qT[r0:r0 + 64,
                                                           jt * S + c["q0g"] + off:
                                                           jt * S + c["q0g"] + off + Lp],
                                                        start=True, stop=True),
                                                        "stm" + LN_[li])
                                            for li in range(2):
                                                if cstep < 1:
                                                    continue
                                                ci = cstep - 1
                                                m = META[li][ci]
                                                c = m["c"]
                                                xe = "xe" + LN_[li]
                                                if c["first"]:
                                                    t.wait_ge(sems["oc" + LN_[li]],
                                                              BB["oc" + LN_[li]]
                                                              + c["hm"])
                                                h, i_, kt = c["h"], c["i"], c["kt"]
                                                base = li * 2048 + (ci % 2) * 1024
                                                for (pa, pl, ereq, needm) in m["pv"]:
                                                    t.wait_ge(sems[xe],
                                                              BB[xe] + ereq)
                                                    if needm:
                                                        t.wait_ge(
                                                            sems["msk" + LN_[li]],
                                                            BB["msk" + LN_[li]]
                                                            + m["mcnt"])
                                                    inc(t.matmul(
                                                        pso[li][0:65 + i_, pa:pa + pl],
                                                        vext[:, kt * TROW + VROW * h:
                                                             kt * TROW + VROW * h
                                                             + 65 + i_],
                                                        ptb[:, base + pa - c["q0r"]:
                                                            base + pa - c["q0r"] + pl],
                                                        start=c["first"],
                                                        stop=c["last"],
                                                        skip_group_check=True),
                                                        "pvm" + LN_[li])

                                    @blk.scalar
                                    def _(s, META=META, n=n):
                                        s = DW(s)
                                        for ci in range(n):
                                            for li in range(2):
                                                m = META[li][ci]
                                                base = li * 2048 + (ci % 2) * 1024
                                                s.wait_ge(sems["pvm" + LN_[li]],
                                                          BB["pvm" + LN_[li]]
                                                          + m["pv_upto2"])
                                                for idx, (off, Lp) in enumerate(
                                                        m["sp"]):
                                                    s.wait_ge(
                                                        sems["stm" + LN_[li]],
                                                        BB["stm" + LN_[li]]
                                                        + m["s_before"] + idx + 1)
                                                    inc(s.activation(
                                                        ptb[:, base + off:
                                                            base + off + Lp],
                                                        pst[li][:, off:off + Lp],
                                                        AF.Exp,
                                                        scale=0.125),
                                                        "xe" + LN_[li])

                                    @blk.vector
                                    def _(v, jobs=jobs, META=META, n=n):
                                        v = DW(v)
                                        for ci in range(n):
                                            for li in range(2):
                                                m = META[li][ci]
                                                c = m["c"]
                                                if c["diag"]:
                                                    v.wait_ge(sems["xe" + LN_[li]],
                                                              BB["xe" + LN_[li]]
                                                              + m["e_before"] + 1)
                                                    base = (li * 2048
                                                            + (ci % 2) * 1024)
                                                    seg = ptb[:, base:base + 128]
                                                    inc(v.tensor_mul(seg, seg,
                                                                     mask_sb[:]),
                                                        "msk" + LN_[li])
                                                if c["last"]:
                                                    g_, qh = jobs[li]
                                                    h = c["h"]
                                                    v.wait_ge(sems["pvm" + LN_[li]],
                                                              BB["pvm" + LN_[li]]
                                                              + m["pv_after"])
                                                    inc(v.tensor_copy(
                                                        oallT[(h % 2) * 64:
                                                              (h % 2) * 64 + 64,
                                                              (h // 2) * S + qh * 1024:
                                                              (h // 2) * S
                                                              + qh * 1024 + 1024],
                                                        pso[li][0:64, :]),
                                                        "oc" + LN_[li])
                                                    if c["hm"] == 3:
                                                        v.wait_ge(
                                                            sems["ssd" + LN_[li]],
                                                            BB["ssd" + LN_[li]])
                                                        inc(v.tensor_copy(
                                                            stage4[0:4, li * 1024:
                                                                   li * 1024 + 1024],
                                                            pso[li][64:68, :]),
                                                            "oc" + LN_[li])

                                    @blk.sync
                                    def _(e, jobs=jobs):
                                        for li in range(2):
                                            g_, qh = jobs[li]
                                            e.wait_ge(sems["oc" + LN_[li]],
                                                      BB["oc" + LN_[li]] + 5)
                                            inc(e.dma_start(
                                                sums[4 * g_:4 * g_ + 4,
                                                     qh * 1024:qh * 1024 + 1024],
                                                stage4[0:4, li * 1024:
                                                       li * 1024 + 1024]),
                                                "ssd" + LN_[li], 16)

                        with contextlib.ExitStack() as p4w:
                            A8 = lambda sh, dt_: p4w.enter_context(nc.sbuf_tensor(_nm('A8'), sh, dt_))
                            P8 = lambda sh, dt_: p4w.enter_context(nc.psum_tensor(_nm('P8'), sh, dt_))
                            wosb = A8([128, 8 * 1024], BF16)
                            sums_bf = A8([16, S], BF16)
                            pw = [P8([128, 512], F32) for _ in range(2)]
                            pnb = P8([128, 512], F32)
                            with nc.Block() as blk:
                                BB = snap()

                                @blk.gpsimd
                                def _(e):
                                    e.wait_ge(sems["ssdA"], BB["ssdA"])
                                    e.wait_ge(sems["ssdB"], BB["ssdB"])
                                    dst = wosb[:].rearrange("p (j c) -> p j c", j=8)
                                    src = wod.ap().rearrange("(j p) e -> p j e", p=128)
                                    inc(e.dma_start(dst, src), "dq", 16)

                                @blk.vector
                                def _(v):
                                    v = DW(v)
                                    v.wait_ge(sems["ssdA"], BB["ssdA"])
                                    v.wait_ge(sems["ssdB"], BB["ssdB"])
                                    v.reciprocal(sums[:], sums[:])
                                    inc(v.tensor_copy(sums_bf[:], sums[:]), "rc")
                                    k = 0
                                    for jt in range(8):
                                        for tq in range(4):
                                            v.wait_ge(sems["nbm"], BB["nbm"] + k + 1)
                                            seg = oallT[:, jt * S + tq * 512:
                                                        jt * S + (tq + 1) * 512]
                                            inc(v.tensor_tensor(seg, seg, pnb[:],
                                                                ALU.mult), "nbc")
                                            k += 1
                                    k = 0
                                    for et in range(8):
                                        for tq in range(4):
                                            v.wait_ge(sems["wm"], BB["wm"] + k + 1)
                                            inc(v.tensor_scalar(
                                                xnt1[:, et * S + tq * 512:
                                                     et * S + (tq + 1) * 512],
                                                pw[k % 2][:],
                                                bcol[:, BO0 + et:BO0 + et + 1], None,
                                                ALU.add), "we")
                                            k += 1

                                @blk.tensor
                                def _(t):
                                    t.wait_ge(sems["rc"], BB["rc"] + 1)
                                    k = 0
                                    for jt in range(8):
                                        for tq in range(4):
                                            if k >= 1:
                                                t.wait_ge(sems["nbc"], BB["nbc"] + k)
                                            inc(t.matmul(
                                                pnb[:], ind_sb[:, jt * 128:(jt + 1) * 128],
                                                sums_bf[:, tq * 512:(tq + 1) * 512],
                                                start=True, stop=True), "nbm")
                                            k += 1
                                    t.wait_ge(sems["dq"], BB["dq"] + 16)
                                    t.wait_ge(sems["nbc"], BB["nbc"] + 32)
                                    k = 0
                                    for et in range(8):
                                        for tq in range(4):
                                            if k >= 2:
                                                t.wait_ge(sems["we"], BB["we"] + k - 1)
                                            last = None
                                            for jj in range(8):
                                                last = t.matmul(
                                                    pw[k % 2][:],
                                                    wosb[:, jj * 1024 + et * 128:
                                                         jj * 1024 + (et + 1) * 128],
                                                    oallT[:, jj * S + tq * 512:
                                                          jj * S + (tq + 1) * 512],
                                                    start=(jj == 0), stop=(jj == 7))
                                            inc(last, "wm")
                                            k += 1

                    with contextlib.ExitStack() as p4e:
                        A9 = lambda sh, dt_: p4e.enter_context(nc.sbuf_tensor(_nm('A9'), sh, dt_))
                        xta = A9([128, 2 * 1024], F32)
                        x2t = A9([128, 2 * 1024], F32)
                        pstr = p4e.enter_context(nc.psum_tensor(_nm('pstr_e'), [128, 2048], BF16))
                        with nc.Block() as blk:
                            BB = snap()

                            @blk.sync
                            def _(e):
                                e.wait_ge(sems["dga"], BB["dga"])
                                e.wait_ge(sems["d10"], BB["d10"])
                                e.wait_ge(sems["d11"], BB["d11"])
                                dst = qT[:].rearrange("p (j s) -> p j s", j=8)
                                src = gtsc.ap()[b, :, :].rearrange(
                                    "(j p) s -> p j s", p=128)
                                inc(e.dma_start(dst, src), "dg", 16)
                                def _store(tau):
                                    p = tau % 2
                                    e.wait_ge(sems["ex"], BB["ex"] + tau + 1)
                                    inc(e.dma_start(
                                        out.ap()[b * S + tau * 128:
                                                 b * S + (tau + 1) * 128, :],
                                        x2t[:, p * 1024:(p + 1) * 1024]),
                                        "st" + str(p), 16)

                                for tau in range(NT):
                                    p = tau % 2
                                    if tau >= 2:
                                        _store(tau - 2)
                                    inc(e.dma_start(
                                        xta[:, p * 1024:(p + 1) * 1024],
                                        x1sc.ap()[b * S + tau * 128:
                                                  b * S + (tau + 1) * 128, :]),
                                        "d2" + str(p), 16)
                                _store(NT - 2)
                                _store(NT - 1)

                            @blk.vector
                            def _(v):
                                v = DW(v)
                                v.wait_ge(sems["dg"], BB["dg"] + 16)
                                for jj in range(8):
                                    seg = xnt1[:, jj * S:(jj + 1) * S]
                                    inc(v.tensor_mul(seg, seg, qT[:, jj * S:(jj + 1) * S]),
                                        "sg")
                                for tau in range(NT):
                                    p = tau % 2
                                    v.wait_ge(sems["et"], BB["et"] + tau + 1)
                                    v.wait_ge(sems["d2" + str(p)],
                                              BB["d2" + str(p)] + 16 * (tau // 2 + 1))
                                    if tau >= 2:
                                        v.wait_ge(sems["st" + str(p)],
                                                  BB["st" + str(p)] + 16 * (tau // 2))
                                    inc(v.tensor_tensor(
                                        x2t[:, p * 1024:(p + 1) * 1024],
                                        xta[:, p * 1024:(p + 1) * 1024],
                                        pstr[:, p * 1024:(p + 1) * 1024],
                                        ALU.add), "ex")

                            @blk.tensor
                            def _(t):
                                t.wait_ge(sems["sg"], BB["sg"] + 8)
                                for tau in range(NT):
                                    p = tau % 2
                                    if tau >= 2:
                                        t.wait_ge(sems["ex"], BB["ex"] + tau - 1)
                                    last = None
                                    for jj in range(8):
                                        last = t.transpose(
                                            pstr[:, p * 1024 + jj * 128:
                                                 p * 1024 + (jj + 1) * 128],
                                            xnt1[:, jj * S + tau * 128:
                                                 jj * S + (tau + 1) * 128],
                                            id_sb[:])
                                    inc(last, "et")

        with nc.Block() as blk:
            @blk.sync
            def _(e):
                e.wait_ge(sems["st0"], semv["st0"])
                e.wait_ge(sems["st1"], semv["st1"])

    return nc


def _consts():
    ident = np.eye(128, dtype=np.float32)
    jt = np.zeros((128, 128), np.float32)
    for blk in range(2):
        o = blk * 64
        jt[o:o + 32, o + 32:o + 64] = np.eye(32)
        jt[o + 32:o + 64, o:o + 32] = -np.eye(32)
    inv = 1.0 / (10000.0 ** (np.arange(0, 64, 2, dtype=np.float64) / 64.0))
    pos = np.arange(S, dtype=np.float64)
    ang = inv[:, None] * pos[None, :]
    cosr = np.tile(np.cos(ang), (4, 1))
    sinr = np.tile(np.sin(ang), (4, 1))
    mask = (np.arange(128)[:, None] <= np.arange(128)[None, :]).astype(np.float32)
    ind16 = np.zeros((16, D), np.float32)
    for h in range(16):
        jt_ = h // 2
        r0 = (h % 2) * 64
        ind16[h, jt_ * 128 + r0: jt_ * 128 + r0 + 64] = 1.0
    import ml_dtypes
    bf = lambda a: np.ascontiguousarray(a).astype(ml_dtypes.bfloat16)
    return dict(identd=bf(ident), jrotd=bf(jt), cosd=bf(cosr), sind=bf(sinr),
                maskd=bf(mask), ind16d=bf(ind16))


def _cols(vec, k):
    return np.ascontiguousarray(vec.reshape(k, 128).T)


def kernel(**inputs):
    f32 = lambda a: np.ascontiguousarray(a, dtype=np.float32)
    i32 = lambda a: np.ascontiguousarray(a, dtype=np.int32)
    x = f32(inputs["L0_attn_x"]).reshape(B_ALL, S, D)
    base = {
        "delta": f32(inputs["L0_attn_delta"].reshape(8, 128, 2)
                     .transpose(1, 0, 2).reshape(128, 16)),
        "alpha": f32(inputs["L0_attn_alpha"].reshape(8, 128, 2)
                     .transpose(1, 0, 2).reshape(128, 16)),
        "betaT": f32(inputs["L0_attn_beta"].reshape(8, 128, 2)
                     .transpose(1, 0, 2).reshape(128, 16)),
        "gammaT": f32(inputs["L0_attn_gamma"].reshape(8, 128, 2)
                      .transpose(1, 0, 2).reshape(128, 16)),
        "omegaT": _cols(f32(inputs["L0_attn_omega"]), 8),
        "w0": f32(inputs["L0_ffn_weight"]),
        "wqd": f32(inputs["L1_attn_Wq"]),
        "wkd": f32(inputs["L1_attn_Wkv"][:, :D]),
        "wvd": f32(inputs["L1_attn_Wkv"][:, D:]),
        "bqc": _cols(f32(inputs["L1_attn_bq"]), 8),
        "bkc": _cols(f32(inputs["L1_attn_bkv"][:D]), 8),
        "bvr": f32(inputs["L1_attn_bkv"][D:]).reshape(1, D),
        "wod": f32(inputs["L1_attn_Wo"]),
        "boc": _cols(f32(inputs["L1_attn_bo"]), 8),
        "lngc": _cols(f32(inputs["L1_ffn_ln_gamma"]), 8),
        "lnbc": _cols(f32(inputs["L1_ffn_ln_beta"]), 8),
        "fc1q": i32(inputs["L1_ffn_fc1_q"]),
        "fc1s": f32(inputs["L1_ffn_fc1_scales"]),
        "fc1bc": _cols(f32(inputs["L1_ffn_fc1_bias"]), 32),
        "fc2q": i32(inputs["L1_ffn_fc2_q"]),
        "fc2s": f32(inputs["L1_ffn_fc2_scales"]),
        "fc2bc": _cols(f32(inputs["L1_ffn_fc2_bias"]), 8),
    }
    base.update(_consts())

    nc = bass.Bass("TRN2", target_bir_lowering=False)
    build(nc)

    in_maps = []
    for c in range(NCORES):
        m = dict(base)
        m["x"] = np.ascontiguousarray(
            x[c * B_PER:(c + 1) * B_PER].reshape(T, D))
        in_maps.append(m)
    res = run_bass_kernel_spmd(nc, in_maps, core_ids=list(range(NCORES)))
    outs = [res.results[c]["out"].reshape(B_PER, S, D) for c in range(NCORES)]
    return np.concatenate(outs, axis=0).astype(np.float32)
